# version 16
# baseline (speedup 1.0000x reference)
import contextlib
import numpy as np

import concourse.bass as bass
import concourse.mybir as mybir
from concourse.bass_utils import run_bass_kernel_spmd

F32 = mybir.dt.float32
BF16 = mybir.dt.bfloat16
AF = mybir.ActivationFunctionType
OP = mybir.AluOpType
AX = mybir.AxisListType

NCORE = 8
DM = 1024
DI = 2048
NS = 16
DTR = 64
TLOC = 2048
HALO = 3
NT = TLOC + HALO
PGM = DM // 128
PGI = DI // 128
TCH = 512
NTC = TLOC // TCH
SC = 256
NSC = TLOC // SC
EPS = 1e-5
ALPHA = 0.1


_RD_KEYS = ("in_", "in0", "in1", "lhsT", "rhs", "identity", "initial",
            "data0", "data1", "scalar", "scalar1", "scalar2", "scale", "bias")


class _Rec:

    def __init__(self):
        self.reads = []
        self.writes = []
        self.ops = []

    def __getattr__(self, name):
        def call(*a, **kw):
            if name in ("iota", "memset") and a and isinstance(a[0], bass.AP):
                self.writes.append(a[0])
            for k, v in kw.items():
                if k == "out":
                    if isinstance(v, bass.AP):
                        self.writes.append(v)
                elif k == "outs":
                    self.writes.extend(x for x in v if isinstance(x, bass.AP))
                elif k == "ins":
                    self.reads.extend(x for x in v if isinstance(x, bass.AP))
                elif k in _RD_KEYS and isinstance(v, bass.AP):
                    self.reads.append(v)
            self.ops.append((name, a, kw))
            return None
        return call


def _rng(ap):
    off = int(ap.offset)
    span = 1
    for d in ap.ap:
        span += abs(int(d[0])) * (int(d[1]) - 1)
    return (ap.tensor.name, off, off + span)


def _cost_ns(eng, dma, rw):
    ops = getattr(rw, "ops", [])
    if dma:
        b = 0
        for name, a, kw in ops:
            o = kw.get("out")
            if isinstance(o, bass.AP):
                el = 1
                for d in o.ap:
                    el *= int(d[1])
                es = 2 if "bf16" in str(o.tensor.dtype) else 4
                b += el * es
        return 900 + b * 0.0035
    tot = 0.0
    for name, a, kw in ops:
        if name == "collective_compute":
            b = 0
            for o in kw.get("outs", []):
                if isinstance(o, bass.AP):
                    el = 1
                    for d in o.ap:
                        el *= int(d[1])
                    b += el * 2
            tot += 3000 + b * 0.006
            continue
        ap = None
        for k in ("out", "in_", "in0"):
            v = kw.get(k)
            if isinstance(v, bass.AP):
                ap = v
                break
        if ap is None and a and isinstance(a[0], bass.AP):
            ap = a[0]
        el = 1
        if ap is not None:
            for d in list(ap.ap)[1:]:
                el *= int(d[1])
        if eng == "tensor":
            if name == "matmul":
                rhs = kw.get("rhs")
                n = 1
                if isinstance(rhs, bass.AP):
                    for d in list(rhs.ap)[1:]:
                        n *= int(d[1])
                tot += (n + 60) * 0.42
            else:
                tot += (el + 60) * 0.42
        elif eng == "vector":
            tot += (58 + el) * 1.05
        elif eng == "scalar":
            tot += (222 + el) * 0.84
        elif eng == "gpsimd":
            tot += (250 + el * 2.4) * 0.72
        else:
            tot += 100.0
    return tot


class Sched:
    ENGINES = ("sync", "tensor", "vector", "scalar", "gpsimd")
    RINGS = {"sync": "dR_sync", "scalar": "dR_scal", "gpsimd": "dR_gp"}

    def __init__(self, nc):
        self.nc = nc
        self.streams = {e: [] for e in self.ENGINES}
        self.cnt = {}
        self.enabled = True
        self.pos = {}
        self.emrec = {}
        self.lastw = {}
        self.lastr = {}
        self.carry = {e: {} for e in self.ENGINES}
        self.act_const = None
        self.phase = "init"
        self.trace = []

    def mksem(self, es, name):
        h = es.enter_context(self.nc.semaphore(name))
        self.cnt[name] = 0
        return h

    def emit(self, eng, fn, waits=(), inc=None, amt=1, dma=False):
        if not self.enabled:
            return None
        rw = _Rec()
        fn(rw)
        stream = ("dR_" + inc.name) if dma else eng
        pos = self.pos.get(stream, 0)
        know = self.carry[eng]
        deps = {}

        def add_dep(rec, p, raw_same=False):
            st = rec["stream"]
            if st == stream and not raw_same:
                return
            cur = deps.get(st)
            if cur is None or p > cur[0]:
                deps[st] = (p, rec)

        raw = []
        for w in waits:
            if not w:
                continue
            sem, val = w
            if val <= 0:
                continue
            r = self.emrec.get((sem.name, val))
            if r is not None:
                add_dep(r, r["pos"])
            else:
                raw.append((sem, val))
        reads = [_rng(a) for a in rw.reads]
        if eng == "scalar" and not dma and self.act_const is not None:
            reads.append(self.act_const)
        writes = [_rng(a) for a in rw.writes]
        for n, lo, hi in reads:
            for st, ent in self.lastw.get(n, {}).items():
                if ent[1] < hi and lo < ent[2]:
                    add_dep(ent[3], ent[0], raw_same=not dma)
        for n, lo, hi in writes:
            for dct in (self.lastw, self.lastr):
                for st, ent in dct.get(n, {}).items():
                    if ent[1] < hi and lo < ent[2]:
                        add_dep(ent[3], ent[0])
        wl = list(raw)
        if dma and pos > 0:
            prev = self.emrec.get((inc.name, self.cnt[inc.name]))
            if prev is not None and know.get(stream, -1) < prev["pos"]:
                wl.append((inc, prev["semval"]))
                for s2, p2 in prev["full"].items():
                    if know.get(s2, -1) < p2:
                        know[s2] = p2
        for st, (p, r) in sorted(deps.items(), key=lambda kv: -kv[1][0]):
            if know.get(st, -1) >= p:
                continue
            wl.append((r["sem"], r["semval"]))
            for s2, p2 in r["full"].items():
                if know.get(s2, -1) < p2:
                    know[s2] = p2
        full = dict(know)
        full[stream] = pos
        self.trace.append((stream, pos, [(r2["stream"], p2) for _, (p2, r2)
                                         in deps.items()],
                           _cost_ns(eng, dma, rw), self.phase))
        semval = None
        if inc is not None:
            self.cnt[inc.name] += amt
            semval = self.cnt[inc.name]
        rec = {"stream": stream, "pos": pos, "sem": inc, "semval": semval,
               "full": full}
        if inc is not None:
            self.emrec[(inc.name, semval)] = rec
        for n, lo, hi in writes:
            d = self.lastw.setdefault(n, {})
            old = d.get(stream)
            if old is not None:
                lo, hi = min(lo, old[1]), max(hi, old[2])
            d[stream] = (pos, lo, hi, rec)
        for n, lo, hi in reads:
            d = self.lastr.setdefault(n, {})
            old = d.get(stream)
            if old is not None:
                lo, hi = min(lo, old[1]), max(hi, old[2])
            d[stream] = (pos, lo, hi, rec)
        self.pos[stream] = pos + 1
        self.streams[eng].append((fn, wl, inc, amt))
        if inc is not None:
            return (inc, semval)
        return None

    def ev(self, sem):
        return (sem, self.cnt[sem.name])

    def build(self, block):
        def mk(stream):
            def body(eng):
                for fn, waits, inc, amt in stream:
                    for s, v in waits:
                        eng.wait_ge(s, v)
                    ins = fn(eng)
                    if inc is not None:
                        ins.then_inc(inc, amt)
            return body
        for name in self.ENGINES:
            if self.streams[name]:
                getattr(block, name)(mk(self.streams[name]))


def ap3(t, off, dims):
    base = t[:]
    part = list(base.ap[0])
    return bass.AP(base.tensor, base.offset + off, [part] + [list(d) for d in dims])


def build_nc(debug_names=(), stage="full"):
    nc = bass.Bass("TRN2", target_bir_lowering=False, debug=False)
    dram = {}

    def din(name, shape, dt=F32):
        dram[name] = nc.dram_tensor(name, shape, dt, kind="ExternalInput").ap()

    def dout(name, shape, dt=F32):
        dram[name] = nc.dram_tensor(name, shape, dt, kind="ExternalOutput").ap()

    def dscr(name, shape, dt=F32):
        dram[name] = nc.dram_tensor(name, shape, dt).ap()

    din("x_slab", [NT, DM])
    din("w_inT", [DM, 2 * DI]); din("w_outT", [DI, DM])
    din("xp_wT", [DI, DTR + 2 * NS]); din("dt_wT", [DTR, DI])
    din("ww_T", [DM, DM]); din("wr_T", [DM, DM])
    din("in_b", [128, 2 * PGI, 1]); din("out_b", [128, PGM, 1])
    din("conv_w", [128, PGI, 4]); din("conv_b", [128, PGI, 1])
    din("dt_b", [128, PGI, 1]); din("A_log", [128, PGI, NS]); din("D_ssm", [128, PGI, 1])
    din("n1w", [128, PGM, 1]); din("n2w", [128, PGM, 1])
    din("decay_r", [128, 1])
    din("sig_T", [DM, 64]); din("knots_r", [128, 8]); din("temp_r", [128, 1])
    din("cw1", [DM, 256]); din("cb1", [128, 2, 1]); din("cw2", [256, 2]); din("cb2", [128, 1])
    din("w1a", [2, 1024]); din("b1f", [128, 8, 1]); din("w2f", [128, 8, 1]); din("b2p", [64, 1])
    din("dirs", [64, DM]); din("gam", [128, PGM, 1]); din("bet", [128, PGM, 1]); din("oscale_r", [128, 1])
    din("eye_bf", [128, 128], BF16); din("eye_f32", [128, 128]); din("ones_bf", [128, 128], BF16)
    din("iota64", [128, 64]); din("iotam", [128, 8]); din("blk8", [128, 8, 64], BF16)
    din("c0mask", [128, 1]); din("chainmask", [128, 7]); din("chaininv", [128, 7]); din("prevsel", [128, 8])
    din("iota256", [1, SC]); din("iotaT", [1, TLOC]); din("consts8", [128, 8])
    dout("out", [TLOC, DM])

    dscr("b_winT", [DM, 2 * DI], BF16); dscr("b_woutT", [DI, DM], BF16)
    dscr("b_xpT", [DI, 96], BF16); dscr("b_dtT", [DTR, DI], BF16)
    dscr("b_wwT", [DM, DM], BF16); dscr("b_wrT", [DM, DM], BF16); dscr("b_cw1", [DM, 256], BF16)
    dscr("b_wwTlo", [DM, DM], BF16); dscr("b_wrTlo", [DM, DM], BF16)
    dscr("s_xfm", [128, PGM, TLOC])
    dscr("s_s1", [128, PGM, NT])
    dscr("s_o2", [128, PGM, TLOC])
    dscr("s_h1", [128, PGM, NT], BF16)
    dscr("s_xm", [128, PGI, TLOC], BF16)
    dscr("s_res", [128, PGI, TLOC])
    dscr("s_g", [128, PGI, TLOC], BF16)
    dscr("s_bc", [2 * NS, TLOC], BF16)
    dscr("s_delta", [128, PGI, TLOC])
    dscr("s_cdl", [128, PGI, TLOC], BF16)
    dscr("s_y", [128, PGI, TLOC])
    dscr("s_reads", [128, PGM, TLOC])
    dscr("s_row", [4, TLOC])
    dscr("cc_ch_i", [2, 128, PGI, NS])
    dscr("cc_ch_o", [NCORE, 2, 128, PGI, NS])
    dscr("cc_h1_i", [128, PGM, HALO])
    dscr("cc_h1_o", [NCORE, 128, PGM, HALO])
    dscr("cc_wk_i", [128, PGM, 1])
    dscr("cc_wk_o", [NCORE, 128, PGM, 1])
    dscr("cc_ag_i", [128, PGM, DM], BF16)
    dscr("cc_ag_o", [NCORE, 128, PGM, DM], BF16)

    with contextlib.ExitStack() as es:
        sch = Sched(nc)
        globals()["LAST_SCHED"] = sch
        q_ld = sch.mksem(es, "q_ld")
        q_g = sch.mksem(es, "q_g")
        q_s = sch.mksem(es, "q_s")
        q_st = sch.mksem(es, "q_st")
        e_pe = sch.mksem(es, "e_pe")
        e_dv = sch.mksem(es, "e_dv")
        e_ac = sch.mksem(es, "e_ac")
        e_gp = sch.mksem(es, "e_gp")
        q_l2 = sch.mksem(es, "q_l2")
        q_l3 = sch.mksem(es, "q_l3")
        q_l4 = sch.mksem(es, "q_l4")

        rot = {"on": False, "i": 0}

        def dma(qeng, sem, out, in_, waits=(), slow=False):
            if rot["on"] and sem is q_ld:
                rot["i"] = (rot["i"] + 1) % 4
                sem = (q_ld, q_l2, q_l3, q_l4)[rot["i"]]
            return sch.emit(qeng, lambda e, o=out, i=in_, s=slow: e.dma_start(
                out=o, in_=i, allow_slow_non_contiguous=s), waits=waits, inc=sem,
                amt=16, dma=True)

        def onpe(fn, waits=()):
            return sch.emit("tensor", fn, waits=waits, inc=e_pe)

        def ondv(fn, waits=()):
            return sch.emit("vector", fn, waits=waits, inc=e_dv)

        def onac(fn, waits=()):
            return sch.emit("scalar", fn, waits=waits, inc=e_ac)

        def ongp(fn, waits=()):
            return sch.emit("gpsimd", fn, waits=waits, inc=e_gp)

        def barrier():
            return [sch.ev(s) for s in (q_ld, q_l2, q_l3, q_l4, q_g, q_s, q_st,
                                        e_pe, e_dv, e_ac, e_gp)]

        def dvfence():
            return ondv(lambda e: e.tensor_reduce(out=k_fence[:, 1:2],
                                                  in_=ap3(k_fence, 0, [[0, 256]]),
                                                  axis=AX.X, op=OP.add))

        def tap(name, ap, waits):
            if name in debug_names:
                dout("tap_" + name, list(ap.shape), ap.dtype)
                dma("gpsimd", q_g, dram["tap_" + name], ap, waits=waits)

        S = lambda n, sh, dt=F32: es.enter_context(nc.sbuf_tensor(n, sh, dt))
        P = lambda n, sh: es.enter_context(nc.psum_tensor(n, sh, F32))

        k_eye = S("k_eye", [128, 128], BF16)
        k_eyef = S("k_eyef", [128, 128])
        k_ones = S("k_ones", [128, 128], BF16)
        k_inb = S("k_inb", [128, 2 * PGI, 1])
        k_outb = S("k_outb", [128, PGM, 1])
        k_cw = S("k_cw", [128, PGI, 4])
        k_cb = S("k_cb", [128, PGI, 1])
        k_dtb = S("k_dtb", [128, PGI, 1])
        k_dssm = S("k_dssm", [128, PGI, 1])
        k_n1 = S("k_n1", [128, PGM, 1])
        k_n2 = S("k_n2", [128, PGM, 1])
        k_gam = S("k_gam", [128, PGM, 1])
        k_bet = S("k_bet", [128, PGM, 1])
        k_knots = S("k_knots", [128, 8])
        k_kA = S("k_kA", [128, 7])
        k_kB = S("k_kB", [128, 7])
        k_io64 = S("k_io64", [128, 64])
        k_iom = S("k_iom", [128, 8])
        k_blk8 = S("k_blk8", [128, 8, 64], BF16)
        k_c0 = S("k_c0", [128, 1])
        k_chm = S("k_chm", [128, 7])
        k_chi = S("k_chi", [128, 7])
        k_psel = S("k_psel", [128, 8])
        k_osc = S("k_osc", [128, 1])
        k_cb1 = S("k_cb1", [128, 2, 1])
        k_cb2 = S("k_cb2", [128, 1])
        k_b1f = S("k_b1f", [128, 8, 1])
        k_w2f = S("k_w2f", [128, 8, 1])
        k_b2p = S("k_b2p", [64, 1])
        k_cw2 = S("k_cw2", [128, 2, 2], BF16)
        k_w1a = S("k_w1a", [2, 1024], BF16)
        k_dirs = S("k_dirs", [64, DM], BF16)
        k_qT = S("k_qT", [128, PGM, 64], BF16)
        k_csT = S("k_csT", [128, PGM, 8], BF16)
        k_misc = S("k_misc", [128, 80])
        k_fence = S("k_fence", [128, 8])
        k_tmp8 = S("k_tmp8", [128, 8])
        k_c8 = S("k_c8", [128, 8])
        sch.act_const = ("k_c8", 0, 9)
        k_MT = S("k_MT", [128, 2, SC])
        k_gw = S("k_gw", [128, 2, 1])
        k_gwi = S("k_gwi", [128, 2, 1])
        k_g256 = S("k_g256", [128, 1])
        k_g2048 = S("k_g2048", [128, 1])
        Apk = S("Apk", [128, PGI, NS])
        states = S("states", [128, PGI, NS])
        hend = S("hend", [128, PGI, NS])
        pend = S("pend", [128, PGI, NS])
        cdend = S("cdend", [128, PGI, 1])
        hin = S("hin", [128, PGI, NS])
        lnh = S("lnh", [128, PGI, NS])
        sgh = S("sgh", [128, PGI, NS])
        wk1 = S("wk1", [128, 2048])
        wk2 = S("wk2", [128, 2048])
        wkb = S("wkb", [128, 2048], BF16)
        wkb2 = S("wkb2", [128, 2048], BF16)
        sqb = S("sqb", [128, PGM, TCH], BF16)
        sA = S("sA", [128, PGM, NT])
        st_box = {}

        p1 = P("p1", [128, 512])
        p2 = P("p2", [128, 512])
        p3 = P("p3", [128, 512])
        p4 = P("p4", [128, 512])
        pbig = P("pbig", [128, 4, 512])

        blockctx = es.enter_context(nc.Block())
        STAGES = ["p0", "tr", "ln0", "m1", "m2", "att", "full"]
        SN_OFF = 2
        SN = STAGES.index(stage)

        sch.phase = "p0"
        for t, n in [(k_eye, "eye_bf"), (k_eyef, "eye_f32"), (k_ones, "ones_bf"),
                     (k_inb, "in_b"), (k_outb, "out_b"), (k_cw, "conv_w"), (k_cb, "conv_b"),
                     (k_dtb, "dt_b"), (k_dssm, "D_ssm"), (k_n1, "n1w"), (k_n2, "n2w"),
                     (k_gam, "gam"), (k_bet, "bet"), (k_knots, "knots_r"),
                     (k_io64, "iota64"), (k_iom, "iotam"), (k_blk8, "blk8"),
                     (k_c0, "c0mask"), (k_chm, "chainmask"), (k_chi, "chaininv"), (k_psel, "prevsel"),
                     (k_osc, "oscale_r"), (k_cb1, "cb1"), (k_cb2, "cb2"),
                     (k_b1f, "b1f"), (k_w2f, "w2f"), (k_b2p, "b2p"), (k_c8, "consts8")]:
            dma("sync", q_ld, t[:], dram[n])
        ld_const = sch.ev(q_ld)
        ondv(lambda e: e.memset(k_fence[:], 0.0))
        rot["on"] = True
        for i, v in enumerate([EPS, 1.0, -1.0, -0.5, 256.0, 2048.0, 0.5, 0.0]):
            nc.const_aps.register(F32, v, k_c8[:, i:i + 1])

        ev = dma("sync", q_ld, wk1[:, 0:PGI * NS], dram["A_log"].rearrange("p a b -> p (a b)"))
        onac(lambda e: e.activation(out=wk2[:, 0:PGI * NS], in_=wk1[:, 0:PGI * NS], func=AF.Exp),
             waits=[ev])
        ondv(lambda e: e.tensor_scalar_mul(out=Apk[:].rearrange("p a b -> p (a b)"),
                                           in0=wk2[:, 0:PGI * NS], scalar1=-1.0),
             waits=[sch.ev(e_ac)])

        def prep_w(src, dst, rows, cols, lo_dst=None, ping=[0]):
            for r0 in range(0, rows, 128):
                rr = min(128, rows - r0)
                for c0 in range(0, cols, 2048):
                    cc = min(2048, cols - c0)
                    if lo_dst is None:
                        wsrc = wk1 if ping[0] == 0 else wk2
                        wdst = wkb if ping[0] == 0 else wkb2
                        ping[0] ^= 1
                    else:
                        wsrc, wdst = wk1, wkb
                    ev1 = dma("sync", q_ld, wsrc[0:rr, 0:cc], dram[src][r0:r0 + rr, c0:c0 + cc])
                    ev2 = ondv(lambda e, rr=rr, cc=cc, wsrc=wsrc, wdst=wdst: e.tensor_copy(
                        out=wdst[0:rr, 0:cc], in_=wsrc[0:rr, 0:cc]), waits=[ev1])
                    dma("scalar", q_s, dram[dst][r0:r0 + rr, c0:c0 + cc], wdst[0:rr, 0:cc],
                        waits=[ev2])
                    if lo_dst is not None:
                        ondv(lambda e, rr=rr, cc=cc: e.tensor_copy(out=wk2[0:rr, 0:cc],
                                                                   in_=wkb[0:rr, 0:cc]))
                        ev3 = ondv(lambda e, rr=rr, cc=cc: e.tensor_sub(out=wkb2[0:rr, 0:cc],
                                                                        in0=wk1[0:rr, 0:cc],
                                                                        in1=wk2[0:rr, 0:cc]))
                        dma("scalar", q_s, dram[lo_dst][r0:r0 + rr, c0:c0 + cc], wkb2[0:rr, 0:cc],
                            waits=[ev3])

        for a in [("w_inT", "b_winT", DM, 2 * DI), ("w_outT", "b_woutT", DI, DM),
                  ("xp_wT", "b_xpT", DI, 96), ("dt_wT", "b_dtT", DTR, DI),
                  ("ww_T", "b_wwT", DM, DM, "b_wwTlo"), ("wr_T", "b_wrT", DM, DM, "b_wrTlo"),
                  ("cw1", "b_cw1", DM, 256)]:
            prep_w(*a)
        ev_wprep = sch.ev(q_s)

        ev1 = dma("sync", q_ld, wk1[:, 0:4].rearrange("p (a c) -> p a c", a=2),
                  dram["cw2"].rearrange("(a p) c -> p a c", p=128),
                  waits=[sch.ev(e_dv)])
        ondv(lambda e: e.tensor_copy(out=k_cw2[:].rearrange("p a b -> p (a b)"), in_=wk1[:, 0:4]),
             waits=[ev1])
        ev1 = dma("sync", q_ld, wk1[0:2, 0:1024], dram["w1a"], waits=[sch.ev(e_dv)])
        ondv(lambda e: e.tensor_copy(out=k_w1a[:], in_=wk1[0:2, 0:1024]), waits=[ev1])
        ev1 = dma("sync", q_ld, wk1[0:64, 0:DM], dram["dirs"], waits=[sch.ev(e_dv)])
        ondv(lambda e: e.tensor_copy(out=k_dirs[:], in_=wk1[0:64, 0:DM]), waits=[ev1])

        ondv(lambda e: e.tensor_sub(out=k_misc[:, 0:7], in0=k_knots[:, 1:8],
                                    in1=k_knots[:, 0:7]), waits=[ld_const])
        dvfence()
        ondv(lambda e: e.tensor_mul(out=k_misc[:, 8:15], in0=k_misc[:, 0:7],
                                    in1=k_io64[:, 0:7]))
        dvfence()
        ondv(lambda e: e.tensor_sub(out=k_misc[:, 8:15], in0=k_knots[:, 0:7],
                                    in1=k_misc[:, 8:15]))
        dvfence()
        def build_calib(e):
            e.tensor_copy(out=k_kA[:, 0:7], in_=k_misc[:, 8:15])
            return e.tensor_copy(out=k_kB[:, 0:7], in_=k_misc[:, 0:7])
        ondv(build_calib)
        dvfence()
        def build_calib2(e):
            e.tensor_sub(out=k_kA[:, 1:7], in0=k_kA[:, 1:7], in1=k_misc[:, 8:14])
            return e.tensor_sub(out=k_kB[:, 1:7], in0=k_kB[:, 1:7], in1=k_misc[:, 0:6])
        ondv(build_calib2)

        ev1 = dma("sync", q_ld, k_misc[:, 16:17], dram["temp_r"])
        ondv(lambda e: e.tensor_scalar_add(out=k_misc[:, 64:72],
                                           in0=ap3(k_misc, 16, [[0, 8]]), scalar1=1e-6),
             waits=[ev1])
        dvfence()
        ondv(lambda e: e.reciprocal(out=k_misc[:, 64:72], in_=k_misc[:, 64:72]))

        ev1 = dma("sync", q_ld, k_misc[:, 17:18], dram["decay_r"])
        ondv(lambda e: e.tensor_copy(out=k_misc[:, 32:40],
                                     in_=ap3(k_misc, 17, [[0, 8]])), waits=[ev1, ld_const])
        def g_a(e):
            e.activation(out=k_misc[:, 40:48], in_=k_misc[:, 32:40], func=AF.Abs)
            e.activation(out=k_misc[:, 40:48], in_=k_misc[:, 40:48], func=AF.Exp, scale=-1.0)
            e.activation(out=k_misc[:, 40:48], in_=k_misc[:, 40:48], func=AF.Ln, bias=1.0)
            return e.activation(out=k_misc[:, 48:56], in_=k_misc[:, 32:40], func=AF.Relu,
                                scale=-1.0)
        onac(g_a, waits=[sch.ev(e_dv)])
        ondv(lambda e: e.scalar_tensor_tensor(out=k_misc[:, 56:64], in0=k_misc[:, 40:48],
                                              scalar=-1.0, in1=k_misc[:, 48:56],
                                              op0=OP.mult, op1=OP.subtract),
             waits=[sch.ev(e_ac)])
        onac(lambda e: e.activation(out=k_misc[:, 24:32], in_=k_misc[:, 56:64],
                                    func=AF.Relu, scale=-1.0), waits=[sch.ev(e_dv)])
        ev_lg = sch.ev(e_ac)
        def g_c(e):
            e.activation(out=k_g256[:], in_=k_misc[:, 56:57], func=AF.Exp, scale=float(SC))
            return e.activation(out=k_g2048[:], in_=k_misc[:, 56:57], func=AF.Exp,
                                scale=float(TLOC))
        onac(g_c, waits=[sch.ev(e_dv)])
        def g_iotas(e):
            e.iota(k_misc[:, 72:73], [[0, 1]], base=SC - 1, channel_multiplier=-1,
                   allow_small_or_imprecise_dtypes=True)
            return e.iota(k_misc[:, 73:74], [[0, 1]], base=SC - 129, channel_multiplier=-1,
                          allow_small_or_imprecise_dtypes=True)
        ongp(g_iotas, waits=[ld_const])
        def g_d(e):
            e.tensor_scalar(out=k_misc[:, 74:75], in0=k_misc[:, 72:73],
                            scalar1=k_misc[:, 24:25], scalar2=-1.0, op0=OP.mult, op1=OP.mult)
            return e.tensor_scalar(out=k_misc[:, 75:76], in0=k_misc[:, 73:74],
                                   scalar1=k_misc[:, 24:25], scalar2=-1.0, op0=OP.mult,
                                   op1=OP.mult)
        ondv(g_d, waits=[sch.ev(e_gp), ev_lg])
        def g_e(e):
            e.activation(out=k_gw[:, 0, :], in_=k_misc[:, 74:75], func=AF.Exp)
            e.activation(out=k_gw[:, 1, :], in_=k_misc[:, 75:76], func=AF.Exp)
            e.activation(out=k_gwi[:, 0, :], in_=k_misc[:, 74:75], func=AF.Exp, scale=-1.0)
            return e.activation(out=k_gwi[:, 1, :], in_=k_misc[:, 75:76], func=AF.Exp,
                                scale=-1.0)
        onac(g_e, waits=[sch.ev(e_dv)])

        ev_i = dma("sync", q_ld, wk1[0:1, 0:SC], dram["iota256"])
        ev_j = dma("sync", q_ld, wk2[0:1, 0:TLOC], dram["iotaT"])
        def g_rows(e):
            e.tensor_scalar(out=wk1[0:1, 0:SC], in0=wk1[0:1, 0:SC], scalar1=k_misc[0:1, 24:25],
                            scalar2=-1.0, op0=OP.mult, op1=OP.mult)
            return e.tensor_scalar(out=wk2[0:1, 0:TLOC], in0=wk2[0:1, 0:TLOC],
                                   scalar1=k_misc[0:1, 24:25], scalar2=-1.0, op0=OP.mult,
                                   op1=OP.mult)
        ondv(g_rows, waits=[ev_i, ev_j, ev_lg])
        def g_rows2(e):
            e.activation(out=wk1[0:1, 0:SC], in_=wk1[0:1, 0:SC], func=AF.Exp)
            return e.activation(out=wk2[0:1, 0:TLOC], in_=wk2[0:1, 0:TLOC], func=AF.Exp)
        onac(g_rows2, waits=[sch.ev(e_dv)])
        ev1 = dma("scalar", q_s, dram["s_row"][1:2, 0:SC], wk1[0:1, 0:SC],
                  waits=[sch.ev(e_ac), ev_wprep])
        ev2 = dma("scalar", q_s, dram["s_row"][0:1, 0:TLOC], wk2[0:1, 0:TLOC], waits=[sch.ev(e_ac)])
        ev_grow = [ev1, ev2]

        def mt_iota(e):
            e.iota(wk1[:, 0:SC], [[1, SC]], base=-1, channel_multiplier=-1,
                   allow_small_or_imprecise_dtypes=True)
            return e.iota(wk1[:, SC:2 * SC], [[1, SC]], base=-129, channel_multiplier=-1,
                          allow_small_or_imprecise_dtypes=True)
        ongp(mt_iota, waits=[sch.ev(e_ac), sch.ev(q_s)])
        ondv(lambda e: e.tensor_scalar(out=wk2[:, 0:2 * SC], in0=wk1[:, 0:2 * SC],
                                       scalar1=k_misc[:, 24:25], scalar2=-1.0, op0=OP.mult,
                                       op1=OP.mult),
             waits=[sch.ev(e_gp)])
        onac(lambda e: e.activation(out=wk2[:, 0:2 * SC], in_=wk2[:, 0:2 * SC], func=AF.Exp),
             waits=[sch.ev(e_dv)])
        def mt_mask(e):
            e.affine_select(out=wk1[:, 0:SC], in_=wk2[:, 0:SC], pattern=[[1, SC]],
                            compare_op=OP.is_ge, fill=0.0, base=-1, channel_multiplier=-1)
            return e.affine_select(out=wk1[:, SC:2 * SC], in_=wk2[:, SC:2 * SC], pattern=[[1, SC]],
                                   compare_op=OP.is_ge, fill=0.0, base=-129, channel_multiplier=-1)
        ongp(mt_mask, waits=[sch.ev(e_ac)])
        ondv(lambda e: e.tensor_copy(out=k_MT[:].rearrange("p a b -> p (a b)"), in_=wk1[:, 0:2 * SC]),
             waits=[sch.ev(e_gp)])

        for pg in range(PGM):
            ev1 = dma("sync", q_ld, wk1[:, 0:64], dram["sig_T"][pg * 128:(pg + 1) * 128, :],
                      waits=[sch.ev(e_dv)])
            def qf(e, pg=pg):
                e.tensor_single_scalar(out=wk2[:, 0:64], in_=wk1[:, 0:64], scalar=0.3, op=OP.is_gt)
                return e.tensor_single_scalar(out=wk2[:, 64:128], in_=wk1[:, 0:64], scalar=-0.3,
                                              op=OP.is_ge)
            ondv(qf, waits=[ev1])
            dvfence()
            ondv(lambda e: e.scalar_tensor_tensor(out=wk2[:, 0:64], in0=wk2[:, 0:64],
                                                  scalar=-1.0, in1=wk2[:, 64:128],
                                                  op0=OP.add, op1=OP.add))
            dvfence()
            def qf2(e, pg=pg):
                e.tensor_copy(out=k_qT[:, pg, :], in_=wk2[:, 0:64])
                return e.tensor_reduce(out=wk2[:, 128:136],
                                       in_=wk2[:, 0:64].rearrange("p (c j) -> p c j", c=8),
                                       axis=AX.X, op=OP.add)
            ondv(qf2)
            onac(lambda e: e.activation(out=wk2[:, 136:144], in_=wk2[:, 128:136], func=AF.Sign),
                 waits=[sch.ev(e_dv)])
            ondv(lambda e, pg=pg: e.tensor_copy(out=k_csT[:, pg, :], in_=wk2[:, 136:144]),
                 waits=[sch.ev(e_ac)])

        sch.phase = "p1_transpose"
        stA_cm = [nc.sbuf_tensor("stA%d" % i, [128, NT], F32) for i in range(3)]
        st_box["mu"], st_box["ri"], st_box["t"] = [c.__enter__() for c in stA_cm]
        if SN < 1:
            sch.enabled = False
        st_mu0 = st_box["mu"]
        NTT = (NT + 127) // 128
        for tt in range(NTT):
            r0 = tt * 128
            rr = min(128, NT - r0)
            xo = (tt % 2) * DM
            evm = None
            if rr < 128:
                evm = ondv(lambda e, xo=xo: e.memset(st_mu0[:, xo:xo + DM], 0.0))
            evl = dma("sync", q_ld, st_mu0[0:rr, xo:xo + DM], dram["x_slab"][r0:r0 + rr, :],
                      waits=[evm] if evm else [])
            for dg in range(PGM):
                evt = onpe(lambda e, dg=dg, xo=xo: e.transpose(
                    out=p1[:, (dg % 4) * 128:(dg % 4) * 128 + 128],
                    in_=st_mu0[:, xo + dg * 128:xo + (dg + 1) * 128],
                    identity=k_eyef[:]), waits=[evl, ld_const])
                ondv(lambda e, dg=dg, r0=r0, rr=rr: e.tensor_copy(
                    out=sA[:, dg, r0:r0 + rr],
                    in_=p1[0:128, (dg % 4) * 128:(dg % 4) * 128 + rr]), waits=[evt])
        ev_xfm = sch.ev(e_dv)
        for pg in range(PGM):
            dma("scalar", q_s, dram["s_xfm"][:, pg, :], sA[:, pg, HALO:NT], waits=[ev_xfm])

        if SN < 2:
            sch.enabled = False
        def emit_ln(src, dst, normw, rms_only, width, cast_bf=None, gb=None, bb=None):
            st_mu, st_ri, st_t = st_box["mu"], st_box["ri"], st_box["t"]
            for nc0 in range(0, width, TCH):
                w = min(TCH, width - nc0)
                eva = onac(lambda e, nc0=nc0, w=w: e.activation(out=sqb[:, :, 0:w],
                                                                in_=src[:, :, nc0:nc0 + w],
                                                                func=AF.Square),
                           waits=[])
                last = None
                for kg in range(PGM):
                    last = onpe(lambda e, kg=kg, w=w: e.matmul(out=p3[:, 0:w], lhsT=k_ones[:],
                                                               rhs=sqb[:, kg, 0:w],
                                                               start=(kg == 0), stop=(kg == PGM - 1)),
                                waits=[eva])
                ondv(lambda e, nc0=nc0, w=w: e.tensor_scalar_mul(out=st_t[:, nc0:nc0 + w],
                                                                 in0=p3[:, 0:w], scalar1=1.0 / DM),
                     waits=[last])
                if not rms_only:
                    evc = onac(lambda e, nc0=nc0, w=w: e.activation(out=sqb[:, :, 0:w],
                                                                    in_=src[:, :, nc0:nc0 + w],
                                                                    func=AF.Identity),
                               waits=[last])
                    last = None
                    for kg in range(PGM):
                        last = onpe(lambda e, kg=kg, w=w: e.matmul(out=p4[:, 0:w], lhsT=k_ones[:],
                                                                   rhs=sqb[:, kg, 0:w],
                                                                   start=(kg == 0), stop=(kg == PGM - 1)),
                                    waits=[evc])
                    ondv(lambda e, nc0=nc0, w=w: e.tensor_scalar_mul(out=st_mu[:, nc0:nc0 + w],
                                                                     in0=p4[:, 0:w], scalar1=1.0 / DM),
                         waits=[last])
            ev_red = sch.ev(e_dv)
            def fin1(e):
                if not rms_only:
                    e.tensor_mul(out=st_ri[:, 0:width], in0=st_mu[:, 0:width], in1=st_mu[:, 0:width])
                    return e.tensor_sub(out=st_t[:, 0:width], in0=st_t[:, 0:width],
                                        in1=st_ri[:, 0:width])
                return e.tensor_copy(out=st_ri[:, 0:width], in_=st_t[:, 0:width])
            ondv(fin1, waits=[ev_red])
            def fin2(e):
                e.activation(out=st_ri[:, 0:width], in_=st_t[:, 0:width], func=AF.Ln, bias=EPS)
                return e.activation(out=st_ri[:, 0:width], in_=st_ri[:, 0:width], func=AF.Exp,
                                    scale=-0.5)
            onac(fin2, waits=[sch.ev(e_dv)])
            ev_r = sch.ev(e_ac)
            def app(e):
                last = None
                for pg in range(PGM):
                    if rms_only:
                        last = e.scalar_tensor_tensor(out=dst[:, pg, 0:width], in0=src[:, pg, 0:width],
                                                      scalar=normw[:, pg, :], in1=st_ri[:, 0:width],
                                                      op0=OP.mult, op1=OP.mult)
                    else:
                        e.tensor_sub(out=dst[:, pg, 0:width], in0=src[:, pg, 0:width],
                                     in1=st_mu[:, 0:width])
                        last = e.tensor_mul(out=dst[:, pg, 0:width], in0=dst[:, pg, 0:width],
                                            in1=st_ri[:, 0:width])
                    if gb is not None:
                        last = e.tensor_scalar(out=dst[:, pg, 0:width], in0=dst[:, pg, 0:width],
                                               scalar1=gb[:, pg, :], scalar2=bb[:, pg, :],
                                               op0=OP.mult, op1=OP.add)
                    if cast_bf is not None:
                        last = e.tensor_copy(out=cast_bf[:, pg, 0:width], in_=dst[:, pg, 0:width])
                return last
            ondv(app, waits=[ev_r])
            return sch.ev(e_dv)

        ev_ln0 = emit_ln(sA, sA, None, False, NT)
        tap("ln0", sA[:, :, :], [ev_ln0])
        if SN < 3:
            sch.enabled = False
        for pg in range(PGM):
            dma("scalar", q_s, dram["s_s1"][:, pg, :], sA[:, pg, :], waits=[ev_ln0])
        ev_s1sp = sch.ev(q_s)

        def emit_mamba(which, arh, wsl):
            normw = k_n1 if which == 1 else k_n2
            ev_rms = emit_ln(sA, sA, normw, True, NT, cast_bf=arh)
            tap("rms%d" % which, arh[:, :, :], [ev_rms])
            for pg in range(PGM):
                dma("scalar", q_s, dram["s_h1"][:, pg, :], arh[:, pg, :], waits=[ev_rms])
            ev_h1sp = sch.ev(q_s)

            sch.phase = "m_inproj"
            rowb = wkb
            with nc.sbuf_tensor("rowb%d" % which, [128, NT], BF16) as rowb2:
                for mg in range(PGI):
                    evw = dma("sync", q_ld, wsl[:].rearrange("p a b -> p a b"),
                              dram["b_winT"][:, mg * 128:(mg + 1) * 128]
                              .rearrange("(a p) c -> p a c", p=128),
                              waits=[])
                    for inch, nc0 in enumerate(range(0, NT, TCH)):
                        w = min(TCH, NT - nc0)
                        pb = p1 if inch % 2 == 0 else p2
                        def ipq(e, nc0=nc0, w=w, pb=pb):
                            r = None
                            for kg in range(PGM):
                                r = e.matmul(out=pb[:, 0:w], lhsT=wsl[:, kg, 0:128],
                                             rhs=arh[:, kg, nc0:nc0 + w], start=(kg == 0),
                                             stop=(kg == PGM - 1))
                            return r
                        last = onpe(ipq, waits=[evw, ev_rms])
                        ondv(lambda e, mg=mg, nc0=nc0, w=w, pb=pb: e.tensor_scalar(
                            out=rowb2[:, nc0:nc0 + w], in0=pb[:, 0:w], scalar1=k_inb[:, mg, :],
                            scalar2=None, op0=OP.add), waits=[last])
                    evrow = ondv(lambda e: e.tensor_scalar(out=rowb2[:, 0:HALO], in0=rowb2[:, 0:HALO],
                                                           scalar1=k_c0[:, 0:1], scalar2=None,
                                                           op0=OP.mult))
                    evd = ondv(lambda e, mg=mg: [e.tensor_scalar(
                        out=wkb2[:, 128 * k:128 * (k + 1)], in0=k_eye[:],
                        scalar1=k_cw[:, mg, k:k + 1], scalar2=None, op0=OP.mult)
                        for k in range(4)][-1], waits=[evrow])
                    for inch, nc0 in enumerate(range(0, TLOC, TCH)):
                        pb = p3 if inch % 2 == 0 else p4
                        def cvq(e, nc0=nc0, pb=pb):
                            r = None
                            for k in range(4):
                                r = e.matmul(out=pb[:, 0:TCH], lhsT=wkb2[:, 128 * k:128 * (k + 1)],
                                             rhs=rowb2[:, nc0 + k:nc0 + k + TCH], start=(k == 0),
                                             stop=(k == 3))
                            return r
                        last = onpe(cvq, waits=[evd])
                        eva = onac(lambda e, mg=mg, pb=pb: e.activation(
                            out=wkb[:, 0:TCH], in_=pb[:, 0:TCH], func=AF.Silu, bias=k_cb[:, mg, :]),
                            waits=[last])
                        dma("scalar", q_s, dram["s_xm"][:, mg, nc0:nc0 + TCH], wkb[:, 0:TCH],
                            waits=[eva])
            ev_xm = sch.ev(q_s)
            tap("xm%d" % which, wkb[:, 0:TCH], [ev_xm])

            sch.phase = "m_xdbl"
            xdb = st_box["t"]
            evw = dma("sync", q_ld, wkb2[:, 0:PGI * 96].rearrange("p (a c) -> p a c", a=PGI),
                      dram["b_xpT"].rearrange("(a p) c -> p a c", p=128),
                      waits=barrier())
            for inch, nc0 in enumerate(range(0, TLOC, TCH)):
                pb = p1 if inch % 2 == 0 else p2
                last = None
                for kh in range(2):
                    evxs = [dma("sync", q_ld, sqb[:, kg % PGM, 0:TCH],
                                dram["s_xm"][:, kg, nc0:nc0 + TCH], waits=[ev_xm])
                            for kg in range(kh * 8, kh * 8 + 8)]
                    def xdq(e, kh=kh, pb=pb):
                        r = None
                        for kg in range(kh * 8, kh * 8 + 8):
                            r = e.matmul(
                                out=pb[0:96, 0:TCH], lhsT=wkb2[:, kg * 96:(kg + 1) * 96],
                                rhs=sqb[:, kg % PGM, 0:TCH], start=(kg == 0),
                                stop=(kg == PGI - 1))
                        return r
                    last = onpe(xdq, waits=evxs + [evw])
                ondv(lambda e, nc0=nc0, pb=pb: e.tensor_copy(out=xdb[0:96, nc0:nc0 + TCH],
                                                             in_=pb[0:96, 0:TCH]), waits=[last])
            ev_xdb = sch.ev(e_dv)
            tap("xdb%d" % which, xdb[0:96, 0:TLOC], [ev_xdb])
            ondv(lambda e: e.tensor_copy(out=wkb[0:32, 0:TLOC], in_=xdb[64:96, 0:TLOC]),
                 waits=[ev_xdb])
            ev_bc = dma("scalar", q_s, dram["s_bc"], wkb[0:32, 0:TLOC], waits=[])

            sch.phase = "m_delta"
            evw = dma("sync", q_ld, wkb2[0:64, 0:DI], dram["b_dtT"], waits=barrier())
            sqf = sqb[:].rearrange("p a b -> p (a b)")
            ev_x64 = ondv(lambda e: e.tensor_copy(out=sqf[0:64, 0:TLOC], in_=xdb[0:64, 0:TLOC]),
                          waits=[ev_xdb, evw])
            ondv(lambda e: e.memset(states[:].rearrange("p a b -> p (a b)"), 0.0))
            ondv(lambda e: e.memset(cdend[:].rearrange("p a b -> p (a b)"), 0.0))
            ondv(lambda e: e.memset(wk2[:, 0:TLOC], 1.0))
            ev_ones = sch.ev(e_dv)
            for pg in range(PGI):
                for itc in range(NTC):
                    nc0 = itc * TCH
                    pb = p3 if itc % 2 == 0 else p4
                    evq = onpe(lambda e, pg=pg, nc0=nc0, pb=pb: e.matmul(
                        out=pb[:, 0:TCH], lhsT=wkb2[0:64, pg * 128:(pg + 1) * 128],
                        rhs=sqf[0:64, nc0:nc0 + TCH], start=True, stop=True),
                        waits=[evw, ev_x64])
                    def sp(e, pg=pg, pb=pb, nc0=nc0):
                        smu = st_box["mu"]
                        e.activation(out=wk1[:, nc0:nc0 + TCH], in_=pb[:, 0:TCH], func=AF.Relu,
                                     bias=k_dtb[:, pg, :])
                        e.activation(out=smu[:, 0:TCH], in_=pb[:, 0:TCH], func=AF.Abs,
                                     bias=k_dtb[:, pg, :])
                        e.activation(out=smu[:, 0:TCH], in_=smu[:, 0:TCH], func=AF.Exp,
                                     scale=-1.0)
                        return e.activation(out=smu[:, 0:TCH], in_=smu[:, 0:TCH],
                                            func=AF.Ln, bias=1.0)
                    evsp = onac(sp, waits=[evq])
                    ondv(lambda e, nc0=nc0: e.tensor_add(out=wk1[:, nc0:nc0 + TCH],
                                                         in0=wk1[:, nc0:nc0 + TCH],
                                                         in1=st_box["mu"][:, 0:TCH]),
                         waits=[evsp, ev_ones, ev_bc])
                def cds(e, pg=pg):
                    e.tensor_tensor_scan(out=wkb[:, 0:TLOC], data0=wk2[:, 0:TLOC],
                                         data1=wk1[:, 0:TLOC], initial=0.0,
                                         op0=OP.mult, op1=OP.add)
                    return e.tensor_reduce(out=cdend[:, pg, :], in_=wk1[:, 0:TLOC], axis=AX.X,
                                           op=OP.add)
                ondv(cds)
                ev2 = sch.ev(e_dv)
                dma("scalar", q_s, dram["s_delta"][:, pg, :], wk1[:, 0:TLOC], waits=[ev2])
                dma("scalar", q_s, dram["s_cdl"][:, pg, :], wkb[:, 0:TLOC], waits=[ev2])
            ev_delta = sch.ev(q_s)

            sch.phase = "m_scan"
            sbc = dram["s_bc"]
            NH = NS // 2
            with nc.sbuf_tensor("bcc%d" % which, [128, 2 * NS, TCH], BF16) as bcc:
                for itc in range(NTC):
                    nc0 = itc * TCH
                    evbc = dma("gpsimd", q_g, bcc[:],
                               bass.AP(sbc.tensor, sbc.offset + nc0,
                                       [[0, 128], [TLOC, 2 * NS], [1, TCH]]),
                               waits=[ev_bc, ev_delta])
                    for pg in range(PGI):
                        po = (pg % 2) * TCH
                        evd = dma("sync", q_ld, wk1[:, po:po + TCH],
                                  dram["s_delta"][:, pg, nc0:nc0 + TCH], waits=[ev_delta])
                        evx = dma("sync", q_ld, wkb2[:, po:po + TCH],
                                  dram["s_xm"][:, pg, nc0:nc0 + TCH], waits=[ev_xm])
                        evwm = ondv(lambda e, po=po: e.tensor_tensor(
                            out=wkb[:, po:po + TCH], in0=wk1[:, po:po + TCH],
                            in1=wkb2[:, po:po + TCH], op=OP.mult),
                            waits=[evd, evx])
                        py = p1 if pg % 2 == 0 else p2
                        for h in range(2):
                            s0 = h * NH
                            evbb = ondv(lambda e, s0=s0, po=po: e.tensor_tensor(
                                out=sqb[:].rearrange("p a b -> p (a b)"),
                                in0=ap3(wkb, po, [[0, NH], [1, TCH]]),
                                in1=bcc[:, s0:s0 + NH, :], op=OP.mult),
                                waits=[evwm, evbc])
                            for q in range(2):
                                ebuf = wk2 if q == 0 else st_box["t"]
                                def evaq(e, pg=pg, s0=s0, q=q, ebuf=ebuf, po=po):
                                    r = None
                                    for si in range(q * 4, q * 4 + 4):
                                        r = e.activation(
                                            out=ebuf[:, (si % 4) * TCH:(si % 4) * TCH + TCH],
                                            in_=wk1[:, po:po + TCH], func=AF.Exp,
                                            scale=Apk[:, pg, s0 + si:s0 + si + 1])
                                    return r
                                eva = onac(evaq, waits=[evd])
                                def scnq(e, pg=pg, s0=s0, q=q, ebuf=ebuf):
                                    r = None
                                    for si in range(q * 4, q * 4 + 4):
                                        r = e.tensor_tensor_scan(
                                            out=sqb[:, si, :],
                                            data0=ebuf[:, (si % 4) * TCH:(si % 4) * TCH + TCH],
                                            data1=sqb[:, si, :],
                                            initial=states[:, pg, s0 + si:s0 + si + 1],
                                            op0=OP.mult, op1=OP.add)
                                    return r
                                ondv(scnq, waits=[eva, evbb])
                            dvfence()
                            ondv(lambda e, pg=pg, s0=s0: e.tensor_copy(
                                out=states[:, pg, s0:s0 + NH],
                                in_=ap3(sqb, TCH - 1, [[TCH, NH]])))
                            evu = ondv(lambda e, s0=s0: e.tensor_tensor(
                                out=sqb[:].rearrange("p a b -> p (a b)"),
                                in0=sqb[:].rearrange("p a b -> p (a b)"),
                                in1=bcc[:, NS + s0:NS + s0 + NH, :], op=OP.mult))
                            def yredq(e, py=py, h=h):
                                r = None
                                for si in range(NH):
                                    r = e.matmul(out=py[:, 0:TCH], lhsT=k_eye[:],
                                                 rhs=sqb[:, si, :],
                                                 start=(h == 0 and si == 0),
                                                 stop=(h == 1 and si == NH - 1))
                                return r
                            onpe(yredq, waits=[evu])
                        evy = ondv(lambda e, pg=pg, py=py, po=po: e.scalar_tensor_tensor(
                            out=st_box["ri"][:, po:po + TCH], in0=wkb2[:, po:po + TCH],
                            scalar=k_dssm[:, pg, :],
                            in1=py[:, 0:TCH], op0=OP.mult, op1=OP.add), waits=[])
                        dma("scalar", q_s, dram["s_y"][:, pg, nc0:nc0 + TCH],
                            st_box["ri"][:, po:po + TCH], waits=[evy])
            ev_scan = sch.ev(q_s)

            ondv(lambda e: e.tensor_copy(out=hend[:].rearrange("p a b -> p (a b)"),
                                         in_=states[:].rearrange("p a b -> p (a b)")),
                 waits=[sch.ev(e_dv)])
            def pendf(e):
                cb = ap3(cdend, 0, [[1, PGI], [0, NS]])
                return e.tensor_tensor(out=pend[:], in0=cb, in1=Apk[:], op=OP.mult)
            ondv(pendf)
            onac(lambda e: e.activation(out=pend[:].rearrange("p a b -> p (a b)"),
                                        in_=pend[:].rearrange("p a b -> p (a b)"), func=AF.Exp),
                 waits=[sch.ev(e_dv)])
            ev_pend = sch.ev(e_ac)

            ev1 = dma("gpsimd", q_g, dram["cc_ch_i"][0], hend[:], waits=[sch.ev(e_dv)])
            ev2 = dma("gpsimd", q_g, dram["cc_ch_i"][1], pend[:], waits=[ev_pend])
            evc = ongp(lambda e: e.collective_compute(
                "AllGather", OP.bypass, replica_groups=[list(range(NCORE))],
                ins=[dram["cc_ch_i"].opt()], outs=[dram["cc_ch_o"].opt()]), waits=[ev1, ev2])
            ondv(lambda e: e.memset(hin[:].rearrange("p a b -> p (a b)"), 0.0))
            for j in range(NCORE - 1):
                evl1 = dma("sync", q_ld, wk1[:, 0:PGI * NS],
                           dram["cc_ch_o"][j, 0].rearrange("p a b -> p (a b)"), waits=[evc])
                evl2 = dma("sync", q_ld, wk2[:, 0:PGI * NS],
                           dram["cc_ch_o"][j, 1].rearrange("p a b -> p (a b)"), waits=[evc])
                def chj(e, j=j):
                    e.tensor_scalar(out=wk2[:, 0:PGI * NS], in0=wk2[:, 0:PGI * NS],
                                    scalar1=k_chm[:, j:j + 1], scalar2=None, op0=OP.mult)
                    e.tensor_scalar(out=wk2[:, 0:PGI * NS], in0=wk2[:, 0:PGI * NS],
                                    scalar1=k_chi[:, j:j + 1], scalar2=None, op0=OP.add)
                    e.tensor_scalar(out=wk1[:, 0:PGI * NS], in0=wk1[:, 0:PGI * NS],
                                    scalar1=k_chm[:, j:j + 1], scalar2=None, op0=OP.mult)
                    e.tensor_tensor(out=hin[:].rearrange("p a b -> p (a b)"),
                                    in0=hin[:].rearrange("p a b -> p (a b)"),
                                    in1=wk2[:, 0:PGI * NS], op=OP.mult)
                    return e.tensor_add(out=hin[:].rearrange("p a b -> p (a b)"),
                                        in0=hin[:].rearrange("p a b -> p (a b)"),
                                        in1=wk1[:, 0:PGI * NS])
                ondv(chj, waits=[evl1, evl2])
            tap("hend%d" % which, hend[:, :, :], [sch.ev(e_dv), sch.ev(e_gp)])
            tap("pend%d" % which, pend[:, :, :], [sch.ev(e_ac), sch.ev(e_gp)])
            tap("cco%d" % which, dram["cc_ch_o"], [sch.ev(q_ld)])
            def lns(e):
                e.activation(out=lnh[:].rearrange("p a b -> p (a b)"),
                             in_=hin[:].rearrange("p a b -> p (a b)"), func=AF.Abs)
                return e.activation(out=sgh[:].rearrange("p a b -> p (a b)"),
                                    in_=hin[:].rearrange("p a b -> p (a b)"), func=AF.Sign)
            onac(lns, waits=[sch.ev(e_dv)])
            ondv(lambda e: e.tensor_scalar_max(out=lnh[:].rearrange("p a b -> p (a b)"),
                                               in0=lnh[:].rearrange("p a b -> p (a b)"),
                                               scalar1=1e-38), waits=[sch.ev(e_ac)])
            onac(lambda e: e.activation(out=lnh[:].rearrange("p a b -> p (a b)"),
                                        in_=lnh[:].rearrange("p a b -> p (a b)"), func=AF.Ln),
                 waits=[sch.ev(e_dv)])
            ev_lnh = sch.ev(e_ac)
            tap("hin%d" % which, hin[:, :, :], [sch.ev(e_dv)])

            sch.phase = "m_res"
            with nc.sbuf_tensor("rwsl%d" % which, [128, 8, 128], BF16) as rwsl:
                for mg in range(PGI):
                    evw = dma("sync", q_ld, rwsl[:],
                              dram["b_winT"][:, DI + mg * 128:DI + (mg + 1) * 128]
                              .rearrange("(a p) c -> p a c", p=128),
                              waits=[sch.ev(e_pe)] if mg else barrier())
                    for inch, nc0 in enumerate(range(0, TLOC, TCH)):
                        pb = p1 if inch % 2 == 0 else p2
                        def resq(e, pb=pb, nc0=nc0):
                            r = None
                            for kg in range(PGM):
                                r = e.matmul(out=pb[:, 0:TCH], lhsT=rwsl[:, kg, :],
                                             rhs=arh[:, kg, HALO + nc0:HALO + nc0 + TCH],
                                             start=(kg == 0), stop=(kg == PGM - 1))
                            return r
                        last = onpe(resq, waits=[evw])
                        eva = onac(lambda e, mg=mg, pb=pb: e.activation(
                            out=wk1[:, 0:TCH], in_=pb[:, 0:TCH], func=AF.Silu,
                            bias=k_inb[:, PGI + mg, :]), waits=[last])
                        dma("scalar", q_s, dram["s_res"][:, mg, nc0:nc0 + TCH], wk1[:, 0:TCH],
                            waits=[eva])
            ev_res = sch.ev(q_s)

            sch.phase = "m_fix"
            with nc.sbuf_tensor("ccb%d" % which, [128, NS, TCH], BF16) as ccb:
                for itc in range(NTC):
                    nc0 = itc * TCH
                    evcc = dma("gpsimd", q_g, ccb[:],
                               bass.AP(sbc.tensor, sbc.offset + NS * TLOC + nc0,
                                       [[0, 128], [TLOC, NS], [1, TCH]]),
                               waits=[ev_bc, ev_scan])
                    for pg in range(PGI):
                        po = (pg % 2) * TCH
                        evy = dma("sync", q_ld, wk1[:, po:po + TCH],
                                  dram["s_y"][:, pg, nc0:nc0 + TCH], waits=[ev_scan])
                        evcd = dma("sync", q_ld, wkb2[:, po:po + TCH],
                                   dram["s_cdl"][:, pg, nc0:nc0 + TCH], waits=[ev_scan])
                        evrs = dma("sync", q_ld, wk2[:, po:po + TCH],
                                   dram["s_res"][:, pg, nc0:nc0 + TCH], waits=[ev_res])
                        py = p3 if pg % 2 == 0 else p4
                        qv2 = ap3(arh, 0, [[NT, NH], [1, TCH]])
                        for h2 in range(2):
                            s0 = h2 * NH
                            dgo = h2 * 1024
                            evdg = ondv(lambda e, pg=pg, s0=s0, dgo=dgo: e.tensor_tensor(
                                out=ap3(wkb, dgo, [[128, NH], [1, 128]]),
                                in0=ap3(k_eye, 0, [[0, NH], [1, 128]]),
                                in1=ap3(sgh, pg * NS + s0, [[1, NH], [0, 128]]),
                                op=OP.mult), waits=[ev_lnh] if h2 == 0 else [])
                            def qexpq(e, pg=pg, s0=s0, h2=h2, po=po):
                                r = None
                                for si in range(NH):
                                    r = e.activation(
                                        out=sqb[:, si, :] if h2 == 0 else arh[:, si, 0:TCH],
                                        in_=wkb2[:, po:po + TCH], func=AF.Exp,
                                        scale=Apk[:, pg, s0 + si:s0 + si + 1],
                                        bias=lnh[:, pg, s0 + si:s0 + si + 1])
                                return r
                            onac(qexpq, waits=[evcd, ev_lnh] if h2 == 0 else [])
                            evu2 = ondv(lambda e, s0=s0, h2=h2, qv2=qv2: e.tensor_tensor(
                                out=sqb[:].rearrange("p a b -> p (a b)") if h2 == 0 else qv2,
                                in0=sqb[:].rearrange("p a b -> p (a b)") if h2 == 0 else qv2,
                                in1=ccb[:, s0:s0 + NH, :], op=OP.mult),
                                waits=[evcc])
                            def fredq(e, py=py, h2=h2, dgo=dgo):
                                r = None
                                for si in range(NH):
                                    r = e.matmul(out=py[:, 0:TCH],
                                                 lhsT=wkb[:, dgo + si * 128:dgo + (si + 1) * 128],
                                                 rhs=sqb[:, si, :] if h2 == 0
                                                 else arh[:, si, 0:TCH],
                                                 start=(h2 == 0 and si == 0),
                                                 stop=(h2 == 1 and si == NH - 1))
                                return r
                            onpe(fredq, waits=[evu2, evdg])
                        def gat(e, py=py, po=po):
                            e.tensor_add(out=wk1[:, po:po + TCH], in0=wk1[:, po:po + TCH],
                                         in1=py[:, 0:TCH])
                            return e.tensor_tensor(out=wkb2[:, 1024 + po:1024 + po + TCH],
                                                   in0=wk1[:, po:po + TCH],
                                                   in1=wk2[:, po:po + TCH], op=OP.mult)
                        evg = ondv(gat, waits=[evy, evrs])
                        dma("scalar", q_s, dram["s_g"][:, pg, nc0:nc0 + TCH],
                            wkb2[:, 1024 + po:1024 + po + TCH], waits=[evg])
            ev_g = sch.ev(q_s)
            tap("gated%d" % which, wkb[:, TCH:2 * TCH], [ev_g])

            sch.phase = "m_outproj"
            def obank(mb):
                return pbig[:, mb, 0:TCH] if mb < 4 else [p1, p2, p3, p4][mb - 4][:, 0:TCH]
            for inch, nc0 in enumerate(range(0, TLOC, TCH)):
                for kg in range(PGI):
                    evwo = dma("sync", q_ld, wkb2[:, (kg % 2) * 1024:(kg % 2) * 1024 + 1024],
                               dram["b_woutT"][kg * 128:(kg + 1) * 128, :],
                               waits=[ev_g] if inch == 0 and kg == 0 else [])
                    evgx = dma("sync", q_ld, sqb[:, kg % 8, 0:TCH],
                               dram["s_g"][:, kg, nc0:nc0 + TCH],
                               waits=[ev_g] if inch == 0 and kg == 0 else [])
                    def opq(e, kg=kg):
                        r = None
                        for mb in range(PGM):
                            r = e.matmul(
                                out=obank(mb),
                                lhsT=wkb2[:, (kg % 2) * 1024 + mb * 128:
                                          (kg % 2) * 1024 + mb * 128 + 128],
                                rhs=sqb[:, kg % 8, 0:TCH], start=(kg == 0),
                                stop=(kg == PGI - 1))
                        return r
                    onpe(opq, waits=[evwo, evgx])
                for mb in range(PGM):
                    evrl = None
                    if which == 1:
                        evrl = dma("sync", q_ld, wk1[:, (mb % 2) * TCH:(mb % 2) * TCH + TCH],
                                   dram["s_s1"][:, mb, HALO + nc0:HALO + nc0 + TCH],
                                   waits=[ev_s1sp])
                    def evo(e, mb=mb, nc0=nc0, which=which):
                        if which == 1:
                            return e.scalar_tensor_tensor(
                                out=sA[:, mb, HALO + nc0:HALO + nc0 + TCH], in0=obank(mb),
                                scalar=k_outb[:, mb, :],
                                in1=wk1[:, (mb % 2) * TCH:(mb % 2) * TCH + TCH],
                                op0=OP.add, op1=OP.add)
                        return e.tensor_scalar(out=sA[:, mb, HALO + nc0:HALO + nc0 + TCH],
                                               in0=obank(mb),
                                               scalar1=k_outb[:, mb, :], scalar2=None,
                                               op0=OP.add)
                    ondv(evo, waits=[evrl])
            return sch.ev(e_dv)

        with nc.sbuf_tensor("arh1", [128, PGM, NT], BF16) as arh1, \
             nc.sbuf_tensor("wsl1", [128, 8, 128], BF16) as wsl1:
            ev_m1 = emit_mamba(1, arh1, wsl1)
        for nm, buf in [("sxm", "s_xm"), ("sdelta", "s_delta"), ("scdl", "s_cdl"),
                        ("sy", "s_y"), ("sres", "s_res"), ("sg", "s_g")]:
            tap(nm, dram[buf], [ev_m1] + barrier())
        tap("x1", sA[:, :, :], [ev_m1])
        if SN < 4:
            sch.enabled = False

        sch.phase = "x1halo"
        ev1 = dma("gpsimd", q_g, dram["cc_h1_i"], sA[:, :, NT - HALO:NT], waits=[ev_m1], slow=True)
        evc = ongp(lambda e: e.collective_compute(
            "AllGather", OP.bypass, replica_groups=[list(range(NCORE))],
            ins=[dram["cc_h1_i"].opt()], outs=[dram["cc_h1_o"].opt()]), waits=[ev1])
        ondv(lambda e: e.memset(sA[:, :, 0:HALO], 0.0), waits=[ev_m1])
        dvfence()
        for j in range(NCORE):
            evl = dma("sync", q_ld, wk1[:, 0:PGM * HALO],
                      dram["cc_h1_o"][j].rearrange("p a b -> p (a b)"), waits=[evc])
            def hj(e, j=j):
                return e.scalar_tensor_tensor(
                    out=sA[:, :, 0:HALO],
                    in0=wk1[:, 0:PGM * HALO].rearrange("p (a b) -> p a b", a=PGM),
                    scalar=k_psel[:, j:j + 1], in1=sA[:, :, 0:HALO],
                    op0=OP.mult, op1=OP.add)
            ondv(hj, waits=[evl])
            dvfence()
        for pg in range(PGM):
            dma("scalar", q_s, dram["s_s1"][:, pg, :], sA[:, pg, :], waits=[])
        ev_s1sp = sch.ev(q_s)

        with nc.sbuf_tensor("arh2", [128, PGM, NT], BF16) as arh2, \
             nc.sbuf_tensor("wsl2", [128, 8, 128], BF16) as wsl2:
            ev_m2 = emit_mamba(2, arh2, wsl2)
        tap("out2", sA[:, :, HALO:NT], [ev_m2])
        if SN < 5:
            sch.enabled = False
        for pg in range(PGM):
            dma("scalar", q_s, dram["s_o2"][:, pg, :], sA[:, pg, HALO:NT], waits=[ev_m2])
        ev_o2sp = sch.ev(q_s)
        for c in reversed(stA_cm):
            c.__exit__(None, None, None)

        with nc.sbuf_tensor("Vq", [128, PGM, DM], F32) as V, \
             nc.sbuf_tensor("Vbf", [128, PGM, DM], BF16) as Vbf, \
             nc.sbuf_tensor("Vlo", [128, PGM, DM], BF16) as Vlo, \
             nc.sbuf_tensor("o2hw", [128, PGM, 264], BF16) as o2hw, \
             nc.sbuf_tensor("o2lw", [128, PGM, 264], BF16) as o2lw, \
             nc.sbuf_tensor("ginf", [128, SC], F32) as ginf:
            rowd = dram["s_row"]
            ev_gin = dma("gpsimd", q_g, ginf[:],
                         bass.AP(rowd.tensor, rowd.offset + TLOC, [[0, 128], [1, SC]]),
                         waits=ev_grow)

            ev1 = dma("gpsimd", q_g, dram["cc_wk_i"], sA[:, :, NT - 1:NT], waits=[ev_m2], slow=True)
            evc = ongp(lambda e: e.collective_compute(
                "AllGather", OP.bypass, replica_groups=[list(range(NCORE))],
                ins=[dram["cc_wk_i"].opt()], outs=[dram["cc_wk_o"].opt()]), waits=[ev1])
            evz = ondv(lambda e: e.memset(wk2[:, 0:PGM], 0.0))
            evz = dvfence()
            for j in range(NCORE):
                evl = dma("sync", q_ld, wk1[:, 0:PGM],
                          dram["cc_wk_o"][j].rearrange("p a b -> p (a b)"), waits=[evc])
                def wkj(e, j=j):
                    return e.scalar_tensor_tensor(out=wk2[:, 0:PGM], in0=wk1[:, 0:PGM],
                                                  scalar=k_psel[:, j:j + 1], in1=wk2[:, 0:PGM],
                                                  op0=OP.mult, op1=OP.add)
                ondv(wkj, waits=[evl, evz])
                dvfence()
            ondv(lambda e: e.tensor_copy(out=sA[:, :, HALO - 1:HALO],
                                         in_=wk2[:, 0:PGM].rearrange("p (a b) -> p a b", b=1)))
            ev_wkh2 = sch.ev(e_dv)

            ondv(lambda e: e.memset(V[:].rearrange("p a b -> p (a b)"), 0.0))
            ev_v0 = sch.ev(e_dv)

            atk_cm = nc.sbuf_tensor("atk", [128, 8, DM], BF16)
            atk = atk_cm.__enter__()
            w2v = ap3(wk2, 0, [[SC, PGM], [1, SC]])
            w1v = ap3(wk1, 0, [[SC, PGM], [1, SC]])
            for c in range(NSC):
                t0 = HALO + c * SC
                sch.phase = "att_win"
                def wsp1(e, t0=t0):
                    e.tensor_copy(out=o2hw[:, :, 1:SC + 1], in_=sA[:, :, t0:t0 + SC])
                    return e.tensor_copy(out=w2v, in_=sA[:, :, t0:t0 + SC])
                ondv(wsp1, waits=[ev_wkh2] if c == 0 else [])
                def wsp2(e, t0=t0):
                    e.tensor_copy(out=w1v, in_=o2hw[:, :, 1:SC + 1])
                    return e.tensor_copy(out=o2hw[:, :, 0:1], in_=sA[:, :, t0 - 1:t0])
                ondv(wsp2)
                dvfence()
                def wsp3(e, t0=t0):
                    e.tensor_sub(out=o2lw[:, :, 1:SC + 1], in0=w2v, in1=w1v)
                    return e.tensor_copy(out=ap3(k_tmp8, 0, [[1, 8], [1, 1]]),
                                         in_=o2hw[:, :, 0:1])
                ondv(wsp3)
                dvfence()
                ondv(lambda e, t0=t0: e.tensor_sub(out=o2lw[:, :, 0:1],
                                                   in0=sA[:, :, t0 - 1:t0],
                                                   in1=ap3(k_tmp8, 0, [[1, 8], [1, 1]])))
                ev_w = sch.ev(e_dv)
                sch.phase = "att_v"
                for kg in range(PGM):
                    evww = dma("sync", q_ld, wkb2[:, 0:1024],
                               dram["b_wwT"][kg * 128:(kg + 1) * 128, :],
                               waits=[ev_wprep, ev_w] if c == 0 and kg == 0 else [])
                    evwl = dma("sync", q_ld, wkb2[:, 1024:2048],
                               dram["b_wwTlo"][kg * 128:(kg + 1) * 128, :])
                    def vq(e, kg=kg):
                        r = None
                        for tg in range(2):
                            for nh in range(2):
                                for tm, (lh, ro) in enumerate([(o2hw, 0), (o2hw, 1024),
                                                               (o2lw, 0)]):
                                    r = e.matmul(
                                        out=pbig[:, 2 * tg + nh, 0:512],
                                        lhsT=lh[:, kg, 1 + tg * 128:1 + tg * 128 + 128],
                                        rhs=wkb2[:, ro + nh * 512:ro + nh * 512 + 512],
                                        start=(kg == 0 and tm == 0),
                                        stop=(kg == PGM - 1 and tm == 2))
                        return r
                    onpe(vq, waits=[evww, evwl])
                for tg in range(2):
                    def vev(e, tg=tg):
                        e.tensor_scalar(out=wk2[:, 0:1024],
                                        in0=ap3(pbig, (2 * tg) * 512, [[1, 1024]]),
                                        scalar1=k_gw[:, tg, :], scalar2=None, op0=OP.mult)
                        e.tensor_copy(out=atk[:, 4 + tg, :], in_=wk2[:, 0:1024])
                        e.tensor_copy(out=wk1[:, 0:1024], in_=atk[:, 4 + tg, :])
                        return e.tensor_sub(out=atk[:, 6 + tg, :], in0=wk2[:, 0:1024],
                                            in1=wk1[:, 0:1024])
                    ondv(vev, waits=[sch.ev(e_pe)])
                ev_vt = sch.ev(e_dv)
                sch.phase = "att_wkt"
                for tg in range(2):
                    for dq in range(2):
                        def trq(e, tg=tg, dq=dq, t0=t0):
                            r = None
                            for di in range(4):
                                dg = dq * 4 + di
                                r = e.transpose(
                                    out=p1[:, di * 128:(di + 1) * 128],
                                    in_=sA[:, dg, t0 - 1 + tg * 128:t0 - 1 + tg * 128 + 128],
                                    identity=k_eyef[:])
                            return r
                        evt = onpe(trq, waits=[ev_vt] if dq == 0 else [])
                        ondv(lambda e, dq=dq: e.tensor_copy(
                            out=wk2[:, dq * 512:(dq + 1) * 512], in_=p1[0:128, 0:512]),
                            waits=[evt])
                    def wksp(e, tg=tg):
                        e.tensor_copy(out=atk[:, 0 + tg, :], in_=wk2[:, 0:1024])
                        e.tensor_copy(out=wk1[:, 0:1024], in_=atk[:, 0 + tg, :])
                        return e.tensor_sub(out=atk[:, 2 + tg, :], in0=wk2[:, 0:1024],
                                            in1=wk1[:, 0:1024])
                    ondv(wksp)
                ev_wkt = sch.ev(e_dv)
                sch.phase = "att_rks"
                def rksf(e, t0=t0):
                    e.tensor_tensor(out=w2v, in0=sA[:, :, t0:t0 + SC],
                                    in1=ap3(ginf, 0, [[0, PGM], [1, SC]]), op=OP.mult)
                    e.tensor_copy(out=sqb[:, :, 0:SC], in_=w2v)
                    e.tensor_copy(out=w1v, in_=sqb[:, :, 0:SC])
                    return e.tensor_sub(out=sqb[:, :, SC:2 * SC], in0=w2v, in1=w1v)
                evrk = ondv(rksf, waits=[ev_gin])
                sch.phase = "att_vsp"
                def vsp(e):
                    for eb in range(PGM):
                        e.tensor_copy(out=Vbf[:, eb, :], in_=V[:, eb, :])
                        e.tensor_copy(out=wk1[:, 1024:2048], in_=Vbf[:, eb, :])
                        e.tensor_sub(out=Vlo[:, eb, :], in0=V[:, eb, :],
                                     in1=wk1[:, 1024:2048])
                    return e.tensor_copy(out=k_fence[:, 2:3], in_=k_fence[:, 3:4])
                ongp(vsp, waits=[ev_v0] if c == 0 else [])
                evvb = sch.ev(e_gp)
                sch.phase = "att_s"
                for ub in range(2):
                    pS = p3 if ub == 0 else p4
                    def sq_(e, ub=ub, pS=pS):
                        r = None
                        for tm, (lh, rh) in enumerate([(o2hw, o2hw), (o2hw, o2lw),
                                                       (o2lw, o2hw)]):
                            for kg in range(PGM):
                                r = e.matmul(
                                    out=pS[:, 0:SC],
                                    lhsT=lh[:, kg, ub * 128:ub * 128 + 128],
                                    rhs=rh[:, kg, 1:1 + SC],
                                    start=(tm == 0 and kg == 0),
                                    stop=(tm == 2 and kg == PGM - 1))
                        return r
                    onpe(sq_, waits=[ev_w])
                    ondv(lambda e, ub=ub, pS=pS: e.scalar_tensor_tensor(
                        out=wk2[:, ub * SC:(ub + 1) * SC], in0=pS[:, 0:SC],
                        scalar=k_gwi[:, ub, :], in1=k_MT[:, ub, :], op0=OP.mult, op1=OP.mult),
                        waits=[sch.ev(e_pe)])
                def smsp(e):
                    e.tensor_copy(out=wkb[:, 0:2 * SC], in_=wk2[:, 0:2 * SC])
                    e.tensor_copy(out=wk1[:, 0:2 * SC], in_=wkb[:, 0:2 * SC])
                    return e.tensor_sub(out=wkb[:, 2 * SC:4 * SC], in0=wk2[:, 0:2 * SC],
                                        in1=wk1[:, 0:2 * SC])
                ondv(smsp)
                ev_sm = sch.ev(e_dv)
                sch.phase = "att_reads"
                for mb in range(PGM):
                    def rdq(e, mb=mb):
                        first = True
                        r = None
                        for kg in range(PGM):
                            for lhs, r0 in [(Vbf, 0), (Vbf, SC), (Vlo, 0)]:
                                r = e.matmul(
                                    out=ap3(pbig, mb * SC, [[1, SC]]),
                                    lhsT=lhs[:, kg, mb * 128:(mb + 1) * 128],
                                    rhs=sqb[:, kg, r0:r0 + SC], start=first, stop=False)
                                first = False
                        for ub in range(2):
                            for tm, (ti, wo) in enumerate([(4, 0), (4, 2 * SC), (6, 0)]):
                                r = e.matmul(
                                    out=ap3(pbig, mb * SC, [[1, SC]]),
                                    lhsT=atk[:, ti + ub, mb * 128:(mb + 1) * 128],
                                    rhs=wkb[:, wo + ub * SC:wo + ub * SC + SC],
                                    start=False, stop=(ub == 1 and tm == 2))
                        return r
                    onpe(rdq, waits=[evvb, evrk, ev_vt, ev_sm])
                evrd = ondv(lambda e: e.tensor_copy(
                    out=wk1[:, 0:PGM * SC],
                    in_=ap3(pbig, 0, [[1, PGM * SC]])), waits=[sch.ev(e_pe)])
                dma("scalar", q_s, dram["s_reads"][:, :, c * SC:(c + 1) * SC],
                    w1v, waits=[evrd])
                sch.phase = "att_vup"
                for eb in range(PGM):
                    def vupq(e, eb=eb):
                        r = None
                        for nh0 in range(2):
                            for ukg in range(2):
                                for tm, (li, ri) in enumerate([(0, 4), (0, 6), (2, 4)]):
                                    r = e.matmul(
                                        out=pbig[:, 2 * (eb % 2) + nh0, 0:512],
                                        lhsT=atk[:, li + ukg, eb * 128:(eb + 1) * 128],
                                        rhs=atk[:, ri + ukg, nh0 * 512:(nh0 + 1) * 512],
                                        start=(ukg == 0 and tm == 0),
                                        stop=(ukg == 1 and tm == 2))
                        return r
                    last = onpe(vupq, waits=[ev_wkt, evrd])
                    def vupd(e, eb=eb):
                        r = None
                        for nh0 in range(2):
                            r = e.scalar_tensor_tensor(
                                out=V[:, eb, nh0 * 512:(nh0 + 1) * 512],
                                in0=V[:, eb, nh0 * 512:(nh0 + 1) * 512], scalar=k_g256[:, 0:1],
                                in1=pbig[:, 2 * (eb % 2) + nh0, 0:512], op0=OP.mult, op1=OP.add)
                        return r
                    ondv(vupd, waits=[last])
                if c == 0:
                    tap("V0", V[:, :, :], [sch.ev(e_dv)] + barrier())
                if c in (5, 6):
                    tap("sq%d" % c, sqb[:, :, :], barrier())
                    tap("smw%d" % c, wkb[:, 0:4 * SC], barrier())
                    tap("atk%d" % c, atk[:, :, :], barrier())
                    tap("vb%d" % c, Vbf[:, :, :], barrier())
                    tap("vl%d" % c, Vlo[:, :, :], barrier())
                    tap("whi%d" % c, o2hw[:, :, :], barrier())
                    tap("wlo%d" % c, o2lw[:, :, :], barrier())
            ev_att = sch.ev(e_dv)
            tap("ginf", ginf[:, :], [ev_gin] + barrier())
            tap("srow", dram["s_row"], barrier())
            tap("g256", k_g256[:, :], barrier())
            tap("kgw", k_gw[:, :, :], barrier())
            tap("kmisc", k_misc[:, 0:80], barrier())
            tap("MT", k_MT[:, :, :], barrier())
            tap("sreads", dram["s_reads"], [ev_att] + barrier())
            tap("Vagg", V[:, :, :], [ev_att] + barrier())

            sch.phase = "att_agg"
            evvb = ondv(lambda e: e.tensor_copy(out=Vbf[:].rearrange("p a b -> p (a b)"),
                                                in_=V[:].rearrange("p a b -> p (a b)")),
                        waits=[ev_att])
            ev1 = dma("gpsimd", q_g, dram["cc_ag_i"], Vbf[:], waits=[evvb])
            evc = ongp(lambda e: e.collective_compute(
                "AllGather", OP.bypass, replica_groups=[list(range(NCORE))],
                ins=[dram["cc_ag_i"].opt()], outs=[dram["cc_ag_o"].opt()]), waits=[ev1])
            carry = V
            ondv(lambda e: e.memset(carry[:].rearrange("p a b -> p (a b)"), 0.0), waits=[evvb])
            ondv(lambda e: e.tensor_scalar(out=wk2[:, 64:64 + NCORE - 1],
                                           in0=k_chm[:, 0:NCORE - 1],
                                           scalar1=k_g2048[:, 0:1], scalar2=None, op0=OP.mult))
            dvfence()
            ondv(lambda e: e.tensor_add(out=wk2[:, 64:64 + NCORE - 1],
                                        in0=wk2[:, 64:64 + NCORE - 1],
                                        in1=k_chi[:, 0:NCORE - 1]))
            dvfence()
            for j in range(NCORE - 1):
                for pg in range(PGM):
                    evl = dma("sync", q_ld, wkb[:, 0:1024], dram["cc_ag_o"][j, :, pg, :],
                              waits=[evc])
                    def cj(e, j=j, pg=pg):
                        e.tensor_scalar(out=carry[:, pg, :], in0=carry[:, pg, :],
                                        scalar1=wk2[:, 64 + j:65 + j], scalar2=None, op0=OP.mult)
                        e.tensor_scalar(out=wk1[:, 0:1024], in0=wkb[:, 0:1024],
                                        scalar1=k_chm[:, j:j + 1], scalar2=None, op0=OP.mult)
                        return e.tensor_add(out=carry[:, pg, :], in0=carry[:, pg, :],
                                            in1=wk1[:, 0:1024])
                    ondv(cj, waits=[evl])
            evcar = ondv(lambda e: e.tensor_copy(out=Vbf[:].rearrange("p a b -> p (a b)"),
                                                 in_=carry[:].rearrange("p a b -> p (a b)")))

            sch.phase = "att_corr"
            atk_cm.__exit__(None, None, None)
            CCH = 256
            rdch_cm = nc.sbuf_tensor("rdch", [128, PGM, CCH], F32)
            rdch = rdch_cm.__enter__()
            rdlo_cm = nc.sbuf_tensor("rdlo", [128, PGM, CCH], BF16)
            rdlo = rdlo_cm.__enter__()
            gtauf_cm = nc.sbuf_tensor("gtauf", [128, CCH], F32)
            gtauf = gtauf_cm.__enter__()
            w2c = ap3(wk2, 0, [[CCH, PGM], [1, CCH]])
            for inch, nc0 in enumerate(range(0, TLOC, CCH)):
                ev_gtau = dma("gpsimd", q_g, gtauf[:],
                              bass.AP(rowd.tensor, rowd.offset + nc0, [[0, 128], [1, CCH]]),
                              waits=ev_grow + [sch.ev(e_dv)])
                evsc = ondv(lambda e, nc0=nc0: e.tensor_tensor(
                    out=sqb[:, :, 0:CCH], in0=sA[:, :, HALO + nc0:HALO + nc0 + CCH],
                    in1=ap3(gtauf, 0, [[0, PGM], [1, CCH]]), op=OP.mult),
                    waits=[evcar, ev_gtau])
                evread = dma("sync", q_ld, rdch[:],
                             dram["s_reads"][:, :, nc0:nc0 + CCH],
                             waits=[sch.ev(e_dv), sch.ev(e_pe)])
                for mb in range(PGM):
                    def crq(e, mb=mb):
                        r = None
                        for kg in range(PGM):
                            r = e.matmul(
                                out=pbig[:, mb % 4, 0:CCH],
                                lhsT=Vbf[:, kg, mb * 128:(mb + 1) * 128],
                                rhs=sqb[:, kg, 0:CCH], start=(kg == 0), stop=(kg == PGM - 1))
                        return r
                    last = onpe(crq, waits=[evsc])
                    ondv(lambda e, mb=mb: e.tensor_add(
                        out=rdch[:, mb, :], in0=rdch[:, mb, :], in1=pbig[:, mb % 4, 0:CCH]),
                        waits=[last, evread])
                def rdsp(e):
                    e.tensor_copy(out=sqb[:, :, CCH:2 * CCH],
                                  in_=rdch[:].rearrange("p a b -> p (a b)")
                                  if False else rdch[:, :, :])
                    e.tensor_copy(out=w2c, in_=sqb[:, :, CCH:2 * CCH])
                    return e.tensor_sub(out=rdlo[:, :, :], in0=rdch[:, :, :], in1=w2c)
                ondv(rdsp)
                ev_rt = sch.ev(e_dv)
                def bank(mb):
                    return pbig[:, mb, 0:CCH] if mb < 4 else [p1, p2, p3, p4][mb - 4][:, 0:CCH]
                last = None
                for kg in range(PGM):
                    evwr = dma("sync", q_ld, wkb2[:, 0:1024],
                               dram["b_wrT"][kg * 128:(kg + 1) * 128, :],
                               waits=[ev_rt] if kg == 0 else [])
                    evwrl = dma("sync", q_ld, wkb2[:, 1024:2048],
                                dram["b_wrTlo"][kg * 128:(kg + 1) * 128, :])
                    def wrq(e, kg=kg):
                        r = None
                        for mb in range(PGM):
                            for tm, (wo, rsrc) in enumerate([(0, 0), (0, 1), (1024, 0)]):
                                r = e.matmul(
                                    out=bank(mb), lhsT=wkb2[:, wo + mb * 128:wo + mb * 128 + 128],
                                    rhs=sqb[:, kg, CCH:2 * CCH] if rsrc == 0 else rdlo[:, kg, :],
                                    start=(kg == 0 and tm == 0),
                                    stop=(kg == PGM - 1 and tm == 2))
                        return r
                    last = onpe(wrq, waits=[evwr, evwrl])
                for mb in range(PGM):
                    evx1 = dma("sync", q_ld, wk1[:, 0:CCH],
                               dram["s_s1"][:, mb, HALO + nc0:HALO + nc0 + CCH], waits=[ev_s1sp])
                    evxf = dma("sync", q_ld, wk2[:, 0:CCH], dram["s_xfm"][:, mb, nc0:nc0 + CCH])
                    evo2 = dma("sync", q_ld, wk2[:, CCH:2 * CCH],
                               dram["s_o2"][:, mb, nc0:nc0 + CCH], waits=[ev_o2sp])
                    def fin(e, mb=mb, nc0=nc0):
                        e.scalar_tensor_tensor(out=sA[:, mb, HALO + nc0:HALO + nc0 + CCH],
                                               in0=bank(mb), scalar=ALPHA,
                                               in1=wk2[:, CCH:2 * CCH], op0=OP.mult, op1=OP.add)
                        e.tensor_add(out=sA[:, mb, HALO + nc0:HALO + nc0 + CCH],
                                     in0=sA[:, mb, HALO + nc0:HALO + nc0 + CCH],
                                     in1=wk1[:, 0:CCH])
                        return e.tensor_add(out=sA[:, mb, HALO + nc0:HALO + nc0 + CCH],
                                            in0=sA[:, mb, HALO + nc0:HALO + nc0 + CCH],
                                            in1=wk2[:, 0:CCH])
                    ondv(fin, waits=[last, evx1, evxf, evo2])
            gtauf_cm.__exit__(None, None, None)
            rdlo_cm.__exit__(None, None, None)
            rdch_cm.__exit__(None, None, None)
        ev_xa = sch.ev(e_dv)
        tap("xafter", sA[:, :, HALO:NT], [ev_xa])
        if SN < 6:
            sch.enabled = False

        for pg in range(PGM):
            dma("scalar", q_s, dram["s_s1"][:, pg, :], sA[:, pg, :], waits=[ev_xa])
        ev_xasp = sch.ev(q_s)

        stB_cm = [nc.sbuf_tensor("stB%d" % i, [128, NT], F32) for i in range(3)]
        st_box["mu"], st_box["ri"], st_box["t"] = [c.__enter__() for c in stB_cm]
        with nc.sbuf_tensor("hib", [128, PGM, NT], BF16) as hib, \
             nc.sbuf_tensor("gts2", [72, TLOC], F32) as gts2, \
             nc.sbuf_tensor("gtok", [128, 16, 80], F32) as gtok, \
             nc.sbuf_tensor("gg", [128, 16, 80], F32) as gg, \
             nc.sbuf_tensor("gA", [128, 16, 80], F32) as gA, \
             nc.sbuf_tensor("ohT", [64, TLOC], BF16) as ohT, \
             nc.sbuf_tensor("compb", [2, TLOC], BF16) as compb, \
             nc.sbuf_tensor("cw1b", [128, PGM, 256], BF16) as cw1b:
            ev_ln1 = emit_ln(sA, sA, None, False, NT, cast_bf=hib)
            xf = sA
            hi = hib

            sch.phase = "ffn_scores"
            for inch, nc0 in enumerate(range(0, TLOC, TCH)):
                last1 = last2 = None
                for kg in range(PGM):
                    def lof(e, kg=kg, nc0=nc0):
                        e.tensor_copy(out=wk1[:, 0:TCH], in_=hi[:, kg, HALO + nc0:HALO + nc0 + TCH])
                        e.tensor_sub(out=wk1[:, 0:TCH], in0=xf[:, kg, HALO + nc0:HALO + nc0 + TCH],
                                     in1=wk1[:, 0:TCH])
                        return e.tensor_copy(out=wkb2[:, 0:TCH], in_=wk1[:, 0:TCH])
                    evlo = ondv(lof, waits=[ev_ln1] if inch == 0 and kg == 0 else [])
                    def scq(e, kg=kg, nc0=nc0):
                        e.matmul(out=p1[0:64, 0:TCH], lhsT=k_qT[:, kg, :],
                                 rhs=hi[:, kg, HALO + nc0:HALO + nc0 + TCH],
                                 start=(kg == 0), stop=False)
                        e.matmul(out=p1[0:64, 0:TCH], lhsT=k_qT[:, kg, :], rhs=wkb2[:, 0:TCH],
                                 start=False, stop=(kg == PGM - 1))
                        e.matmul(out=p2[0:8, 0:TCH], lhsT=k_csT[:, kg, :],
                                 rhs=hi[:, kg, HALO + nc0:HALO + nc0 + TCH],
                                 start=(kg == 0), stop=False)
                        return e.matmul(out=p2[0:8, 0:TCH], lhsT=k_csT[:, kg, :],
                                        rhs=wkb2[:, 0:TCH], start=False,
                                        stop=(kg == PGM - 1))
                    last1 = last2 = onpe(scq, waits=[evlo])
                def sev(e, nc0=nc0):
                    e.tensor_copy(out=gts2[0:64, nc0:nc0 + TCH], in_=p1[0:64, 0:TCH])
                    return e.tensor_copy(out=gts2[64:72, nc0:nc0 + TCH], in_=p2[0:8, 0:TCH])
                ondv(sev, waits=[last1, last2])
            ev_sc = sch.ev(e_dv)
            tap("scores", gts2[0:72, 0:TLOC], [ev_sc])

            sch.phase = "ffn_gtok"
            for tg in range(16):
                evt = onpe(lambda e, tg=tg: e.transpose(out=p3[:, 0:128],
                                                        in_=gts2[0:72, tg * 128:(tg + 1) * 128],
                                                        identity=k_eyef[0:72, :]), waits=[ev_sc])
                ondv(lambda e, tg=tg: e.tensor_copy(out=gtok[:, tg, 0:72], in_=p3[0:128, 0:72]),
                     waits=[evt])
            ev_gt = sch.ev(e_dv)

            sch.phase = "ffn_calib"
            ondv(lambda e: e.tensor_scalar(out=gtok[:].rearrange("p a b -> p (a b)"),
                                           in0=gtok[:].rearrange("p a b -> p (a b)"),
                                           scalar1=k_misc[:, 64:65], scalar2=None, op0=OP.mult),
                 waits=[ev_gt])
            onac(lambda e: e.activation(out=gtok[:].rearrange("p a b -> p (a b)"),
                                        in_=gtok[:].rearrange("p a b -> p (a b)"),
                                        func=AF.Sigmoid), waits=[sch.ev(e_dv)])
            ondv(lambda e: e.tensor_scalar_mul(out=gtok[:].rearrange("p a b -> p (a b)"),
                                               in0=gtok[:].rearrange("p a b -> p (a b)"),
                                               scalar1=7.0), waits=[sch.ev(e_ac)])
            def tele(e):
                v = gtok[:].rearrange("p a b -> p (a b)")
                a = gA[:].rearrange("p a b -> p (a b)")
                g = gg[:].rearrange("p a b -> p (a b)")
                e.tensor_scalar(out=a, in0=v, scalar1=0.0, scalar2=k_kA[:, 0:1],
                                op0=OP.mult, op1=OP.add)
                e.tensor_scalar(out=g, in0=v, scalar1=0.0, scalar2=k_kB[:, 0:1],
                                op0=OP.mult, op1=OP.add)
                for k in range(1, 7):
                    e.tensor_single_scalar(out=wk1[:, 0:1280], in_=v, scalar=float(k), op=OP.is_ge)
                    e.scalar_tensor_tensor(out=a, in0=wk1[:, 0:1280], scalar=k_kA[:, k:k + 1],
                                           in1=a, op0=OP.mult, op1=OP.add)
                    e.scalar_tensor_tensor(out=g, in0=wk1[:, 0:1280], scalar=k_kB[:, k:k + 1],
                                           in1=g, op0=OP.mult, op1=OP.add)
                e.tensor_tensor(out=g, in0=g, in1=v, op=OP.mult)
                return e.tensor_add(out=g, in0=g, in1=a)
            ondv(tele, waits=[sch.ev(e_dv)])
            ev_gates = sch.ev(e_dv)
            tap("gates", gg[:, :, 0:72], [ev_gates])

            ondv(lambda e: e.tensor_reduce(out=wk1[:, 0:16], in_=gg[:, :, 64:72], axis=AX.X,
                                           op=OP.max), waits=[ev_gates])
            dvfence()
            def amax(e):
                mb_ = ap3(wk1, 0, [[1, 16], [0, 8]])
                return e.tensor_tensor(out=gA[:, :, 64:72], in0=gg[:, :, 64:72], in1=mb_,
                                       op=OP.is_ge)
            ondv(amax)
            dvfence()
            ondv(lambda e: e.tensor_tensor(out=gA[:, :, 64:72], in0=gA[:, :, 64:72],
                                           in1=ap3(k_iom, 0, [[0, 16], [1, 8]]), op=OP.mult))
            dvfence()
            ondv(lambda e: e.tensor_reduce(out=wk1[:, 16:32], in_=gA[:, :, 64:72], axis=AX.X,
                                           op=OP.min))
            dvfence()
            ondv(lambda e: e.tensor_scalar_add(out=wk1[:, 16:32], in0=wk1[:, 16:32],
                                               scalar1=999.0))
            dvfence()
            ondv(lambda e: e.memset(gA[:, :, 0:8], 0.0))
            for cc in range(8):
                def selcc(e, cc=cc):
                    return e.tensor_single_scalar(out=wk1[:, 32:48], in_=wk1[:, 16:32],
                                                  scalar=float(cc), op=OP.is_equal)
                ondv(selcc)
                dvfence()
                def selm(e, cc=cc):
                    ohb = ap3(wk1, 32, [[1, 16], [0, 8]])
                    return e.tensor_tensor(out=gg[:, :, 72:80], in0=gg[:, :, cc * 8:(cc + 1) * 8],
                                           in1=ohb, op=OP.mult)
                ondv(selm)
                dvfence()
                ondv(lambda e: e.tensor_add(out=gA[:, :, 0:8], in0=gA[:, :, 0:8],
                                            in1=gg[:, :, 72:80]))
                dvfence()
            ondv(lambda e: e.tensor_reduce(out=wk1[:, 48:64], in_=gA[:, :, 0:8], axis=AX.X,
                                           op=OP.max))
            dvfence()
            def selg(e):
                mb_ = ap3(wk1, 48, [[1, 16], [0, 8]])
                return e.tensor_tensor(out=gA[:, :, 8:16], in0=gA[:, :, 0:8], in1=mb_,
                                       op=OP.is_ge)
            ondv(selg)
            dvfence()
            ondv(lambda e: e.tensor_tensor(out=gA[:, :, 8:16], in0=gA[:, :, 8:16],
                                           in1=ap3(k_iom, 0, [[0, 16], [1, 8]]), op=OP.mult))
            dvfence()
            ondv(lambda e: e.tensor_reduce(out=wk1[:, 64:80], in_=gA[:, :, 8:16], axis=AX.X,
                                           op=OP.min))
            dvfence()
            ondv(lambda e: e.tensor_scalar(out=wk1[:, 80:96], in0=wk1[:, 16:32], scalar1=8.0,
                                           scalar2=None, op0=OP.mult))
            dvfence()
            ondv(lambda e: e.tensor_add(out=wk1[:, 80:96], in0=wk1[:, 80:96], in1=wk1[:, 64:80]))
            dvfence()
            ondv(lambda e: e.tensor_scalar_add(out=wk1[:, 80:96], in0=wk1[:, 80:96],
                                               scalar1=999.0))
            dvfence()
            ev_ti = sch.ev(e_dv)

            def ohf(e):
                tib = ap3(wk1, 80, [[1, 16], [0, 64]])
                iob = ap3(k_io64, 0, [[0, 16], [1, 64]])
                return e.tensor_tensor(out=ap3(wk2, 0, [[64, 16], [1, 64]]), in0=iob, in1=tib,
                                       op=OP.is_equal)
            ondv(ohf, waits=[ev_ti])
            ev_oh = sch.ev(e_dv)
            for tg in range(16):
                evt = onpe(lambda e, tg=tg: e.transpose(out=p3[0:64, 0:128],
                                                        in_=ap3(wk2, tg * 64, [[1, 64]]),
                                                        identity=k_eyef[:]), waits=[ev_oh])
                ondv(lambda e, tg=tg: e.tensor_copy(out=ohT[0:64, tg * 128:(tg + 1) * 128],
                                                    in_=p3[0:64, 0:128]), waits=[evt])
            ev_ohT = sch.ev(e_dv)
            tap("ohT", ohT[0:64, 0:TLOC], [ev_ohT])

            sch.phase = "ffn_comp"
            for kg in range(PGM):
                dma("sync", q_ld, cw1b[:, kg, :], dram["b_cw1"][kg * 128:(kg + 1) * 128, :],
                    waits=[ev_wprep])
            ev_cw1 = sch.ev(q_ld)
            ev_cw1b = sch.ev(q_l2)
            ev_cw1c = sch.ev(q_l3)
            ev_cw1d = sch.ev(q_l4)
            for inch, nc0 in enumerate(range(0, TLOC, TCH)):
                def hgq(e, nc0=nc0):
                    r = None
                    for mg in range(2):
                        for kg in range(PGM):
                            r = e.matmul(
                                out=(p1 if mg == 0 else p2)[:, 0:TCH],
                                lhsT=cw1b[:, kg, mg * 128:(mg + 1) * 128],
                                rhs=hi[:, kg, HALO + nc0:HALO + nc0 + TCH],
                                start=(kg == 0), stop=(kg == PGM - 1))
                    return r
                last = onpe(hgq, waits=[ev_cw1, ev_cw1b, ev_cw1c, ev_cw1d, ev_ln1])
                def hga(e, nc0=nc0):
                    e.activation(out=sqb[:, 0, 0:TCH], in_=p1[:, 0:TCH],
                                 func=AF.Gelu, bias=k_cb1[:, 0, :])
                    return e.activation(out=sqb[:, 1, 0:TCH], in_=p2[:, 0:TCH],
                                        func=AF.Gelu, bias=k_cb1[:, 1, :])
                onac(hga, waits=[last])
                last = None
                for kg in range(2):
                    last = onpe(lambda e, kg=kg: e.matmul(
                        out=p3[0:2, 0:TCH], lhsT=k_cw2[:, kg, :], rhs=sqb[:, kg, 0:TCH],
                        start=(kg == 0), stop=(kg == 1)), waits=[sch.ev(e_ac)])
                onac(lambda e, nc0=nc0: e.activation(out=compb[0:2, nc0:nc0 + TCH],
                                                     in_=p3[0:2, 0:TCH], func=AF.Tanh,
                                                     bias=k_cb2[0:2, 0:1]), waits=[last])
                ev_cp = sch.ev(e_ac)
                for mg in range(PGM):
                    evh = onpe(lambda e, mg=mg, nc0=nc0: e.matmul(
                        out=p4[:, 0:TCH], lhsT=k_w1a[0:2, mg * 128:(mg + 1) * 128],
                        rhs=compb[0:2, nc0:nc0 + TCH], start=True, stop=True), waits=[ev_cp])
                    evr = onac(lambda e, mg=mg: e.activation(out=wk1[:, 0:TCH], in_=p4[:, 0:TCH],
                                                             func=AF.Relu, bias=k_b1f[:, mg, :]),
                               waits=[evh])
                    evw2 = ondv(lambda e, mg=mg: e.tensor_scalar(
                        out=wkb2[:, 0:TCH], in0=wk1[:, 0:TCH], scalar1=k_w2f[:, mg, :],
                        scalar2=None, op0=OP.mult), waits=[evr])
                    onpe(lambda e, mg=mg: e.matmul(out=p3[0:64, 0:TCH], lhsT=k_blk8[:, mg, :],
                                                   rhs=wkb2[:, 0:TCH], start=(mg == 0),
                                                   stop=(mg == PGM - 1)), waits=[evw2])
                def magf(e, nc0=nc0):
                    e.tensor_scalar(out=wk1[0:64, 0:TCH], in0=p3[0:64, 0:TCH],
                                    scalar1=k_b2p[0:64, 0:1], scalar2=None, op0=OP.add)
                    return e.tensor_tensor(out=wkb[0:64, 0:TCH], in0=wk1[0:64, 0:TCH],
                                           in1=ohT[0:64, nc0:nc0 + TCH], op=OP.mult)
                evm = ondv(magf, waits=[sch.ev(e_pe), ev_ohT])
                evms = onpe(lambda e: e.matmul(out=p4[:, 0:TCH], lhsT=k_ones[0:64, :],
                                               rhs=wkb[0:64, 0:TCH], start=True, stop=True),
                            waits=[evm])
                evmsel = ondv(lambda e, nc0=nc0: e.tensor_copy(out=st_box["ri"][:, nc0:nc0 + TCH],
                                                                in_=p4[:, 0:TCH]), waits=[evms])
                for mq in range(2):
                    def dsq(e, mq=mq, nc0=nc0):
                        r = None
                        for mi in range(4):
                            mb = mq * 4 + mi
                            r = e.matmul(out=pbig[:, mi, 0:TCH],
                                         lhsT=k_dirs[0:64, mb * 128:(mb + 1) * 128],
                                         rhs=ohT[0:64, nc0:nc0 + TCH], start=True, stop=True)
                        return r
                    last = onpe(dsq, waits=[ev_ohT])
                    def ffin(e, mq=mq, nc0=nc0):
                        r = None
                        for mi in range(4):
                            mb = mq * 4 + mi
                            e.tensor_scalar(out=wk1[:, 0:TCH], in0=pbig[:, mi, 0:TCH],
                                            scalar1=k_osc[:, 0:1], scalar2=None, op0=OP.mult)
                            e.tensor_mul(out=wk1[:, 0:TCH], in0=wk1[:, 0:TCH],
                                         in1=st_box["ri"][:, nc0:nc0 + TCH])
                            r = e.tensor_add(out=xf[:, mb, HALO + nc0:HALO + nc0 + TCH],
                                             in0=xf[:, mb, HALO + nc0:HALO + nc0 + TCH],
                                             in1=wk1[:, 0:TCH])
                        return r
                    ondv(ffin, waits=[last, evmsel])
            ev_y = sch.ev(e_dv)
            tap("ffny", sA[:, :, HALO:NT], [ev_y])

            sch.phase = "ffn_final"
            ev_lnf = emit_ln(sA, sA, None, False, NT, gb=k_gam, bb=k_bet)
            for inch, nc0 in enumerate(range(0, TLOC, TCH)):
                for mb in range(PGM):
                    evr = dma("sync", q_ld, wk1[:, 0:TCH],
                              dram["s_s1"][:, mb, HALO + nc0:HALO + nc0 + TCH], waits=[ev_xasp])
                    ondv(lambda e, mb=mb, nc0=nc0: e.tensor_add(
                        out=sA[:, mb, HALO + nc0:HALO + nc0 + TCH],
                        in0=sA[:, mb, HALO + nc0:HALO + nc0 + TCH], in1=wk1[:, 0:TCH]),
                        waits=[evr, ev_lnf])
        for c in reversed(stB_cm):
            c.__exit__(None, None, None)
        ev_fin = sch.ev(e_dv)

        sch.phase = "out_store"
        sch.enabled = True
        bar_out = barrier()
        nst = 0
        for tg in range(16):
            evv = None
            xo = (tg % 2) * DM
            for dq in range(2):
                def otrq(e, tg=tg, dq=dq):
                    r = None
                    for di in range(4):
                        dg = dq * 4 + di
                        r = e.transpose(
                            out=p1[:, di * 128:(di + 1) * 128],
                            in_=sA[:, dg, HALO + tg * 128:HALO + (tg + 1) * 128],
                            identity=k_eyef[:])
                    return r
                evt = onpe(otrq, waits=bar_out if tg == 0 and dq == 0 else [])
                evv = ondv(lambda e, dq=dq, xo=xo: e.tensor_copy(
                    out=wk2[:, xo + dq * 512:xo + (dq + 1) * 512],
                    in_=p1[0:128, 0:512]), waits=[evt])
            dma("gpsimd", q_st, dram["out"][tg * 128:(tg + 1) * 128, :],
                wk2[:, xo:xo + DM], waits=[evv])
            nst += 16
        sch.emit("gpsimd", lambda e: e.engine_nop(), waits=[(q_st, nst)])

        sch.build(blockctx)
    return nc, dram


def _pack_inputs(inputs):
    from ml_dtypes import bfloat16
    x = np.asarray(inputs["L0_attn_x"], np.float32).reshape(NCORE * TLOC, DM)
    eye = np.eye(128, dtype=np.float32)
    packP = lambda a, pgs: np.ascontiguousarray(
        np.asarray(a, np.float32).reshape(pgs, 128, -1).transpose(1, 0, 2))
    rep = lambda v: np.full((128, 1), np.float32(v), np.float32)
    blk8 = np.zeros((128, 8, 64), np.float32)
    for p in range(128):
        for pg in range(8):
            blk8[p, pg, (pg * 128 + p) // 16] = 1.0
    base = {
        "w_inT": np.ascontiguousarray(np.asarray(inputs["L0_attn_in_proj_w"], np.float32).T),
        "w_outT": np.ascontiguousarray(np.asarray(inputs["L0_attn_out_proj_w"], np.float32).T),
        "xp_wT": np.ascontiguousarray(np.asarray(inputs["L0_attn_x_proj_w"], np.float32).T),
        "dt_wT": np.ascontiguousarray(np.asarray(inputs["L0_attn_dt_proj_w"], np.float32).T),
        "ww_T": np.ascontiguousarray(np.asarray(inputs["L0_attn_proj_write_w"], np.float32).T),
        "wr_T": np.ascontiguousarray(np.asarray(inputs["L0_attn_proj_read_w"], np.float32).T),
        "in_b": packP(inputs["L0_attn_in_proj_b"], 32),
        "out_b": packP(inputs["L0_attn_out_proj_b"], 8),
        "conv_w": packP(np.asarray(inputs["L0_attn_conv_w"], np.float32).reshape(DI, 4), 16),
        "conv_b": packP(inputs["L0_attn_conv_b"], 16),
        "dt_b": packP(inputs["L0_attn_dt_proj_b"], 16),
        "A_log": packP(inputs["L0_attn_A_log"], 16),
        "D_ssm": packP(inputs["L0_attn_D_ssm"], 16),
        "n1w": packP(inputs["L0_attn_norm1_w"], 8),
        "n2w": packP(inputs["L0_attn_norm2_w"], 8),
        "decay_r": rep(np.asarray(inputs["L0_attn_decay"], np.float32)),
        "sig_T": np.ascontiguousarray(np.asarray(inputs["L0_ffn_signatures_raw"], np.float32).T),
        "knots_r": np.ascontiguousarray(np.broadcast_to(
            np.asarray(inputs["L0_ffn_knot_values"], np.float32), (128, 8))),
        "temp_r": rep(np.asarray(inputs["L0_ffn_temperature"], np.float32)[0]),
        "cw1": np.asarray(inputs["L0_ffn_compress_w1"], np.float32),
        "cb1": packP(inputs["L0_ffn_compress_b1"], 2),
        "cw2": np.asarray(inputs["L0_ffn_compress_w2"], np.float32),
        "w1a": np.ascontiguousarray(np.asarray(inputs["L0_ffn_spline_w1"], np.float32)
                                    .transpose(1, 0, 2).reshape(2, 1024)),
        "b1f": packP(np.asarray(inputs["L0_ffn_spline_b1"], np.float32).reshape(-1), 8),
        "w2f": packP(np.asarray(inputs["L0_ffn_spline_w2"], np.float32).reshape(-1), 8),
        "b2p": np.asarray(inputs["L0_ffn_spline_b2"], np.float32).reshape(64, 1),
        "dirs": np.asarray(inputs["L0_ffn_directions"], np.float32),
        "gam": packP(inputs["L0_ffn_norm_gamma"], 8),
        "bet": packP(inputs["L0_ffn_norm_beta"], 8),
        "oscale_r": rep(np.asarray(inputs["L0_ffn_output_scale"], np.float32)[0]),
        "eye_bf": eye.astype(bfloat16),
        "eye_f32": eye,
        "ones_bf": np.ones((128, 128), np.float32).astype(bfloat16),
        "iota64": np.ascontiguousarray(np.broadcast_to(np.arange(64, dtype=np.float32), (128, 64))),
        "iotam": np.ascontiguousarray(np.broadcast_to(
            np.arange(8, dtype=np.float32) - 999.0, (128, 8))),
        "blk8": blk8.astype(bfloat16),
        "iota256": np.arange(SC, dtype=np.float32)[None, :],
        "iotaT": np.arange(TLOC, dtype=np.float32)[None, :],
        "consts8": np.ascontiguousarray(np.broadcast_to(np.array(
            [1e-5, 1.0, -1.0, -0.5, 256.0, 2048.0, 0.5, 0.0], np.float32), (128, 8))),
    }
    cb2 = np.zeros((128, 1), np.float32)
    cb2[0:2, 0] = np.asarray(inputs["L0_ffn_compress_b2"], np.float32)
    base["cb2"] = cb2
    maps = []
    for i in range(NCORE):
        m = dict(base)
        sl = np.zeros((NT, DM), np.float32)
        lo = i * TLOC - HALO
        s0 = max(lo, 0)
        sl[s0 - lo:, :] = x[s0:(i + 1) * TLOC, :]
        m["x_slab"] = sl
        m["c0mask"] = rep(0.0 if i == 0 else 1.0)
        chm = np.zeros((128, 7), np.float32)
        chm[:, :i] = 1.0
        m["chainmask"] = chm
        m["chaininv"] = np.ascontiguousarray(1.0 - chm)
        psel = np.zeros((128, 8), np.float32)
        if i > 0:
            psel[:, i - 1] = 1.0
        m["prevsel"] = psel
        maps.append(m)
    return maps


def kernel(**inputs):
    maps = _pack_inputs(inputs)
    nc, dram = build_nc()
    res = run_bass_kernel_spmd(nc, maps, core_ids=list(range(NCORE)))
    out = np.concatenate([res.results[i]["out"] for i in range(NCORE)], axis=0)
    return out.reshape(1, NCORE * TLOC, DM).astype(np.float32)


# version 17
# speedup vs baseline: 1.0464x; 1.0464x over previous
import contextlib
import numpy as np

import concourse.bass as bass
import concourse.mybir as mybir
from concourse.bass_utils import run_bass_kernel_spmd

F32 = mybir.dt.float32
BF16 = mybir.dt.bfloat16
AF = mybir.ActivationFunctionType
OP = mybir.AluOpType
AX = mybir.AxisListType

NCORE = 8
DM = 1024
DI = 2048
NS = 16
DTR = 64
TLOC = 2048
HALO = 3
NT = TLOC + HALO
PGM = DM // 128
PGI = DI // 128
TCH = 512
NTC = TLOC // TCH
SC = 256
NSC = TLOC // SC
EPS = 1e-5
ALPHA = 0.1


_RD_KEYS = ("in_", "in0", "in1", "lhsT", "rhs", "identity", "initial",
            "data0", "data1", "scalar", "scalar1", "scalar2", "scale", "bias")


class _Rec:

    def __init__(self):
        self.reads = []
        self.writes = []
        self.ops = []

    def __getattr__(self, name):
        def call(*a, **kw):
            if name in ("iota", "memset") and a and isinstance(a[0], bass.AP):
                self.writes.append(a[0])
            for k, v in kw.items():
                if k == "out":
                    if isinstance(v, bass.AP):
                        self.writes.append(v)
                elif k == "outs":
                    self.writes.extend(x for x in v if isinstance(x, bass.AP))
                elif k == "ins":
                    self.reads.extend(x for x in v if isinstance(x, bass.AP))
                elif k in _RD_KEYS and isinstance(v, bass.AP):
                    self.reads.append(v)
            self.ops.append((name, a, kw))
            return None
        return call


def _rng(ap):
    off = int(ap.offset)
    span = 1
    for d in ap.ap:
        span += abs(int(d[0])) * (int(d[1]) - 1)
    return (ap.tensor.name, off, off + span)


def _cost_ns(eng, dma, rw):
    ops = getattr(rw, "ops", [])
    if dma:
        b = 0
        for name, a, kw in ops:
            o = kw.get("out")
            if isinstance(o, bass.AP):
                el = 1
                for d in o.ap:
                    el *= int(d[1])
                es = 2 if "bf16" in str(o.tensor.dtype) else 4
                b += el * es
        return 900 + b * 0.0035
    tot = 0.0
    for name, a, kw in ops:
        if name == "collective_compute":
            b = 0
            for o in kw.get("outs", []):
                if isinstance(o, bass.AP):
                    el = 1
                    for d in o.ap:
                        el *= int(d[1])
                    b += el * 2
            tot += 3000 + b * 0.006
            continue
        ap = None
        for k in ("out", "in_", "in0"):
            v = kw.get(k)
            if isinstance(v, bass.AP):
                ap = v
                break
        if ap is None and a and isinstance(a[0], bass.AP):
            ap = a[0]
        el = 1
        if ap is not None:
            for d in list(ap.ap)[1:]:
                el *= int(d[1])
        if eng == "tensor":
            if name == "matmul":
                rhs = kw.get("rhs")
                n = 1
                if isinstance(rhs, bass.AP):
                    for d in list(rhs.ap)[1:]:
                        n *= int(d[1])
                tot += (n + 60) * 0.42
            else:
                tot += (el + 60) * 0.42
        elif eng == "vector":
            tot += (58 + el) * 1.05
        elif eng == "scalar":
            tot += (222 + el) * 0.84
        elif eng == "gpsimd":
            tot += (250 + el * 2.4) * 0.72
        else:
            tot += 100.0
    return tot


class Sched:
    ENGINES = ("sync", "tensor", "vector", "scalar", "gpsimd")
    RINGS = {"sync": "dR_sync", "scalar": "dR_scal", "gpsimd": "dR_gp"}

    def __init__(self, nc):
        self.nc = nc
        self.streams = {e: [] for e in self.ENGINES}
        self.cnt = {}
        self.enabled = True
        self.pos = {}
        self.emrec = {}
        self.lastw = {}
        self.lastr = {}
        self.carry = {e: {} for e in self.ENGINES}
        self.act_const = None
        self.phase = "init"
        self.trace = []

    def mksem(self, es, name):
        h = es.enter_context(self.nc.semaphore(name))
        self.cnt[name] = 0
        return h

    def emit(self, eng, fn, waits=(), inc=None, amt=1, dma=False):
        if not self.enabled:
            return None
        rw = _Rec()
        fn(rw)
        stream = ("dR_" + inc.name) if dma else eng
        pos = self.pos.get(stream, 0)
        know = self.carry[eng]
        deps = {}

        def add_dep(rec, p, raw_same=False):
            st = rec["stream"]
            if st == stream and not raw_same:
                return
            cur = deps.get(st)
            if cur is None or p > cur[0]:
                deps[st] = (p, rec)

        raw = []
        for w in waits:
            if not w:
                continue
            sem, val = w
            if val <= 0:
                continue
            r = self.emrec.get((sem.name, val))
            if r is not None:
                add_dep(r, r["pos"])
            else:
                raw.append((sem, val))
        reads = [_rng(a) for a in rw.reads]
        if eng == "scalar" and not dma and self.act_const is not None:
            reads.append(self.act_const)
        writes = [_rng(a) for a in rw.writes]
        for n, lo, hi in reads:
            for st, ent in self.lastw.get(n, {}).items():
                if ent[1] < hi and lo < ent[2]:
                    add_dep(ent[3], ent[0], raw_same=not dma)
        for n, lo, hi in writes:
            for dct in (self.lastw, self.lastr):
                for st, ent in dct.get(n, {}).items():
                    if ent[1] < hi and lo < ent[2]:
                        add_dep(ent[3], ent[0])
        wl = list(raw)
        if dma and pos > 0:
            prev = self.emrec.get((inc.name, self.cnt[inc.name]))
            if prev is not None and know.get(stream, -1) < prev["pos"]:
                wl.append((inc, prev["semval"]))
                for s2, p2 in prev["full"].items():
                    if know.get(s2, -1) < p2:
                        know[s2] = p2
        for st, (p, r) in sorted(deps.items(), key=lambda kv: -kv[1][0]):
            if know.get(st, -1) >= p:
                continue
            wl.append((r["sem"], r["semval"]))
            for s2, p2 in r["full"].items():
                if know.get(s2, -1) < p2:
                    know[s2] = p2
        full = dict(know)
        full[stream] = pos
        self.trace.append((stream, pos, [(r2["stream"], p2) for _, (p2, r2)
                                         in deps.items()],
                           _cost_ns(eng, dma, rw), self.phase))
        semval = None
        if inc is not None:
            self.cnt[inc.name] += amt
            semval = self.cnt[inc.name]
        rec = {"stream": stream, "pos": pos, "sem": inc, "semval": semval,
               "full": full}
        if inc is not None:
            self.emrec[(inc.name, semval)] = rec
        for n, lo, hi in writes:
            d = self.lastw.setdefault(n, {})
            old = d.get(stream)
            if old is not None:
                lo, hi = min(lo, old[1]), max(hi, old[2])
            d[stream] = (pos, lo, hi, rec)
        for n, lo, hi in reads:
            d = self.lastr.setdefault(n, {})
            old = d.get(stream)
            if old is not None:
                lo, hi = min(lo, old[1]), max(hi, old[2])
            d[stream] = (pos, lo, hi, rec)
        self.pos[stream] = pos + 1
        self.streams[eng].append((fn, wl, inc, amt))
        if inc is not None:
            return (inc, semval)
        return None

    def ev(self, sem):
        return (sem, self.cnt[sem.name])

    def build(self, block):
        def mk(stream):
            def body(eng):
                for fn, waits, inc, amt in stream:
                    for s, v in waits:
                        eng.wait_ge(s, v)
                    ins = fn(eng)
                    if inc is not None:
                        ins.then_inc(inc, amt)
            return body
        for name in self.ENGINES:
            if self.streams[name]:
                getattr(block, name)(mk(self.streams[name]))


def ap3(t, off, dims):
    base = t[:]
    part = list(base.ap[0])
    return bass.AP(base.tensor, base.offset + off, [part] + [list(d) for d in dims])


def build_nc(debug_names=(), stage="full"):
    nc = bass.Bass("TRN2", target_bir_lowering=False, debug=False)
    dram = {}

    def din(name, shape, dt=F32):
        dram[name] = nc.dram_tensor(name, shape, dt, kind="ExternalInput").ap()

    def dout(name, shape, dt=F32):
        dram[name] = nc.dram_tensor(name, shape, dt, kind="ExternalOutput").ap()

    def dscr(name, shape, dt=F32):
        dram[name] = nc.dram_tensor(name, shape, dt).ap()

    din("x_slab", [NT, DM])
    din("w_inT", [DM, 2 * DI]); din("w_outT", [DI, DM])
    din("xp_wT", [DI, DTR + 2 * NS]); din("dt_wT", [DTR, DI])
    din("ww_T", [DM, DM]); din("wr_T", [DM, DM])
    din("in_b", [128, 2 * PGI, 1]); din("out_b", [128, PGM, 1])
    din("conv_w", [128, PGI, 4]); din("conv_b", [128, PGI, 1])
    din("dt_b", [128, PGI, 1]); din("A_log", [128, PGI, NS]); din("D_ssm", [128, PGI, 1])
    din("n1w", [128, PGM, 1]); din("n2w", [128, PGM, 1])
    din("decay_r", [128, 1])
    din("sig_T", [DM, 64]); din("knots_r", [128, 8]); din("temp_r", [128, 1])
    din("cw1", [DM, 256]); din("cb1", [128, 2, 1]); din("cw2", [256, 2]); din("cb2", [128, 1])
    din("w1a", [2, 1024]); din("b1f", [128, 8, 1]); din("w2f", [128, 8, 1]); din("b2p", [64, 1])
    din("dirs", [64, DM]); din("gam", [128, PGM, 1]); din("bet", [128, PGM, 1]); din("oscale_r", [128, 1])
    din("eye_bf", [128, 128], BF16); din("eye_f32", [128, 128]); din("ones_bf", [128, 128], BF16)
    din("iota64", [128, 64]); din("iotam", [128, 8]); din("blk8", [128, 8, 64], BF16)
    din("c0mask", [128, 1]); din("chainmask", [128, 7]); din("chaininv", [128, 7]); din("prevsel", [128, 8])
    din("iota256", [1, SC]); din("iotaT", [1, TLOC]); din("consts8", [128, 8])
    dout("out", [TLOC, DM])

    dscr("b_winT", [DM, 2 * DI], BF16); dscr("b_woutT", [DI, DM], BF16)
    dscr("b_xpT", [DI, 96], BF16); dscr("b_dtT", [DTR, DI], BF16)
    dscr("b_wwT", [DM, DM], BF16); dscr("b_wrT", [DM, DM], BF16); dscr("b_cw1", [DM, 256], BF16)
    dscr("b_wwTlo", [DM, DM], BF16); dscr("b_wrTlo", [DM, DM], BF16)
    dscr("s_xfm", [128, PGM, TLOC])
    dscr("s_s1", [128, PGM, NT])
    dscr("s_o2", [128, PGM, TLOC])
    dscr("s_h1", [128, PGM, NT], BF16)
    dscr("s_xm", [128, PGI, TLOC], BF16)
    dscr("s_res", [128, PGI, TLOC])
    dscr("s_g", [128, PGI, TLOC], BF16)
    dscr("s_bc", [2 * NS, TLOC], BF16)
    dscr("s_delta", [128, PGI, TLOC])
    dscr("s_cdl", [128, PGI, TLOC], BF16)
    dscr("s_y", [128, PGI, TLOC])
    dscr("s_reads", [128, PGM, TLOC])
    dscr("s_row", [4, TLOC])
    dscr("cc_ch_i", [2, 128, PGI, NS])
    dscr("cc_ch_o", [NCORE, 2, 128, PGI, NS])
    dscr("cc_h1_i", [128, PGM, HALO])
    dscr("cc_h1_o", [NCORE, 128, PGM, HALO])
    dscr("cc_wk_i", [128, PGM, 1])
    dscr("cc_wk_o", [NCORE, 128, PGM, 1])
    dscr("cc_ag_i", [128, PGM, DM], BF16)
    dscr("cc_ag_o", [NCORE, 128, PGM, DM], BF16)

    with contextlib.ExitStack() as es:
        sch = Sched(nc)
        globals()["LAST_SCHED"] = sch
        q_ld = sch.mksem(es, "q_ld")
        q_g = sch.mksem(es, "q_g")
        q_s = sch.mksem(es, "q_s")
        q_st = sch.mksem(es, "q_st")
        e_pe = sch.mksem(es, "e_pe")
        e_dv = sch.mksem(es, "e_dv")
        e_ac = sch.mksem(es, "e_ac")
        e_gp = sch.mksem(es, "e_gp")
        q_l2 = sch.mksem(es, "q_l2")
        q_l3 = sch.mksem(es, "q_l3")
        q_l4 = sch.mksem(es, "q_l4")

        rot = {"on": False, "i": 0}

        def dma(qeng, sem, out, in_, waits=(), slow=False):
            if rot["on"] and sem is q_ld:
                rot["i"] = (rot["i"] + 1) % 4
                sem = (q_ld, q_l2, q_l3, q_l4)[rot["i"]]
            return sch.emit(qeng, lambda e, o=out, i=in_, s=slow: e.dma_start(
                out=o, in_=i, allow_slow_non_contiguous=s), waits=waits, inc=sem,
                amt=16, dma=True)

        def onpe(fn, waits=()):
            return sch.emit("tensor", fn, waits=waits, inc=e_pe)

        def ondv(fn, waits=()):
            return sch.emit("vector", fn, waits=waits, inc=e_dv)

        def onac(fn, waits=()):
            return sch.emit("scalar", fn, waits=waits, inc=e_ac)

        def ongp(fn, waits=()):
            return sch.emit("gpsimd", fn, waits=waits, inc=e_gp)

        def barrier():
            return [sch.ev(s) for s in (q_ld, q_l2, q_l3, q_l4, q_g, q_s, q_st,
                                        e_pe, e_dv, e_ac, e_gp)]

        def dvfence():
            return ondv(lambda e: e.tensor_reduce(out=k_fence[:, 1:2],
                                                  in_=ap3(k_fence, 0, [[0, 256]]),
                                                  axis=AX.X, op=OP.add))

        def tap(name, ap, waits):
            if name in debug_names:
                dout("tap_" + name, list(ap.shape), ap.dtype)
                dma("gpsimd", q_g, dram["tap_" + name], ap, waits=waits)

        S = lambda n, sh, dt=F32: es.enter_context(nc.sbuf_tensor(n, sh, dt))
        P = lambda n, sh: es.enter_context(nc.psum_tensor(n, sh, F32))

        k_eye = S("k_eye", [128, 128], BF16)
        k_eyef = S("k_eyef", [128, 128])
        k_ones = S("k_ones", [128, 128], BF16)
        k_inb = S("k_inb", [128, 2 * PGI, 1])
        k_outb = S("k_outb", [128, PGM, 1])
        k_cw = S("k_cw", [128, PGI, 4])
        k_cb = S("k_cb", [128, PGI, 1])
        k_dtb = S("k_dtb", [128, PGI, 1])
        k_dssm = S("k_dssm", [128, PGI, 1])
        k_n1 = S("k_n1", [128, PGM, 1])
        k_n2 = S("k_n2", [128, PGM, 1])
        k_gam = S("k_gam", [128, PGM, 1])
        k_bet = S("k_bet", [128, PGM, 1])
        k_knots = S("k_knots", [128, 8])
        k_kA = S("k_kA", [128, 7])
        k_kB = S("k_kB", [128, 7])
        k_io64 = S("k_io64", [128, 64])
        k_iom = S("k_iom", [128, 8])
        k_blk8 = S("k_blk8", [128, 8, 64], BF16)
        k_c0 = S("k_c0", [128, 1])
        k_chm = S("k_chm", [128, 7])
        k_chi = S("k_chi", [128, 7])
        k_psel = S("k_psel", [128, 8])
        k_osc = S("k_osc", [128, 1])
        k_cb1 = S("k_cb1", [128, 2, 1])
        k_cb2 = S("k_cb2", [128, 1])
        k_b1f = S("k_b1f", [128, 8, 1])
        k_w2f = S("k_w2f", [128, 8, 1])
        k_b2p = S("k_b2p", [64, 1])
        k_cw2 = S("k_cw2", [128, 2, 2], BF16)
        k_w1a = S("k_w1a", [2, 1024], BF16)
        k_dirs = S("k_dirs", [64, DM], BF16)
        k_qT = S("k_qT", [128, PGM, 64], BF16)
        k_csT = S("k_csT", [128, PGM, 8], BF16)
        k_misc = S("k_misc", [128, 80])
        k_fence = S("k_fence", [128, 8])
        k_tmp8 = S("k_tmp8", [128, 8])
        k_c8 = S("k_c8", [128, 8])
        sch.act_const = ("k_c8", 0, 9)
        k_MT = S("k_MT", [128, 2, SC])
        k_gw = S("k_gw", [128, 2, 1])
        k_gwi = S("k_gwi", [128, 2, 1])
        k_g256 = S("k_g256", [128, 1])
        k_g2048 = S("k_g2048", [128, 1])
        Apk = S("Apk", [128, PGI, NS])
        states = S("states", [128, PGI, NS])
        hend = S("hend", [128, PGI, NS])
        pend = S("pend", [128, PGI, NS])
        cdend = S("cdend", [128, PGI, 1])
        hin = S("hin", [128, PGI, NS])
        lnh = S("lnh", [128, PGI, NS])
        sgh = S("sgh", [128, PGI, NS])
        wk1 = S("wk1", [128, 2048])
        wk2 = S("wk2", [128, 2048])
        wkb = S("wkb", [128, 2048], BF16)
        wkb2 = S("wkb2", [128, 2048], BF16)
        sqb = S("sqb", [128, PGM, TCH], BF16)
        sA = S("sA", [128, PGM, NT])
        st_box = {}

        p1 = P("p1", [128, 512])
        p2 = P("p2", [128, 512])
        p3 = P("p3", [128, 512])
        p4 = P("p4", [128, 512])
        pbig = P("pbig", [128, 4, 512])

        blockctx = es.enter_context(nc.Block())
        STAGES = ["p0", "tr", "ln0", "m1", "m2", "att", "full"]
        SN_OFF = 2
        SN = STAGES.index(stage)

        sch.phase = "p0"
        for t, n in [(k_eye, "eye_bf"), (k_eyef, "eye_f32"), (k_ones, "ones_bf"),
                     (k_inb, "in_b"), (k_outb, "out_b"), (k_cw, "conv_w"), (k_cb, "conv_b"),
                     (k_dtb, "dt_b"), (k_dssm, "D_ssm"), (k_n1, "n1w"), (k_n2, "n2w"),
                     (k_gam, "gam"), (k_bet, "bet"), (k_knots, "knots_r"),
                     (k_io64, "iota64"), (k_iom, "iotam"), (k_blk8, "blk8"),
                     (k_c0, "c0mask"), (k_chm, "chainmask"), (k_chi, "chaininv"), (k_psel, "prevsel"),
                     (k_osc, "oscale_r"), (k_cb1, "cb1"), (k_cb2, "cb2"),
                     (k_b1f, "b1f"), (k_w2f, "w2f"), (k_b2p, "b2p"), (k_c8, "consts8")]:
            dma("sync", q_ld, t[:], dram[n])
        ld_const = sch.ev(q_ld)
        ondv(lambda e: e.memset(k_fence[:], 0.0))
        rot["on"] = True
        for i, v in enumerate([EPS, 1.0, -1.0, -0.5, 256.0, 2048.0, 0.5, 0.0]):
            nc.const_aps.register(F32, v, k_c8[:, i:i + 1])

        ev = dma("sync", q_ld, wk1[:, 0:PGI * NS], dram["A_log"].rearrange("p a b -> p (a b)"))
        onac(lambda e: e.activation(out=wk2[:, 0:PGI * NS], in_=wk1[:, 0:PGI * NS], func=AF.Exp),
             waits=[ev])
        ondv(lambda e: e.tensor_scalar_mul(out=Apk[:].rearrange("p a b -> p (a b)"),
                                           in0=wk2[:, 0:PGI * NS], scalar1=-1.0),
             waits=[sch.ev(e_ac)])

        def prep_w(src, dst, rows, cols, lo_dst=None, ping=[0]):
            for r0 in range(0, rows, 128):
                rr = min(128, rows - r0)
                for c0 in range(0, cols, 2048):
                    cc = min(2048, cols - c0)
                    if lo_dst is None:
                        wsrc = wk1 if ping[0] == 0 else wk2
                        wdst = wkb if ping[0] == 0 else wkb2
                        ping[0] ^= 1
                    else:
                        wsrc, wdst = wk1, wkb
                    ev1 = dma("sync", q_ld, wsrc[0:rr, 0:cc], dram[src][r0:r0 + rr, c0:c0 + cc])
                    ev2 = ondv(lambda e, rr=rr, cc=cc, wsrc=wsrc, wdst=wdst: e.tensor_copy(
                        out=wdst[0:rr, 0:cc], in_=wsrc[0:rr, 0:cc]), waits=[ev1])
                    dma("scalar", q_s, dram[dst][r0:r0 + rr, c0:c0 + cc], wdst[0:rr, 0:cc],
                        waits=[ev2])
                    if lo_dst is not None:
                        ondv(lambda e, rr=rr, cc=cc: e.tensor_copy(out=wk2[0:rr, 0:cc],
                                                                   in_=wkb[0:rr, 0:cc]))
                        ev3 = ondv(lambda e, rr=rr, cc=cc: e.tensor_sub(out=wkb2[0:rr, 0:cc],
                                                                        in0=wk1[0:rr, 0:cc],
                                                                        in1=wk2[0:rr, 0:cc]))
                        dma("scalar", q_s, dram[lo_dst][r0:r0 + rr, c0:c0 + cc], wkb2[0:rr, 0:cc],
                            waits=[ev3])

        for a in [("w_inT", "b_winT", DM, 2 * DI), ("w_outT", "b_woutT", DI, DM),
                  ("xp_wT", "b_xpT", DI, 96), ("dt_wT", "b_dtT", DTR, DI),
                  ("ww_T", "b_wwT", DM, DM, "b_wwTlo"), ("wr_T", "b_wrT", DM, DM, "b_wrTlo"),
                  ("cw1", "b_cw1", DM, 256)]:
            prep_w(*a)
        ev_wprep = sch.ev(q_s)

        ev1 = dma("sync", q_ld, wk1[:, 0:4].rearrange("p (a c) -> p a c", a=2),
                  dram["cw2"].rearrange("(a p) c -> p a c", p=128),
                  waits=[sch.ev(e_dv)])
        ondv(lambda e: e.tensor_copy(out=k_cw2[:].rearrange("p a b -> p (a b)"), in_=wk1[:, 0:4]),
             waits=[ev1])
        ev1 = dma("sync", q_ld, wk1[0:2, 0:1024], dram["w1a"], waits=[sch.ev(e_dv)])
        ondv(lambda e: e.tensor_copy(out=k_w1a[:], in_=wk1[0:2, 0:1024]), waits=[ev1])
        ev1 = dma("sync", q_ld, wk1[0:64, 0:DM], dram["dirs"], waits=[sch.ev(e_dv)])
        ondv(lambda e: e.tensor_copy(out=k_dirs[:], in_=wk1[0:64, 0:DM]), waits=[ev1])

        ondv(lambda e: e.tensor_sub(out=k_misc[:, 0:7], in0=k_knots[:, 1:8],
                                    in1=k_knots[:, 0:7]), waits=[ld_const])
        dvfence()
        ondv(lambda e: e.tensor_mul(out=k_misc[:, 8:15], in0=k_misc[:, 0:7],
                                    in1=k_io64[:, 0:7]))
        dvfence()
        ondv(lambda e: e.tensor_sub(out=k_misc[:, 8:15], in0=k_knots[:, 0:7],
                                    in1=k_misc[:, 8:15]))
        dvfence()
        def build_calib(e):
            e.tensor_copy(out=k_kA[:, 0:7], in_=k_misc[:, 8:15])
            return e.tensor_copy(out=k_kB[:, 0:7], in_=k_misc[:, 0:7])
        ondv(build_calib)
        dvfence()
        def build_calib2(e):
            e.tensor_sub(out=k_kA[:, 1:7], in0=k_kA[:, 1:7], in1=k_misc[:, 8:14])
            return e.tensor_sub(out=k_kB[:, 1:7], in0=k_kB[:, 1:7], in1=k_misc[:, 0:6])
        ondv(build_calib2)

        ev1 = dma("sync", q_ld, k_misc[:, 16:17], dram["temp_r"])
        ondv(lambda e: e.tensor_scalar_add(out=k_misc[:, 64:72],
                                           in0=ap3(k_misc, 16, [[0, 8]]), scalar1=1e-6),
             waits=[ev1])
        dvfence()
        ondv(lambda e: e.reciprocal(out=k_misc[:, 64:72], in_=k_misc[:, 64:72]))

        ev1 = dma("sync", q_ld, k_misc[:, 17:18], dram["decay_r"])
        ondv(lambda e: e.tensor_copy(out=k_misc[:, 32:40],
                                     in_=ap3(k_misc, 17, [[0, 8]])), waits=[ev1, ld_const])
        def g_a(e):
            e.activation(out=k_misc[:, 40:48], in_=k_misc[:, 32:40], func=AF.Abs)
            e.activation(out=k_misc[:, 40:48], in_=k_misc[:, 40:48], func=AF.Exp, scale=-1.0)
            e.activation(out=k_misc[:, 40:48], in_=k_misc[:, 40:48], func=AF.Ln, bias=1.0)
            return e.activation(out=k_misc[:, 48:56], in_=k_misc[:, 32:40], func=AF.Relu,
                                scale=-1.0)
        onac(g_a, waits=[sch.ev(e_dv)])
        ondv(lambda e: e.scalar_tensor_tensor(out=k_misc[:, 56:64], in0=k_misc[:, 40:48],
                                              scalar=-1.0, in1=k_misc[:, 48:56],
                                              op0=OP.mult, op1=OP.subtract),
             waits=[sch.ev(e_ac)])
        onac(lambda e: e.activation(out=k_misc[:, 24:32], in_=k_misc[:, 56:64],
                                    func=AF.Relu, scale=-1.0), waits=[sch.ev(e_dv)])
        ev_lg = sch.ev(e_ac)
        def g_c(e):
            e.activation(out=k_g256[:], in_=k_misc[:, 56:57], func=AF.Exp, scale=float(SC))
            return e.activation(out=k_g2048[:], in_=k_misc[:, 56:57], func=AF.Exp,
                                scale=float(TLOC))
        onac(g_c, waits=[sch.ev(e_dv)])
        def g_iotas(e):
            e.iota(k_misc[:, 72:73], [[0, 1]], base=SC - 1, channel_multiplier=-1,
                   allow_small_or_imprecise_dtypes=True)
            return e.iota(k_misc[:, 73:74], [[0, 1]], base=SC - 129, channel_multiplier=-1,
                          allow_small_or_imprecise_dtypes=True)
        ongp(g_iotas, waits=[ld_const])
        def g_d(e):
            e.tensor_scalar(out=k_misc[:, 74:75], in0=k_misc[:, 72:73],
                            scalar1=k_misc[:, 24:25], scalar2=-1.0, op0=OP.mult, op1=OP.mult)
            return e.tensor_scalar(out=k_misc[:, 75:76], in0=k_misc[:, 73:74],
                                   scalar1=k_misc[:, 24:25], scalar2=-1.0, op0=OP.mult,
                                   op1=OP.mult)
        ondv(g_d, waits=[sch.ev(e_gp), ev_lg])
        def g_e(e):
            e.activation(out=k_gw[:, 0, :], in_=k_misc[:, 74:75], func=AF.Exp)
            e.activation(out=k_gw[:, 1, :], in_=k_misc[:, 75:76], func=AF.Exp)
            e.activation(out=k_gwi[:, 0, :], in_=k_misc[:, 74:75], func=AF.Exp, scale=-1.0)
            return e.activation(out=k_gwi[:, 1, :], in_=k_misc[:, 75:76], func=AF.Exp,
                                scale=-1.0)
        onac(g_e, waits=[sch.ev(e_dv)])

        ev_i = dma("sync", q_ld, wk1[0:1, 0:SC], dram["iota256"])
        ev_j = dma("sync", q_ld, wk2[0:1, 0:TLOC], dram["iotaT"])
        def g_rows(e):
            e.tensor_scalar(out=wk1[0:1, 0:SC], in0=wk1[0:1, 0:SC], scalar1=k_misc[0:1, 24:25],
                            scalar2=-1.0, op0=OP.mult, op1=OP.mult)
            return e.tensor_scalar(out=wk2[0:1, 0:TLOC], in0=wk2[0:1, 0:TLOC],
                                   scalar1=k_misc[0:1, 24:25], scalar2=-1.0, op0=OP.mult,
                                   op1=OP.mult)
        ondv(g_rows, waits=[ev_i, ev_j, ev_lg])
        def g_rows2(e):
            e.activation(out=wk1[0:1, 0:SC], in_=wk1[0:1, 0:SC], func=AF.Exp)
            return e.activation(out=wk2[0:1, 0:TLOC], in_=wk2[0:1, 0:TLOC], func=AF.Exp)
        onac(g_rows2, waits=[sch.ev(e_dv)])
        ev1 = dma("scalar", q_s, dram["s_row"][1:2, 0:SC], wk1[0:1, 0:SC],
                  waits=[sch.ev(e_ac), ev_wprep])
        ev2 = dma("scalar", q_s, dram["s_row"][0:1, 0:TLOC], wk2[0:1, 0:TLOC], waits=[sch.ev(e_ac)])
        ev_grow = [ev1, ev2]

        def mt_iota(e):
            e.iota(wk1[:, 0:SC], [[1, SC]], base=-1, channel_multiplier=-1,
                   allow_small_or_imprecise_dtypes=True)
            return e.iota(wk1[:, SC:2 * SC], [[1, SC]], base=-129, channel_multiplier=-1,
                          allow_small_or_imprecise_dtypes=True)
        ongp(mt_iota, waits=[sch.ev(e_ac), sch.ev(q_s)])
        ondv(lambda e: e.tensor_scalar(out=wk2[:, 0:2 * SC], in0=wk1[:, 0:2 * SC],
                                       scalar1=k_misc[:, 24:25], scalar2=-1.0, op0=OP.mult,
                                       op1=OP.mult),
             waits=[sch.ev(e_gp)])
        onac(lambda e: e.activation(out=wk2[:, 0:2 * SC], in_=wk2[:, 0:2 * SC], func=AF.Exp),
             waits=[sch.ev(e_dv)])
        def mt_mask(e):
            e.affine_select(out=wk1[:, 0:SC], in_=wk2[:, 0:SC], pattern=[[1, SC]],
                            compare_op=OP.is_ge, fill=0.0, base=-1, channel_multiplier=-1)
            return e.affine_select(out=wk1[:, SC:2 * SC], in_=wk2[:, SC:2 * SC], pattern=[[1, SC]],
                                   compare_op=OP.is_ge, fill=0.0, base=-129, channel_multiplier=-1)
        ongp(mt_mask, waits=[sch.ev(e_ac)])
        ondv(lambda e: e.tensor_copy(out=k_MT[:].rearrange("p a b -> p (a b)"), in_=wk1[:, 0:2 * SC]),
             waits=[sch.ev(e_gp)])

        for pg in range(PGM):
            ev1 = dma("sync", q_ld, wk1[:, 0:64], dram["sig_T"][pg * 128:(pg + 1) * 128, :],
                      waits=[sch.ev(e_dv)])
            def qf(e, pg=pg):
                e.tensor_single_scalar(out=wk2[:, 0:64], in_=wk1[:, 0:64], scalar=0.3, op=OP.is_gt)
                return e.tensor_single_scalar(out=wk2[:, 64:128], in_=wk1[:, 0:64], scalar=-0.3,
                                              op=OP.is_ge)
            ondv(qf, waits=[ev1])
            dvfence()
            ondv(lambda e: e.scalar_tensor_tensor(out=wk2[:, 0:64], in0=wk2[:, 0:64],
                                                  scalar=-1.0, in1=wk2[:, 64:128],
                                                  op0=OP.add, op1=OP.add))
            dvfence()
            def qf2(e, pg=pg):
                e.tensor_copy(out=k_qT[:, pg, :], in_=wk2[:, 0:64])
                return e.tensor_reduce(out=wk2[:, 128:136],
                                       in_=wk2[:, 0:64].rearrange("p (c j) -> p c j", c=8),
                                       axis=AX.X, op=OP.add)
            ondv(qf2)
            onac(lambda e: e.activation(out=wk2[:, 136:144], in_=wk2[:, 128:136], func=AF.Sign),
                 waits=[sch.ev(e_dv)])
            ondv(lambda e, pg=pg: e.tensor_copy(out=k_csT[:, pg, :], in_=wk2[:, 136:144]),
                 waits=[sch.ev(e_ac)])

        sch.phase = "p1_transpose"
        stA_cm = [nc.sbuf_tensor("stA%d" % i, [128, NT], F32) for i in range(3)]
        st_box["mu"], st_box["ri"], st_box["t"] = [c.__enter__() for c in stA_cm]
        if SN < 1:
            sch.enabled = False
        st_mu0 = st_box["mu"]
        NTT = (NT + 127) // 128
        for tt in range(NTT):
            r0 = tt * 128
            rr = min(128, NT - r0)
            xo = (tt % 2) * DM
            evm = None
            if rr < 128:
                evm = ondv(lambda e, xo=xo: e.memset(st_mu0[:, xo:xo + DM], 0.0))
            evl = dma("sync", q_ld, st_mu0[0:rr, xo:xo + DM], dram["x_slab"][r0:r0 + rr, :],
                      waits=[evm] if evm else [])
            for dg in range(PGM):
                evt = onpe(lambda e, dg=dg, xo=xo: e.transpose(
                    out=p1[:, (dg % 4) * 128:(dg % 4) * 128 + 128],
                    in_=st_mu0[:, xo + dg * 128:xo + (dg + 1) * 128],
                    identity=k_eyef[:]), waits=[evl, ld_const])
                ondv(lambda e, dg=dg, r0=r0, rr=rr: e.tensor_copy(
                    out=sA[:, dg, r0:r0 + rr],
                    in_=p1[0:128, (dg % 4) * 128:(dg % 4) * 128 + rr]), waits=[evt])
        ev_xfm = sch.ev(e_dv)
        for pg in range(PGM):
            dma("scalar", q_s, dram["s_xfm"][:, pg, :], sA[:, pg, HALO:NT], waits=[ev_xfm])

        if SN < 2:
            sch.enabled = False
        def emit_ln(src, dst, normw, rms_only, width, cast_bf=None, gb=None, bb=None):
            st_mu, st_ri, st_t = st_box["mu"], st_box["ri"], st_box["t"]
            for nc0 in range(0, width, TCH):
                w = min(TCH, width - nc0)
                eva = onac(lambda e, nc0=nc0, w=w: e.activation(out=sqb[:, :, 0:w],
                                                                in_=src[:, :, nc0:nc0 + w],
                                                                func=AF.Square),
                           waits=[])
                last = None
                for kg in range(PGM):
                    last = onpe(lambda e, kg=kg, w=w: e.matmul(out=p3[:, 0:w], lhsT=k_ones[:],
                                                               rhs=sqb[:, kg, 0:w],
                                                               start=(kg == 0), stop=(kg == PGM - 1)),
                                waits=[eva])
                ondv(lambda e, nc0=nc0, w=w: e.tensor_scalar_mul(out=st_t[:, nc0:nc0 + w],
                                                                 in0=p3[:, 0:w], scalar1=1.0 / DM),
                     waits=[last])
                if not rms_only:
                    evc = onac(lambda e, nc0=nc0, w=w: e.activation(out=sqb[:, :, 0:w],
                                                                    in_=src[:, :, nc0:nc0 + w],
                                                                    func=AF.Identity),
                               waits=[last])
                    last = None
                    for kg in range(PGM):
                        last = onpe(lambda e, kg=kg, w=w: e.matmul(out=p4[:, 0:w], lhsT=k_ones[:],
                                                                   rhs=sqb[:, kg, 0:w],
                                                                   start=(kg == 0), stop=(kg == PGM - 1)),
                                    waits=[evc])
                    ondv(lambda e, nc0=nc0, w=w: e.tensor_scalar_mul(out=st_mu[:, nc0:nc0 + w],
                                                                     in0=p4[:, 0:w], scalar1=1.0 / DM),
                         waits=[last])
            ev_red = sch.ev(e_dv)
            def fin1(e):
                if not rms_only:
                    e.tensor_mul(out=st_ri[:, 0:width], in0=st_mu[:, 0:width], in1=st_mu[:, 0:width])
                    return e.tensor_sub(out=st_t[:, 0:width], in0=st_t[:, 0:width],
                                        in1=st_ri[:, 0:width])
                return e.tensor_copy(out=st_ri[:, 0:width], in_=st_t[:, 0:width])
            ondv(fin1, waits=[ev_red])
            def fin2(e):
                e.activation(out=st_ri[:, 0:width], in_=st_t[:, 0:width], func=AF.Ln, bias=EPS)
                return e.activation(out=st_ri[:, 0:width], in_=st_ri[:, 0:width], func=AF.Exp,
                                    scale=-0.5)
            onac(fin2, waits=[sch.ev(e_dv)])
            ev_r = sch.ev(e_ac)
            def app(e):
                last = None
                for pg in range(PGM):
                    if rms_only:
                        last = e.scalar_tensor_tensor(out=dst[:, pg, 0:width], in0=src[:, pg, 0:width],
                                                      scalar=normw[:, pg, :], in1=st_ri[:, 0:width],
                                                      op0=OP.mult, op1=OP.mult)
                    else:
                        e.tensor_sub(out=dst[:, pg, 0:width], in0=src[:, pg, 0:width],
                                     in1=st_mu[:, 0:width])
                        last = e.tensor_mul(out=dst[:, pg, 0:width], in0=dst[:, pg, 0:width],
                                            in1=st_ri[:, 0:width])
                    if gb is not None:
                        last = e.tensor_scalar(out=dst[:, pg, 0:width], in0=dst[:, pg, 0:width],
                                               scalar1=gb[:, pg, :], scalar2=bb[:, pg, :],
                                               op0=OP.mult, op1=OP.add)
                    if cast_bf is not None:
                        last = e.tensor_copy(out=cast_bf[:, pg, 0:width], in_=dst[:, pg, 0:width])
                return last
            ondv(app, waits=[ev_r])
            return sch.ev(e_dv)

        ev_ln0 = emit_ln(sA, sA, None, False, NT)
        tap("ln0", sA[:, :, :], [ev_ln0])
        if SN < 3:
            sch.enabled = False
        for pg in range(PGM):
            dma("scalar", q_s, dram["s_s1"][:, pg, :], sA[:, pg, :], waits=[ev_ln0])
        ev_s1sp = sch.ev(q_s)

        def emit_mamba(which, arh, wsl):
            normw = k_n1 if which == 1 else k_n2
            ev_rms = emit_ln(sA, sA, normw, True, NT, cast_bf=arh)
            tap("rms%d" % which, arh[:, :, :], [ev_rms])
            for pg in range(PGM):
                dma("scalar", q_s, dram["s_h1"][:, pg, :], arh[:, pg, :], waits=[ev_rms])
            ev_h1sp = sch.ev(q_s)

            sch.phase = "m_inproj"
            rowb = wkb
            with nc.sbuf_tensor("rowb%d" % which, [128, NT], BF16) as rowb2:
                for mg in range(PGI):
                    evw = dma("sync", q_ld, wsl[:].rearrange("p a b -> p a b"),
                              dram["b_winT"][:, mg * 128:(mg + 1) * 128]
                              .rearrange("(a p) c -> p a c", p=128),
                              waits=[])
                    for inch, nc0 in enumerate(range(0, NT, TCH)):
                        w = min(TCH, NT - nc0)
                        pb = p1 if inch % 2 == 0 else p2
                        def ipq(e, nc0=nc0, w=w, pb=pb):
                            r = None
                            for kg in range(PGM):
                                r = e.matmul(out=pb[:, 0:w], lhsT=wsl[:, kg, 0:128],
                                             rhs=arh[:, kg, nc0:nc0 + w], start=(kg == 0),
                                             stop=(kg == PGM - 1))
                            return r
                        last = onpe(ipq, waits=[evw, ev_rms])
                        ondv(lambda e, mg=mg, nc0=nc0, w=w, pb=pb: e.tensor_scalar(
                            out=rowb2[:, nc0:nc0 + w], in0=pb[:, 0:w], scalar1=k_inb[:, mg, :],
                            scalar2=None, op0=OP.add), waits=[last])
                    evrow = ondv(lambda e: e.tensor_scalar(out=rowb2[:, 0:HALO], in0=rowb2[:, 0:HALO],
                                                           scalar1=k_c0[:, 0:1], scalar2=None,
                                                           op0=OP.mult))
                    evd = ondv(lambda e, mg=mg: [e.tensor_scalar(
                        out=wkb2[:, 128 * k:128 * (k + 1)], in0=k_eye[:],
                        scalar1=k_cw[:, mg, k:k + 1], scalar2=None, op0=OP.mult)
                        for k in range(4)][-1], waits=[evrow])
                    for inch, nc0 in enumerate(range(0, TLOC, TCH)):
                        pb = p3 if inch % 2 == 0 else p4
                        def cvq(e, nc0=nc0, pb=pb):
                            r = None
                            for k in range(4):
                                r = e.matmul(out=pb[:, 0:TCH], lhsT=wkb2[:, 128 * k:128 * (k + 1)],
                                             rhs=rowb2[:, nc0 + k:nc0 + k + TCH], start=(k == 0),
                                             stop=(k == 3))
                            return r
                        last = onpe(cvq, waits=[evd])
                        eva = onac(lambda e, mg=mg, pb=pb: e.activation(
                            out=wkb[:, 0:TCH], in_=pb[:, 0:TCH], func=AF.Silu, bias=k_cb[:, mg, :]),
                            waits=[last])
                        dma("scalar", q_s, dram["s_xm"][:, mg, nc0:nc0 + TCH], wkb[:, 0:TCH],
                            waits=[eva])
            ev_xm = sch.ev(q_s)
            tap("xm%d" % which, wkb[:, 0:TCH], [ev_xm])

            sch.phase = "m_xdbl"
            xdb = st_box["t"]
            evw = dma("sync", q_ld, wkb2[:, 0:PGI * 96].rearrange("p (a c) -> p a c", a=PGI),
                      dram["b_xpT"].rearrange("(a p) c -> p a c", p=128),
                      waits=barrier())
            for inch, nc0 in enumerate(range(0, TLOC, TCH)):
                pb = p1 if inch % 2 == 0 else p2
                last = None
                for kh in range(2):
                    evxs = [dma("sync", q_ld, sqb[:, kg % PGM, 0:TCH],
                                dram["s_xm"][:, kg, nc0:nc0 + TCH], waits=[ev_xm])
                            for kg in range(kh * 8, kh * 8 + 8)]
                    def xdq(e, kh=kh, pb=pb):
                        r = None
                        for kg in range(kh * 8, kh * 8 + 8):
                            r = e.matmul(
                                out=pb[0:96, 0:TCH], lhsT=wkb2[:, kg * 96:(kg + 1) * 96],
                                rhs=sqb[:, kg % PGM, 0:TCH], start=(kg == 0),
                                stop=(kg == PGI - 1))
                        return r
                    last = onpe(xdq, waits=evxs + [evw])
                ondv(lambda e, nc0=nc0, pb=pb: e.tensor_copy(out=xdb[0:96, nc0:nc0 + TCH],
                                                             in_=pb[0:96, 0:TCH]), waits=[last])
            ev_xdb = sch.ev(e_dv)
            tap("xdb%d" % which, xdb[0:96, 0:TLOC], [ev_xdb])
            ondv(lambda e: e.tensor_copy(out=wkb[0:32, 0:TLOC], in_=xdb[64:96, 0:TLOC]),
                 waits=[ev_xdb])
            ev_bc = dma("scalar", q_s, dram["s_bc"], wkb[0:32, 0:TLOC], waits=[])

            sch.phase = "m_delta"
            evw = dma("sync", q_ld, wkb2[0:64, 0:DI], dram["b_dtT"], waits=barrier())
            sqf = sqb[:].rearrange("p a b -> p (a b)")
            ev_x64 = ondv(lambda e: e.tensor_copy(out=sqf[0:64, 0:TLOC], in_=xdb[0:64, 0:TLOC]),
                          waits=[ev_xdb, evw])
            ondv(lambda e: e.memset(states[:].rearrange("p a b -> p (a b)"), 0.0))
            ondv(lambda e: e.memset(cdend[:].rearrange("p a b -> p (a b)"), 0.0))
            ondv(lambda e: e.memset(wk2[:, 0:TLOC], 1.0))
            ev_ones = sch.ev(e_dv)
            for pg in range(PGI):
                for itc in range(NTC):
                    nc0 = itc * TCH
                    pb = p3 if itc % 2 == 0 else p4
                    evq = onpe(lambda e, pg=pg, nc0=nc0, pb=pb: e.matmul(
                        out=pb[:, 0:TCH], lhsT=wkb2[0:64, pg * 128:(pg + 1) * 128],
                        rhs=sqf[0:64, nc0:nc0 + TCH], start=True, stop=True),
                        waits=[evw, ev_x64])
                    def sp(e, pg=pg, pb=pb, nc0=nc0):
                        smu = st_box["mu"]
                        e.activation(out=wk1[:, nc0:nc0 + TCH], in_=pb[:, 0:TCH], func=AF.Relu,
                                     bias=k_dtb[:, pg, :])
                        e.activation(out=smu[:, 0:TCH], in_=pb[:, 0:TCH], func=AF.Abs,
                                     bias=k_dtb[:, pg, :])
                        e.activation(out=smu[:, 0:TCH], in_=smu[:, 0:TCH], func=AF.Exp,
                                     scale=-1.0)
                        return e.activation(out=smu[:, 0:TCH], in_=smu[:, 0:TCH],
                                            func=AF.Ln, bias=1.0)
                    evsp = onac(sp, waits=[evq])
                    ondv(lambda e, nc0=nc0: e.tensor_add(out=wk1[:, nc0:nc0 + TCH],
                                                         in0=wk1[:, nc0:nc0 + TCH],
                                                         in1=st_box["mu"][:, 0:TCH]),
                         waits=[evsp, ev_ones, ev_bc])
                def cds(e, pg=pg):
                    e.tensor_tensor_scan(out=wkb[:, 0:TLOC], data0=wk2[:, 0:TLOC],
                                         data1=wk1[:, 0:TLOC], initial=0.0,
                                         op0=OP.mult, op1=OP.add)
                    return e.tensor_reduce(out=cdend[:, pg, :], in_=wk1[:, 0:TLOC], axis=AX.X,
                                           op=OP.add)
                ondv(cds)
                ev2 = sch.ev(e_dv)
                dma("scalar", q_s, dram["s_delta"][:, pg, :], wk1[:, 0:TLOC], waits=[ev2])
                dma("scalar", q_s, dram["s_cdl"][:, pg, :], wkb[:, 0:TLOC], waits=[ev2])
            ev_delta = sch.ev(q_s)

            sch.phase = "m_scan"
            sbc = dram["s_bc"]
            NH = NS // 2
            with nc.sbuf_tensor("bcc%d" % which, [128, 2 * NS, TCH], BF16) as bcc:
                for itc in range(NTC):
                    nc0 = itc * TCH
                    evbc = dma("gpsimd", q_g, bcc[:],
                               bass.AP(sbc.tensor, sbc.offset + nc0,
                                       [[0, 128], [TLOC, 2 * NS], [1, TCH]]),
                               waits=[ev_bc, ev_delta])
                    for pg in range(PGI):
                        po = (pg % 2) * TCH
                        if pg % 2 == 0:
                            evd2 = dma("sync", q_ld,
                                       ap3(wk1, 0, [[TCH, 2], [1, TCH]]),
                                       dram["s_delta"][:, pg:pg + 2, nc0:nc0 + TCH],
                                       waits=[ev_delta])
                            evx2 = dma("sync", q_ld,
                                       ap3(wkb2, 0, [[TCH, 2], [1, TCH]]),
                                       dram["s_xm"][:, pg:pg + 2, nc0:nc0 + TCH],
                                       waits=[ev_xm])
                        evd, evx = evd2, evx2
                        evwm = ondv(lambda e, po=po: e.tensor_tensor(
                            out=wkb[:, po:po + TCH], in0=wk1[:, po:po + TCH],
                            in1=wkb2[:, po:po + TCH], op=OP.mult),
                            waits=[evd, evx])
                        py = p1 if pg % 2 == 0 else p2
                        for h in range(2):
                            s0 = h * NH
                            evbb = ondv(lambda e, s0=s0, po=po: e.tensor_tensor(
                                out=sqb[:].rearrange("p a b -> p (a b)"),
                                in0=ap3(wkb, po, [[0, NH], [1, TCH]]),
                                in1=bcc[:, s0:s0 + NH, :], op=OP.mult),
                                waits=[evwm, evbc])
                            for q in range(2):
                                ebuf = wk2 if q == 0 else st_box["t"]
                                def evaq(e, pg=pg, s0=s0, q=q, ebuf=ebuf, po=po):
                                    r = None
                                    for si in range(q * 4, q * 4 + 4):
                                        r = e.activation(
                                            out=ebuf[:, (si % 4) * TCH:(si % 4) * TCH + TCH],
                                            in_=wk1[:, po:po + TCH], func=AF.Exp,
                                            scale=Apk[:, pg, s0 + si:s0 + si + 1])
                                    return r
                                eva = onac(evaq, waits=[evd])
                                def scnq(e, pg=pg, s0=s0, q=q, ebuf=ebuf):
                                    r = None
                                    for si in range(q * 4, q * 4 + 4):
                                        r = e.tensor_tensor_scan(
                                            out=sqb[:, si, :],
                                            data0=ebuf[:, (si % 4) * TCH:(si % 4) * TCH + TCH],
                                            data1=sqb[:, si, :],
                                            initial=states[:, pg, s0 + si:s0 + si + 1],
                                            op0=OP.mult, op1=OP.add)
                                    return r
                                ondv(scnq, waits=[eva, evbb])
                            dvfence()
                            ondv(lambda e, pg=pg, s0=s0: e.tensor_copy(
                                out=states[:, pg, s0:s0 + NH],
                                in_=ap3(sqb, TCH - 1, [[TCH, NH]])))
                            evu = ondv(lambda e, s0=s0: e.tensor_tensor(
                                out=sqb[:].rearrange("p a b -> p (a b)"),
                                in0=sqb[:].rearrange("p a b -> p (a b)"),
                                in1=bcc[:, NS + s0:NS + s0 + NH, :], op=OP.mult))
                            def yredq(e, py=py, h=h):
                                r = None
                                for si in range(NH):
                                    r = e.matmul(out=py[:, 0:TCH], lhsT=k_eye[:],
                                                 rhs=sqb[:, si, :],
                                                 start=(h == 0 and si == 0),
                                                 stop=(h == 1 and si == NH - 1))
                                return r
                            onpe(yredq, waits=[evu])
                        evy = ondv(lambda e, pg=pg, py=py, po=po: e.scalar_tensor_tensor(
                            out=st_box["ri"][:, po:po + TCH], in0=wkb2[:, po:po + TCH],
                            scalar=k_dssm[:, pg, :],
                            in1=py[:, 0:TCH], op0=OP.mult, op1=OP.add), waits=[])
                        dma("scalar", q_s, dram["s_y"][:, pg, nc0:nc0 + TCH],
                            st_box["ri"][:, po:po + TCH], waits=[evy])
            ev_scan = sch.ev(q_s)

            ondv(lambda e: e.tensor_copy(out=hend[:].rearrange("p a b -> p (a b)"),
                                         in_=states[:].rearrange("p a b -> p (a b)")),
                 waits=[sch.ev(e_dv)])
            def pendf(e):
                cb = ap3(cdend, 0, [[1, PGI], [0, NS]])
                return e.tensor_tensor(out=pend[:], in0=cb, in1=Apk[:], op=OP.mult)
            ondv(pendf)
            onac(lambda e: e.activation(out=pend[:].rearrange("p a b -> p (a b)"),
                                        in_=pend[:].rearrange("p a b -> p (a b)"), func=AF.Exp),
                 waits=[sch.ev(e_dv)])
            ev_pend = sch.ev(e_ac)

            ev1 = dma("gpsimd", q_g, dram["cc_ch_i"][0], hend[:], waits=[sch.ev(e_dv)])
            ev2 = dma("gpsimd", q_g, dram["cc_ch_i"][1], pend[:], waits=[ev_pend])
            evc = ongp(lambda e: e.collective_compute(
                "AllGather", OP.bypass, replica_groups=[list(range(NCORE))],
                ins=[dram["cc_ch_i"].opt()], outs=[dram["cc_ch_o"].opt()]), waits=[ev1, ev2])
            ondv(lambda e: e.memset(hin[:].rearrange("p a b -> p (a b)"), 0.0))
            for j in range(NCORE - 1):
                evl1 = dma("sync", q_ld, wk1[:, 0:PGI * NS],
                           dram["cc_ch_o"][j, 0].rearrange("p a b -> p (a b)"), waits=[evc])
                evl2 = dma("sync", q_ld, wk2[:, 0:PGI * NS],
                           dram["cc_ch_o"][j, 1].rearrange("p a b -> p (a b)"), waits=[evc])
                def chj(e, j=j):
                    e.tensor_scalar(out=wk2[:, 0:PGI * NS], in0=wk2[:, 0:PGI * NS],
                                    scalar1=k_chm[:, j:j + 1], scalar2=None, op0=OP.mult)
                    e.tensor_scalar(out=wk2[:, 0:PGI * NS], in0=wk2[:, 0:PGI * NS],
                                    scalar1=k_chi[:, j:j + 1], scalar2=None, op0=OP.add)
                    e.tensor_scalar(out=wk1[:, 0:PGI * NS], in0=wk1[:, 0:PGI * NS],
                                    scalar1=k_chm[:, j:j + 1], scalar2=None, op0=OP.mult)
                    e.tensor_tensor(out=hin[:].rearrange("p a b -> p (a b)"),
                                    in0=hin[:].rearrange("p a b -> p (a b)"),
                                    in1=wk2[:, 0:PGI * NS], op=OP.mult)
                    return e.tensor_add(out=hin[:].rearrange("p a b -> p (a b)"),
                                        in0=hin[:].rearrange("p a b -> p (a b)"),
                                        in1=wk1[:, 0:PGI * NS])
                ondv(chj, waits=[evl1, evl2])
            tap("hend%d" % which, hend[:, :, :], [sch.ev(e_dv), sch.ev(e_gp)])
            tap("pend%d" % which, pend[:, :, :], [sch.ev(e_ac), sch.ev(e_gp)])
            tap("cco%d" % which, dram["cc_ch_o"], [sch.ev(q_ld)])
            def lns(e):
                e.activation(out=lnh[:].rearrange("p a b -> p (a b)"),
                             in_=hin[:].rearrange("p a b -> p (a b)"), func=AF.Abs)
                return e.activation(out=sgh[:].rearrange("p a b -> p (a b)"),
                                    in_=hin[:].rearrange("p a b -> p (a b)"), func=AF.Sign)
            onac(lns, waits=[sch.ev(e_dv)])
            ondv(lambda e: e.tensor_scalar_max(out=lnh[:].rearrange("p a b -> p (a b)"),
                                               in0=lnh[:].rearrange("p a b -> p (a b)"),
                                               scalar1=1e-38), waits=[sch.ev(e_ac)])
            onac(lambda e: e.activation(out=lnh[:].rearrange("p a b -> p (a b)"),
                                        in_=lnh[:].rearrange("p a b -> p (a b)"), func=AF.Ln),
                 waits=[sch.ev(e_dv)])
            ev_lnh = sch.ev(e_ac)
            tap("hin%d" % which, hin[:, :, :], [sch.ev(e_dv)])

            sch.phase = "m_res"
            with nc.sbuf_tensor("rwsl%d" % which, [128, 8, 128], BF16) as rwsl:
                for mg in range(PGI):
                    evw = dma("sync", q_ld, rwsl[:],
                              dram["b_winT"][:, DI + mg * 128:DI + (mg + 1) * 128]
                              .rearrange("(a p) c -> p a c", p=128),
                              waits=[sch.ev(e_pe)] if mg else barrier())
                    for inch, nc0 in enumerate(range(0, TLOC, TCH)):
                        pb = p1 if inch % 2 == 0 else p2
                        def resq(e, pb=pb, nc0=nc0):
                            r = None
                            for kg in range(PGM):
                                r = e.matmul(out=pb[:, 0:TCH], lhsT=rwsl[:, kg, :],
                                             rhs=arh[:, kg, HALO + nc0:HALO + nc0 + TCH],
                                             start=(kg == 0), stop=(kg == PGM - 1))
                            return r
                        last = onpe(resq, waits=[evw])
                        eva = onac(lambda e, mg=mg, pb=pb: e.activation(
                            out=wk1[:, 0:TCH], in_=pb[:, 0:TCH], func=AF.Silu,
                            bias=k_inb[:, PGI + mg, :]), waits=[last])
                        dma("scalar", q_s, dram["s_res"][:, mg, nc0:nc0 + TCH], wk1[:, 0:TCH],
                            waits=[eva])
            ev_res = sch.ev(q_s)

            sch.phase = "m_fix"
            with nc.sbuf_tensor("ccb%d" % which, [128, NS, TCH], BF16) as ccb:
                for itc in range(NTC):
                    nc0 = itc * TCH
                    evcc = dma("gpsimd", q_g, ccb[:],
                               bass.AP(sbc.tensor, sbc.offset + NS * TLOC + nc0,
                                       [[0, 128], [TLOC, NS], [1, TCH]]),
                               waits=[ev_bc, ev_scan])
                    for pg in range(PGI):
                        po = (pg % 2) * TCH
                        if pg % 2 == 0:
                            evy2 = dma("sync", q_ld, ap3(wk1, 0, [[TCH, 2], [1, TCH]]),
                                       dram["s_y"][:, pg:pg + 2, nc0:nc0 + TCH],
                                       waits=[ev_scan])
                            evcd2 = dma("sync", q_ld, ap3(wkb2, 0, [[TCH, 2], [1, TCH]]),
                                        dram["s_cdl"][:, pg:pg + 2, nc0:nc0 + TCH],
                                        waits=[ev_scan])
                            evrs2 = dma("sync", q_ld, ap3(wk2, 0, [[TCH, 2], [1, TCH]]),
                                        dram["s_res"][:, pg:pg + 2, nc0:nc0 + TCH],
                                        waits=[ev_res])
                        evy, evcd, evrs = evy2, evcd2, evrs2
                        py = p3 if pg % 2 == 0 else p4
                        qv2 = ap3(arh, 0, [[NT, NH], [1, TCH]])
                        for h2 in range(2):
                            s0 = h2 * NH
                            dgo = h2 * 1024
                            evdg = ondv(lambda e, pg=pg, s0=s0, dgo=dgo: e.tensor_tensor(
                                out=ap3(wkb, dgo, [[128, NH], [1, 128]]),
                                in0=ap3(k_eye, 0, [[0, NH], [1, 128]]),
                                in1=ap3(sgh, pg * NS + s0, [[1, NH], [0, 128]]),
                                op=OP.mult), waits=[ev_lnh] if h2 == 0 else [])
                            def qexpq(e, pg=pg, s0=s0, h2=h2, po=po):
                                r = None
                                for si in range(NH):
                                    r = e.activation(
                                        out=sqb[:, si, :] if h2 == 0 else arh[:, si, 0:TCH],
                                        in_=wkb2[:, po:po + TCH], func=AF.Exp,
                                        scale=Apk[:, pg, s0 + si:s0 + si + 1],
                                        bias=lnh[:, pg, s0 + si:s0 + si + 1])
                                return r
                            onac(qexpq, waits=[evcd, ev_lnh] if h2 == 0 else [])
                            evu2 = ondv(lambda e, s0=s0, h2=h2, qv2=qv2: e.tensor_tensor(
                                out=sqb[:].rearrange("p a b -> p (a b)") if h2 == 0 else qv2,
                                in0=sqb[:].rearrange("p a b -> p (a b)") if h2 == 0 else qv2,
                                in1=ccb[:, s0:s0 + NH, :], op=OP.mult),
                                waits=[evcc])
                            def fredq(e, py=py, h2=h2, dgo=dgo):
                                r = None
                                for si in range(NH):
                                    r = e.matmul(out=py[:, 0:TCH],
                                                 lhsT=wkb[:, dgo + si * 128:dgo + (si + 1) * 128],
                                                 rhs=sqb[:, si, :] if h2 == 0
                                                 else arh[:, si, 0:TCH],
                                                 start=(h2 == 0 and si == 0),
                                                 stop=(h2 == 1 and si == NH - 1))
                                return r
                            onpe(fredq, waits=[evu2, evdg])
                        def gat(e, py=py, po=po):
                            e.tensor_add(out=wk1[:, po:po + TCH], in0=wk1[:, po:po + TCH],
                                         in1=py[:, 0:TCH])
                            return e.tensor_tensor(out=wkb2[:, 1024 + po:1024 + po + TCH],
                                                   in0=wk1[:, po:po + TCH],
                                                   in1=wk2[:, po:po + TCH], op=OP.mult)
                        evg = ondv(gat, waits=[evy, evrs])
                        dma("scalar", q_s, dram["s_g"][:, pg, nc0:nc0 + TCH],
                            wkb2[:, 1024 + po:1024 + po + TCH], waits=[evg])
            ev_g = sch.ev(q_s)
            tap("gated%d" % which, wkb[:, TCH:2 * TCH], [ev_g])

            sch.phase = "m_outproj"
            def obank(mb):
                return pbig[:, mb, 0:TCH] if mb < 4 else [p1, p2, p3, p4][mb - 4][:, 0:TCH]
            for inch, nc0 in enumerate(range(0, TLOC, TCH)):
                for kg in range(PGI):
                    evwo = dma("sync", q_ld, wkb2[:, (kg % 2) * 1024:(kg % 2) * 1024 + 1024],
                               dram["b_woutT"][kg * 128:(kg + 1) * 128, :],
                               waits=[ev_g] if inch == 0 and kg == 0 else [])
                    evgx = dma("sync", q_ld, sqb[:, kg % 8, 0:TCH],
                               dram["s_g"][:, kg, nc0:nc0 + TCH],
                               waits=[ev_g] if inch == 0 and kg == 0 else [])
                    def opq(e, kg=kg):
                        r = None
                        for mb in range(PGM):
                            r = e.matmul(
                                out=obank(mb),
                                lhsT=wkb2[:, (kg % 2) * 1024 + mb * 128:
                                          (kg % 2) * 1024 + mb * 128 + 128],
                                rhs=sqb[:, kg % 8, 0:TCH], start=(kg == 0),
                                stop=(kg == PGI - 1))
                        return r
                    onpe(opq, waits=[evwo, evgx])
                for mb in range(PGM):
                    evrl = None
                    if which == 1:
                        evrl = dma("sync", q_ld, wk1[:, (mb % 2) * TCH:(mb % 2) * TCH + TCH],
                                   dram["s_s1"][:, mb, HALO + nc0:HALO + nc0 + TCH],
                                   waits=[ev_s1sp])
                    def evo(e, mb=mb, nc0=nc0, which=which):
                        if which == 1:
                            return e.scalar_tensor_tensor(
                                out=sA[:, mb, HALO + nc0:HALO + nc0 + TCH], in0=obank(mb),
                                scalar=k_outb[:, mb, :],
                                in1=wk1[:, (mb % 2) * TCH:(mb % 2) * TCH + TCH],
                                op0=OP.add, op1=OP.add)
                        return e.tensor_scalar(out=sA[:, mb, HALO + nc0:HALO + nc0 + TCH],
                                               in0=obank(mb),
                                               scalar1=k_outb[:, mb, :], scalar2=None,
                                               op0=OP.add)
                    ondv(evo, waits=[evrl])
            return sch.ev(e_dv)

        with nc.sbuf_tensor("arh1", [128, PGM, NT], BF16) as arh1, \
             nc.sbuf_tensor("wsl1", [128, 8, 128], BF16) as wsl1:
            ev_m1 = emit_mamba(1, arh1, wsl1)
        for nm, buf in [("sxm", "s_xm"), ("sdelta", "s_delta"), ("scdl", "s_cdl"),
                        ("sy", "s_y"), ("sres", "s_res"), ("sg", "s_g")]:
            tap(nm, dram[buf], [ev_m1] + barrier())
        tap("x1", sA[:, :, :], [ev_m1])
        if SN < 4:
            sch.enabled = False

        sch.phase = "x1halo"
        ev1 = dma("gpsimd", q_g, dram["cc_h1_i"], sA[:, :, NT - HALO:NT], waits=[ev_m1], slow=True)
        evc = ongp(lambda e: e.collective_compute(
            "AllGather", OP.bypass, replica_groups=[list(range(NCORE))],
            ins=[dram["cc_h1_i"].opt()], outs=[dram["cc_h1_o"].opt()]), waits=[ev1])
        ondv(lambda e: e.memset(sA[:, :, 0:HALO], 0.0), waits=[ev_m1])
        dvfence()
        for j in range(NCORE):
            evl = dma("sync", q_ld, wk1[:, 0:PGM * HALO],
                      dram["cc_h1_o"][j].rearrange("p a b -> p (a b)"), waits=[evc])
            def hj(e, j=j):
                return e.scalar_tensor_tensor(
                    out=sA[:, :, 0:HALO],
                    in0=wk1[:, 0:PGM * HALO].rearrange("p (a b) -> p a b", a=PGM),
                    scalar=k_psel[:, j:j + 1], in1=sA[:, :, 0:HALO],
                    op0=OP.mult, op1=OP.add)
            ondv(hj, waits=[evl])
            dvfence()
        for pg in range(PGM):
            dma("scalar", q_s, dram["s_s1"][:, pg, :], sA[:, pg, :], waits=[])
        ev_s1sp = sch.ev(q_s)

        with nc.sbuf_tensor("arh2", [128, PGM, NT], BF16) as arh2, \
             nc.sbuf_tensor("wsl2", [128, 8, 128], BF16) as wsl2:
            ev_m2 = emit_mamba(2, arh2, wsl2)
        tap("out2", sA[:, :, HALO:NT], [ev_m2])
        if SN < 5:
            sch.enabled = False
        for pg in range(PGM):
            dma("scalar", q_s, dram["s_o2"][:, pg, :], sA[:, pg, HALO:NT], waits=[ev_m2])
        ev_o2sp = sch.ev(q_s)
        for c in reversed(stA_cm):
            c.__exit__(None, None, None)

        with nc.sbuf_tensor("Vq", [128, PGM, DM], F32) as V, \
             nc.sbuf_tensor("Vbf", [128, PGM, DM], BF16) as Vbf, \
             nc.sbuf_tensor("Vlo", [128, PGM, DM], BF16) as Vlo, \
             nc.sbuf_tensor("o2hw", [128, PGM, 264], BF16) as o2hw, \
             nc.sbuf_tensor("o2lw", [128, PGM, 264], BF16) as o2lw, \
             nc.sbuf_tensor("ginf", [128, SC], F32) as ginf:
            rowd = dram["s_row"]
            ev_gin = dma("gpsimd", q_g, ginf[:],
                         bass.AP(rowd.tensor, rowd.offset + TLOC, [[0, 128], [1, SC]]),
                         waits=ev_grow)

            ev1 = dma("gpsimd", q_g, dram["cc_wk_i"], sA[:, :, NT - 1:NT], waits=[ev_m2], slow=True)
            evc = ongp(lambda e: e.collective_compute(
                "AllGather", OP.bypass, replica_groups=[list(range(NCORE))],
                ins=[dram["cc_wk_i"].opt()], outs=[dram["cc_wk_o"].opt()]), waits=[ev1])
            evz = ondv(lambda e: e.memset(wk2[:, 0:PGM], 0.0))
            evz = dvfence()
            for j in range(NCORE):
                evl = dma("sync", q_ld, wk1[:, 0:PGM],
                          dram["cc_wk_o"][j].rearrange("p a b -> p (a b)"), waits=[evc])
                def wkj(e, j=j):
                    return e.scalar_tensor_tensor(out=wk2[:, 0:PGM], in0=wk1[:, 0:PGM],
                                                  scalar=k_psel[:, j:j + 1], in1=wk2[:, 0:PGM],
                                                  op0=OP.mult, op1=OP.add)
                ondv(wkj, waits=[evl, evz])
                dvfence()
            ondv(lambda e: e.tensor_copy(out=sA[:, :, HALO - 1:HALO],
                                         in_=wk2[:, 0:PGM].rearrange("p (a b) -> p a b", b=1)))
            ev_wkh2 = sch.ev(e_dv)

            ondv(lambda e: e.memset(V[:].rearrange("p a b -> p (a b)"), 0.0))
            ev_v0 = sch.ev(e_dv)

            atk_cm = nc.sbuf_tensor("atk", [128, 8, DM], BF16)
            atk = atk_cm.__enter__()
            w2v = ap3(wk2, 0, [[SC, PGM], [1, SC]])
            w1v = ap3(wk1, 0, [[SC, PGM], [1, SC]])
            for c in range(NSC):
                t0 = HALO + c * SC
                sch.phase = "att_win"
                def wsp1(e, t0=t0):
                    e.tensor_copy(out=o2hw[:, :, 1:SC + 1], in_=sA[:, :, t0:t0 + SC])
                    return e.tensor_copy(out=w2v, in_=sA[:, :, t0:t0 + SC])
                ondv(wsp1, waits=[ev_wkh2] if c == 0 else [])
                def wsp2(e, t0=t0):
                    e.tensor_copy(out=w1v, in_=o2hw[:, :, 1:SC + 1])
                    return e.tensor_copy(out=o2hw[:, :, 0:1], in_=sA[:, :, t0 - 1:t0])
                ondv(wsp2)
                dvfence()
                def wsp3(e, t0=t0):
                    e.tensor_sub(out=o2lw[:, :, 1:SC + 1], in0=w2v, in1=w1v)
                    return e.tensor_copy(out=ap3(k_tmp8, 0, [[1, 8], [1, 1]]),
                                         in_=o2hw[:, :, 0:1])
                ondv(wsp3)
                dvfence()
                ondv(lambda e, t0=t0: e.tensor_sub(out=o2lw[:, :, 0:1],
                                                   in0=sA[:, :, t0 - 1:t0],
                                                   in1=ap3(k_tmp8, 0, [[1, 8], [1, 1]])))
                ev_w = sch.ev(e_dv)
                sch.phase = "att_v"
                for kg in range(PGM):
                    evww = dma("sync", q_ld, wkb2[:, 0:1024],
                               dram["b_wwT"][kg * 128:(kg + 1) * 128, :],
                               waits=[ev_wprep, ev_w] if c == 0 and kg == 0 else [])
                    evwl = dma("sync", q_ld, wkb2[:, 1024:2048],
                               dram["b_wwTlo"][kg * 128:(kg + 1) * 128, :])
                    def vq(e, kg=kg):
                        r = None
                        for tg in range(2):
                            for nh in range(2):
                                for tm, (lh, ro) in enumerate([(o2hw, 0), (o2hw, 1024),
                                                               (o2lw, 0)]):
                                    r = e.matmul(
                                        out=pbig[:, 2 * tg + nh, 0:512],
                                        lhsT=lh[:, kg, 1 + tg * 128:1 + tg * 128 + 128],
                                        rhs=wkb2[:, ro + nh * 512:ro + nh * 512 + 512],
                                        start=(kg == 0 and tm == 0),
                                        stop=(kg == PGM - 1 and tm == 2))
                        return r
                    onpe(vq, waits=[evww, evwl])
                for tg in range(2):
                    def vev(e, tg=tg):
                        e.tensor_scalar(out=wk2[:, 0:1024],
                                        in0=ap3(pbig, (2 * tg) * 512, [[1, 1024]]),
                                        scalar1=k_gw[:, tg, :], scalar2=None, op0=OP.mult)
                        e.tensor_copy(out=atk[:, 4 + tg, :], in_=wk2[:, 0:1024])
                        e.tensor_copy(out=wk1[:, 0:1024], in_=atk[:, 4 + tg, :])
                        return e.tensor_sub(out=atk[:, 6 + tg, :], in0=wk2[:, 0:1024],
                                            in1=wk1[:, 0:1024])
                    ondv(vev, waits=[sch.ev(e_pe)])
                ev_vt = sch.ev(e_dv)
                sch.phase = "att_wkt"
                for tg in range(2):
                    for dq in range(2):
                        def trq(e, tg=tg, dq=dq, t0=t0):
                            r = None
                            for di in range(4):
                                dg = dq * 4 + di
                                r = e.transpose(
                                    out=p1[:, di * 128:(di + 1) * 128],
                                    in_=sA[:, dg, t0 - 1 + tg * 128:t0 - 1 + tg * 128 + 128],
                                    identity=k_eyef[:])
                            return r
                        evt = onpe(trq, waits=[ev_vt] if dq == 0 else [])
                        ondv(lambda e, dq=dq: e.tensor_copy(
                            out=wk2[:, dq * 512:(dq + 1) * 512], in_=p1[0:128, 0:512]),
                            waits=[evt])
                    def wksp(e, tg=tg):
                        e.tensor_copy(out=atk[:, 0 + tg, :], in_=wk2[:, 0:1024])
                        e.tensor_copy(out=wk1[:, 0:1024], in_=atk[:, 0 + tg, :])
                        return e.tensor_sub(out=atk[:, 2 + tg, :], in0=wk2[:, 0:1024],
                                            in1=wk1[:, 0:1024])
                    ondv(wksp)
                ev_wkt = sch.ev(e_dv)
                sch.phase = "att_rks"
                def rksf(e, t0=t0):
                    e.tensor_tensor(out=w2v, in0=sA[:, :, t0:t0 + SC],
                                    in1=ap3(ginf, 0, [[0, PGM], [1, SC]]), op=OP.mult)
                    e.tensor_copy(out=sqb[:, :, 0:SC], in_=w2v)
                    e.tensor_copy(out=w1v, in_=sqb[:, :, 0:SC])
                    return e.tensor_sub(out=sqb[:, :, SC:2 * SC], in0=w2v, in1=w1v)
                evrk = ondv(rksf, waits=[ev_gin])
                sch.phase = "att_vsp"
                def vsp(e):
                    for eb in range(PGM):
                        e.tensor_copy(out=Vbf[:, eb, :], in_=V[:, eb, :])
                        e.tensor_copy(out=wk1[:, 1024:2048], in_=Vbf[:, eb, :])
                        e.tensor_sub(out=Vlo[:, eb, :], in0=V[:, eb, :],
                                     in1=wk1[:, 1024:2048])
                    return e.tensor_copy(out=k_fence[:, 2:3], in_=k_fence[:, 3:4])
                ondv(vsp, waits=[ev_v0] if c == 0 else [])
                evvb = sch.ev(e_dv)
                sch.phase = "att_s"
                for ub in range(2):
                    pS = p3 if ub == 0 else p4
                    def sq_(e, ub=ub, pS=pS):
                        r = None
                        for tm, (lh, rh) in enumerate([(o2hw, o2hw), (o2hw, o2lw),
                                                       (o2lw, o2hw)]):
                            for kg in range(PGM):
                                r = e.matmul(
                                    out=pS[:, 0:SC],
                                    lhsT=lh[:, kg, ub * 128:ub * 128 + 128],
                                    rhs=rh[:, kg, 1:1 + SC],
                                    start=(tm == 0 and kg == 0),
                                    stop=(tm == 2 and kg == PGM - 1))
                        return r
                    onpe(sq_, waits=[ev_w])
                    ondv(lambda e, ub=ub, pS=pS: e.scalar_tensor_tensor(
                        out=wk2[:, ub * SC:(ub + 1) * SC], in0=pS[:, 0:SC],
                        scalar=k_gwi[:, ub, :], in1=k_MT[:, ub, :], op0=OP.mult, op1=OP.mult),
                        waits=[sch.ev(e_pe)])
                def smsp(e):
                    e.tensor_copy(out=wkb[:, 0:2 * SC], in_=wk2[:, 0:2 * SC])
                    e.tensor_copy(out=wk1[:, 0:2 * SC], in_=wkb[:, 0:2 * SC])
                    return e.tensor_sub(out=wkb[:, 2 * SC:4 * SC], in0=wk2[:, 0:2 * SC],
                                        in1=wk1[:, 0:2 * SC])
                ondv(smsp)
                ev_sm = sch.ev(e_dv)
                sch.phase = "att_reads"
                for mb in range(PGM):
                    def rdq(e, mb=mb):
                        first = True
                        r = None
                        for kg in range(PGM):
                            for lhs, r0 in [(Vbf, 0), (Vbf, SC), (Vlo, 0)]:
                                r = e.matmul(
                                    out=ap3(pbig, mb * SC, [[1, SC]]),
                                    lhsT=lhs[:, kg, mb * 128:(mb + 1) * 128],
                                    rhs=sqb[:, kg, r0:r0 + SC], start=first, stop=False)
                                first = False
                        for ub in range(2):
                            for tm, (ti, wo) in enumerate([(4, 0), (4, 2 * SC), (6, 0)]):
                                r = e.matmul(
                                    out=ap3(pbig, mb * SC, [[1, SC]]),
                                    lhsT=atk[:, ti + ub, mb * 128:(mb + 1) * 128],
                                    rhs=wkb[:, wo + ub * SC:wo + ub * SC + SC],
                                    start=False, stop=(ub == 1 and tm == 2))
                        return r
                    onpe(rdq, waits=[evvb, evrk, ev_vt, ev_sm])
                evrd = ondv(lambda e: e.tensor_copy(
                    out=wk1[:, 0:PGM * SC],
                    in_=ap3(pbig, 0, [[1, PGM * SC]])), waits=[sch.ev(e_pe)])
                dma("scalar", q_s, dram["s_reads"][:, :, c * SC:(c + 1) * SC],
                    w1v, waits=[evrd])
                sch.phase = "att_vup"
                for eb in range(PGM):
                    def vupq(e, eb=eb):
                        r = None
                        for nh0 in range(2):
                            for ukg in range(2):
                                for tm, (li, ri) in enumerate([(0, 4), (0, 6), (2, 4)]):
                                    r = e.matmul(
                                        out=pbig[:, 2 * (eb % 2) + nh0, 0:512],
                                        lhsT=atk[:, li + ukg, eb * 128:(eb + 1) * 128],
                                        rhs=atk[:, ri + ukg, nh0 * 512:(nh0 + 1) * 512],
                                        start=(ukg == 0 and tm == 0),
                                        stop=(ukg == 1 and tm == 2))
                        return r
                    last = onpe(vupq, waits=[ev_wkt, evrd])
                    def vupd(e, eb=eb):
                        r = None
                        for nh0 in range(2):
                            r = e.scalar_tensor_tensor(
                                out=V[:, eb, nh0 * 512:(nh0 + 1) * 512],
                                in0=V[:, eb, nh0 * 512:(nh0 + 1) * 512], scalar=k_g256[:, 0:1],
                                in1=pbig[:, 2 * (eb % 2) + nh0, 0:512], op0=OP.mult, op1=OP.add)
                        return r
                    ondv(vupd, waits=[last])
                if c == 0:
                    tap("V0", V[:, :, :], [sch.ev(e_dv)] + barrier())
                if c in (5, 6):
                    tap("sq%d" % c, sqb[:, :, :], barrier())
                    tap("smw%d" % c, wkb[:, 0:4 * SC], barrier())
                    tap("atk%d" % c, atk[:, :, :], barrier())
                    tap("vb%d" % c, Vbf[:, :, :], barrier())
                    tap("vl%d" % c, Vlo[:, :, :], barrier())
                    tap("whi%d" % c, o2hw[:, :, :], barrier())
                    tap("wlo%d" % c, o2lw[:, :, :], barrier())
            ev_att = sch.ev(e_dv)
            tap("ginf", ginf[:, :], [ev_gin] + barrier())
            tap("srow", dram["s_row"], barrier())
            tap("g256", k_g256[:, :], barrier())
            tap("kgw", k_gw[:, :, :], barrier())
            tap("kmisc", k_misc[:, 0:80], barrier())
            tap("MT", k_MT[:, :, :], barrier())
            tap("sreads", dram["s_reads"], [ev_att] + barrier())
            tap("Vagg", V[:, :, :], [ev_att] + barrier())

            sch.phase = "att_agg"
            evvb = ondv(lambda e: e.tensor_copy(out=Vbf[:].rearrange("p a b -> p (a b)"),
                                                in_=V[:].rearrange("p a b -> p (a b)")),
                        waits=[ev_att])
            ev1 = dma("gpsimd", q_g, dram["cc_ag_i"], Vbf[:], waits=[evvb])
            evc = ongp(lambda e: e.collective_compute(
                "AllGather", OP.bypass, replica_groups=[list(range(NCORE))],
                ins=[dram["cc_ag_i"].opt()], outs=[dram["cc_ag_o"].opt()]), waits=[ev1])
            carry = V
            ondv(lambda e: e.memset(carry[:].rearrange("p a b -> p (a b)"), 0.0), waits=[evvb])
            ondv(lambda e: e.tensor_scalar(out=wk2[:, 64:64 + NCORE - 1],
                                           in0=k_chm[:, 0:NCORE - 1],
                                           scalar1=k_g2048[:, 0:1], scalar2=None, op0=OP.mult))
            dvfence()
            ondv(lambda e: e.tensor_add(out=wk2[:, 64:64 + NCORE - 1],
                                        in0=wk2[:, 64:64 + NCORE - 1],
                                        in1=k_chi[:, 0:NCORE - 1]))
            dvfence()
            for j in range(NCORE - 1):
                for pg in range(PGM):
                    evl = dma("sync", q_ld, wkb[:, 0:1024], dram["cc_ag_o"][j, :, pg, :],
                              waits=[evc])
                    def cj(e, j=j, pg=pg):
                        e.tensor_scalar(out=carry[:, pg, :], in0=carry[:, pg, :],
                                        scalar1=wk2[:, 64 + j:65 + j], scalar2=None, op0=OP.mult)
                        e.tensor_scalar(out=wk1[:, 0:1024], in0=wkb[:, 0:1024],
                                        scalar1=k_chm[:, j:j + 1], scalar2=None, op0=OP.mult)
                        return e.tensor_add(out=carry[:, pg, :], in0=carry[:, pg, :],
                                            in1=wk1[:, 0:1024])
                    ondv(cj, waits=[evl])
            evcar = ondv(lambda e: e.tensor_copy(out=Vbf[:].rearrange("p a b -> p (a b)"),
                                                 in_=carry[:].rearrange("p a b -> p (a b)")))

            sch.phase = "att_corr"
            atk_cm.__exit__(None, None, None)
            CCH = 256
            rdch_cm = nc.sbuf_tensor("rdch", [128, PGM, CCH], F32)
            rdch = rdch_cm.__enter__()
            rdlo_cm = nc.sbuf_tensor("rdlo", [128, PGM, CCH], BF16)
            rdlo = rdlo_cm.__enter__()
            gtauf_cm = nc.sbuf_tensor("gtauf", [128, CCH], F32)
            gtauf = gtauf_cm.__enter__()
            w2c = ap3(wk2, 0, [[CCH, PGM], [1, CCH]])
            for inch, nc0 in enumerate(range(0, TLOC, CCH)):
                ev_gtau = dma("gpsimd", q_g, gtauf[:],
                              bass.AP(rowd.tensor, rowd.offset + nc0, [[0, 128], [1, CCH]]),
                              waits=ev_grow + [sch.ev(e_dv)])
                evsc = ondv(lambda e, nc0=nc0: e.tensor_tensor(
                    out=sqb[:, :, 0:CCH], in0=sA[:, :, HALO + nc0:HALO + nc0 + CCH],
                    in1=ap3(gtauf, 0, [[0, PGM], [1, CCH]]), op=OP.mult),
                    waits=[evcar, ev_gtau])
                evread = dma("sync", q_ld, rdch[:],
                             dram["s_reads"][:, :, nc0:nc0 + CCH],
                             waits=[sch.ev(e_dv), sch.ev(e_pe)])
                for mb in range(PGM):
                    def crq(e, mb=mb):
                        r = None
                        for kg in range(PGM):
                            r = e.matmul(
                                out=pbig[:, mb % 4, 0:CCH],
                                lhsT=Vbf[:, kg, mb * 128:(mb + 1) * 128],
                                rhs=sqb[:, kg, 0:CCH], start=(kg == 0), stop=(kg == PGM - 1))
                        return r
                    last = onpe(crq, waits=[evsc])
                    ondv(lambda e, mb=mb: e.tensor_add(
                        out=rdch[:, mb, :], in0=rdch[:, mb, :], in1=pbig[:, mb % 4, 0:CCH]),
                        waits=[last, evread])
                def rdsp(e):
                    e.tensor_copy(out=sqb[:, :, CCH:2 * CCH],
                                  in_=rdch[:].rearrange("p a b -> p (a b)")
                                  if False else rdch[:, :, :])
                    e.tensor_copy(out=w2c, in_=sqb[:, :, CCH:2 * CCH])
                    return e.tensor_sub(out=rdlo[:, :, :], in0=rdch[:, :, :], in1=w2c)
                ondv(rdsp)
                ev_rt = sch.ev(e_dv)
                def bank(mb):
                    return pbig[:, mb, 0:CCH] if mb < 4 else [p1, p2, p3, p4][mb - 4][:, 0:CCH]
                last = None
                for kg in range(PGM):
                    evwr = dma("sync", q_ld, wkb2[:, 0:1024],
                               dram["b_wrT"][kg * 128:(kg + 1) * 128, :],
                               waits=[ev_rt] if kg == 0 else [])
                    evwrl = dma("sync", q_ld, wkb2[:, 1024:2048],
                                dram["b_wrTlo"][kg * 128:(kg + 1) * 128, :])
                    def wrq(e, kg=kg):
                        r = None
                        for mb in range(PGM):
                            for tm, (wo, rsrc) in enumerate([(0, 0), (0, 1), (1024, 0)]):
                                r = e.matmul(
                                    out=bank(mb), lhsT=wkb2[:, wo + mb * 128:wo + mb * 128 + 128],
                                    rhs=sqb[:, kg, CCH:2 * CCH] if rsrc == 0 else rdlo[:, kg, :],
                                    start=(kg == 0 and tm == 0),
                                    stop=(kg == PGM - 1 and tm == 2))
                        return r
                    last = onpe(wrq, waits=[evwr, evwrl])
                for mb in range(PGM):
                    evx1 = dma("sync", q_ld, wk1[:, 0:CCH],
                               dram["s_s1"][:, mb, HALO + nc0:HALO + nc0 + CCH], waits=[ev_s1sp])
                    evxf = dma("sync", q_ld, wk2[:, 0:CCH], dram["s_xfm"][:, mb, nc0:nc0 + CCH])
                    evo2 = dma("sync", q_ld, wk2[:, CCH:2 * CCH],
                               dram["s_o2"][:, mb, nc0:nc0 + CCH], waits=[ev_o2sp])
                    def fin(e, mb=mb, nc0=nc0):
                        e.scalar_tensor_tensor(out=sA[:, mb, HALO + nc0:HALO + nc0 + CCH],
                                               in0=bank(mb), scalar=ALPHA,
                                               in1=wk2[:, CCH:2 * CCH], op0=OP.mult, op1=OP.add)
                        e.tensor_add(out=sA[:, mb, HALO + nc0:HALO + nc0 + CCH],
                                     in0=sA[:, mb, HALO + nc0:HALO + nc0 + CCH],
                                     in1=wk1[:, 0:CCH])
                        return e.tensor_add(out=sA[:, mb, HALO + nc0:HALO + nc0 + CCH],
                                            in0=sA[:, mb, HALO + nc0:HALO + nc0 + CCH],
                                            in1=wk2[:, 0:CCH])
                    ondv(fin, waits=[last, evx1, evxf, evo2])
            gtauf_cm.__exit__(None, None, None)
            rdlo_cm.__exit__(None, None, None)
            rdch_cm.__exit__(None, None, None)
        ev_xa = sch.ev(e_dv)
        tap("xafter", sA[:, :, HALO:NT], [ev_xa])
        if SN < 6:
            sch.enabled = False

        for pg in range(PGM):
            dma("scalar", q_s, dram["s_s1"][:, pg, :], sA[:, pg, :], waits=[ev_xa])
        ev_xasp = sch.ev(q_s)

        stB_cm = [nc.sbuf_tensor("stB%d" % i, [128, NT], F32) for i in range(3)]
        st_box["mu"], st_box["ri"], st_box["t"] = [c.__enter__() for c in stB_cm]
        with nc.sbuf_tensor("hib", [128, PGM, NT], BF16) as hib, \
             nc.sbuf_tensor("gts2", [72, TLOC], F32) as gts2, \
             nc.sbuf_tensor("gtok", [128, 16, 80], F32) as gtok, \
             nc.sbuf_tensor("gg", [128, 16, 80], F32) as gg, \
             nc.sbuf_tensor("gA", [128, 16, 80], F32) as gA, \
             nc.sbuf_tensor("ohT", [64, TLOC], BF16) as ohT, \
             nc.sbuf_tensor("compb", [2, TLOC], BF16) as compb, \
             nc.sbuf_tensor("cw1b", [128, PGM, 256], BF16) as cw1b:
            ev_ln1 = emit_ln(sA, sA, None, False, NT, cast_bf=hib)
            xf = sA
            hi = hib

            sch.phase = "ffn_scores"
            for inch, nc0 in enumerate(range(0, TLOC, TCH)):
                last1 = last2 = None
                for kg in range(PGM):
                    def lof(e, kg=kg, nc0=nc0):
                        e.tensor_copy(out=wk1[:, 0:TCH], in_=hi[:, kg, HALO + nc0:HALO + nc0 + TCH])
                        e.tensor_sub(out=wk1[:, 0:TCH], in0=xf[:, kg, HALO + nc0:HALO + nc0 + TCH],
                                     in1=wk1[:, 0:TCH])
                        return e.tensor_copy(out=wkb2[:, 0:TCH], in_=wk1[:, 0:TCH])
                    evlo = ondv(lof, waits=[ev_ln1] if inch == 0 and kg == 0 else [])
                    def scq(e, kg=kg, nc0=nc0):
                        e.matmul(out=p1[0:64, 0:TCH], lhsT=k_qT[:, kg, :],
                                 rhs=hi[:, kg, HALO + nc0:HALO + nc0 + TCH],
                                 start=(kg == 0), stop=False)
                        e.matmul(out=p1[0:64, 0:TCH], lhsT=k_qT[:, kg, :], rhs=wkb2[:, 0:TCH],
                                 start=False, stop=(kg == PGM - 1))
                        e.matmul(out=p2[0:8, 0:TCH], lhsT=k_csT[:, kg, :],
                                 rhs=hi[:, kg, HALO + nc0:HALO + nc0 + TCH],
                                 start=(kg == 0), stop=False)
                        return e.matmul(out=p2[0:8, 0:TCH], lhsT=k_csT[:, kg, :],
                                        rhs=wkb2[:, 0:TCH], start=False,
                                        stop=(kg == PGM - 1))
                    last1 = last2 = onpe(scq, waits=[evlo])
                def sev(e, nc0=nc0):
                    e.tensor_copy(out=gts2[0:64, nc0:nc0 + TCH], in_=p1[0:64, 0:TCH])
                    return e.tensor_copy(out=gts2[64:72, nc0:nc0 + TCH], in_=p2[0:8, 0:TCH])
                ondv(sev, waits=[last1, last2])
            ev_sc = sch.ev(e_dv)
            tap("scores", gts2[0:72, 0:TLOC], [ev_sc])

            sch.phase = "ffn_gtok"
            for tg in range(16):
                evt = onpe(lambda e, tg=tg: e.transpose(out=p3[:, 0:128],
                                                        in_=gts2[0:72, tg * 128:(tg + 1) * 128],
                                                        identity=k_eyef[0:72, :]), waits=[ev_sc])
                ondv(lambda e, tg=tg: e.tensor_copy(out=gtok[:, tg, 0:72], in_=p3[0:128, 0:72]),
                     waits=[evt])
            ev_gt = sch.ev(e_dv)

            sch.phase = "ffn_calib"
            ondv(lambda e: e.tensor_scalar(out=gtok[:].rearrange("p a b -> p (a b)"),
                                           in0=gtok[:].rearrange("p a b -> p (a b)"),
                                           scalar1=k_misc[:, 64:65], scalar2=None, op0=OP.mult),
                 waits=[ev_gt])
            onac(lambda e: e.activation(out=gtok[:].rearrange("p a b -> p (a b)"),
                                        in_=gtok[:].rearrange("p a b -> p (a b)"),
                                        func=AF.Sigmoid), waits=[sch.ev(e_dv)])
            ondv(lambda e: e.tensor_scalar_mul(out=gtok[:].rearrange("p a b -> p (a b)"),
                                               in0=gtok[:].rearrange("p a b -> p (a b)"),
                                               scalar1=7.0), waits=[sch.ev(e_ac)])
            def tele(e):
                v = gtok[:].rearrange("p a b -> p (a b)")
                a = gA[:].rearrange("p a b -> p (a b)")
                g = gg[:].rearrange("p a b -> p (a b)")
                e.tensor_scalar(out=a, in0=v, scalar1=0.0, scalar2=k_kA[:, 0:1],
                                op0=OP.mult, op1=OP.add)
                e.tensor_scalar(out=g, in0=v, scalar1=0.0, scalar2=k_kB[:, 0:1],
                                op0=OP.mult, op1=OP.add)
                for k in range(1, 7):
                    e.tensor_single_scalar(out=wk1[:, 0:1280], in_=v, scalar=float(k), op=OP.is_ge)
                    e.scalar_tensor_tensor(out=a, in0=wk1[:, 0:1280], scalar=k_kA[:, k:k + 1],
                                           in1=a, op0=OP.mult, op1=OP.add)
                    e.scalar_tensor_tensor(out=g, in0=wk1[:, 0:1280], scalar=k_kB[:, k:k + 1],
                                           in1=g, op0=OP.mult, op1=OP.add)
                e.tensor_tensor(out=g, in0=g, in1=v, op=OP.mult)
                return e.tensor_add(out=g, in0=g, in1=a)
            ondv(tele, waits=[sch.ev(e_dv)])
            ev_gates = sch.ev(e_dv)
            tap("gates", gg[:, :, 0:72], [ev_gates])

            ondv(lambda e: e.tensor_reduce(out=wk1[:, 0:16], in_=gg[:, :, 64:72], axis=AX.X,
                                           op=OP.max), waits=[ev_gates])
            dvfence()
            def amax(e):
                mb_ = ap3(wk1, 0, [[1, 16], [0, 8]])
                return e.tensor_tensor(out=gA[:, :, 64:72], in0=gg[:, :, 64:72], in1=mb_,
                                       op=OP.is_ge)
            ondv(amax)
            dvfence()
            ondv(lambda e: e.tensor_tensor(out=gA[:, :, 64:72], in0=gA[:, :, 64:72],
                                           in1=ap3(k_iom, 0, [[0, 16], [1, 8]]), op=OP.mult))
            dvfence()
            ondv(lambda e: e.tensor_reduce(out=wk1[:, 16:32], in_=gA[:, :, 64:72], axis=AX.X,
                                           op=OP.min))
            dvfence()
            ondv(lambda e: e.tensor_scalar_add(out=wk1[:, 16:32], in0=wk1[:, 16:32],
                                               scalar1=999.0))
            dvfence()
            ondv(lambda e: e.memset(gA[:, :, 0:8], 0.0))
            for cc in range(8):
                def selcc(e, cc=cc):
                    return e.tensor_single_scalar(out=wk1[:, 32:48], in_=wk1[:, 16:32],
                                                  scalar=float(cc), op=OP.is_equal)
                ondv(selcc)
                dvfence()
                def selm(e, cc=cc):
                    ohb = ap3(wk1, 32, [[1, 16], [0, 8]])
                    return e.tensor_tensor(out=gg[:, :, 72:80], in0=gg[:, :, cc * 8:(cc + 1) * 8],
                                           in1=ohb, op=OP.mult)
                ondv(selm)
                dvfence()
                ondv(lambda e: e.tensor_add(out=gA[:, :, 0:8], in0=gA[:, :, 0:8],
                                            in1=gg[:, :, 72:80]))
                dvfence()
            ondv(lambda e: e.tensor_reduce(out=wk1[:, 48:64], in_=gA[:, :, 0:8], axis=AX.X,
                                           op=OP.max))
            dvfence()
            def selg(e):
                mb_ = ap3(wk1, 48, [[1, 16], [0, 8]])
                return e.tensor_tensor(out=gA[:, :, 8:16], in0=gA[:, :, 0:8], in1=mb_,
                                       op=OP.is_ge)
            ondv(selg)
            dvfence()
            ondv(lambda e: e.tensor_tensor(out=gA[:, :, 8:16], in0=gA[:, :, 8:16],
                                           in1=ap3(k_iom, 0, [[0, 16], [1, 8]]), op=OP.mult))
            dvfence()
            ondv(lambda e: e.tensor_reduce(out=wk1[:, 64:80], in_=gA[:, :, 8:16], axis=AX.X,
                                           op=OP.min))
            dvfence()
            ondv(lambda e: e.tensor_scalar(out=wk1[:, 80:96], in0=wk1[:, 16:32], scalar1=8.0,
                                           scalar2=None, op0=OP.mult))
            dvfence()
            ondv(lambda e: e.tensor_add(out=wk1[:, 80:96], in0=wk1[:, 80:96], in1=wk1[:, 64:80]))
            dvfence()
            ondv(lambda e: e.tensor_scalar_add(out=wk1[:, 80:96], in0=wk1[:, 80:96],
                                               scalar1=999.0))
            dvfence()
            ev_ti = sch.ev(e_dv)

            def ohf(e):
                tib = ap3(wk1, 80, [[1, 16], [0, 64]])
                iob = ap3(k_io64, 0, [[0, 16], [1, 64]])
                return e.tensor_tensor(out=ap3(wk2, 0, [[64, 16], [1, 64]]), in0=iob, in1=tib,
                                       op=OP.is_equal)
            ondv(ohf, waits=[ev_ti])
            ev_oh = sch.ev(e_dv)
            for tg in range(16):
                evt = onpe(lambda e, tg=tg: e.transpose(out=p3[0:64, 0:128],
                                                        in_=ap3(wk2, tg * 64, [[1, 64]]),
                                                        identity=k_eyef[:]), waits=[ev_oh])
                ondv(lambda e, tg=tg: e.tensor_copy(out=ohT[0:64, tg * 128:(tg + 1) * 128],
                                                    in_=p3[0:64, 0:128]), waits=[evt])
            ev_ohT = sch.ev(e_dv)
            tap("ohT", ohT[0:64, 0:TLOC], [ev_ohT])

            sch.phase = "ffn_comp"
            for kg in range(PGM):
                dma("sync", q_ld, cw1b[:, kg, :], dram["b_cw1"][kg * 128:(kg + 1) * 128, :],
                    waits=[ev_wprep])
            ev_cw1 = sch.ev(q_ld)
            ev_cw1b = sch.ev(q_l2)
            ev_cw1c = sch.ev(q_l3)
            ev_cw1d = sch.ev(q_l4)
            for inch, nc0 in enumerate(range(0, TLOC, TCH)):
                def hgq(e, nc0=nc0):
                    r = None
                    for mg in range(2):
                        for kg in range(PGM):
                            r = e.matmul(
                                out=(p1 if mg == 0 else p2)[:, 0:TCH],
                                lhsT=cw1b[:, kg, mg * 128:(mg + 1) * 128],
                                rhs=hi[:, kg, HALO + nc0:HALO + nc0 + TCH],
                                start=(kg == 0), stop=(kg == PGM - 1))
                    return r
                last = onpe(hgq, waits=[ev_cw1, ev_cw1b, ev_cw1c, ev_cw1d, ev_ln1])
                def hga(e, nc0=nc0):
                    e.activation(out=sqb[:, 0, 0:TCH], in_=p1[:, 0:TCH],
                                 func=AF.Gelu, bias=k_cb1[:, 0, :])
                    return e.activation(out=sqb[:, 1, 0:TCH], in_=p2[:, 0:TCH],
                                        func=AF.Gelu, bias=k_cb1[:, 1, :])
                onac(hga, waits=[last])
                last = None
                for kg in range(2):
                    last = onpe(lambda e, kg=kg: e.matmul(
                        out=p3[0:2, 0:TCH], lhsT=k_cw2[:, kg, :], rhs=sqb[:, kg, 0:TCH],
                        start=(kg == 0), stop=(kg == 1)), waits=[sch.ev(e_ac)])
                onac(lambda e, nc0=nc0: e.activation(out=compb[0:2, nc0:nc0 + TCH],
                                                     in_=p3[0:2, 0:TCH], func=AF.Tanh,
                                                     bias=k_cb2[0:2, 0:1]), waits=[last])
                ev_cp = sch.ev(e_ac)
                for mg in range(PGM):
                    evh = onpe(lambda e, mg=mg, nc0=nc0: e.matmul(
                        out=p4[:, 0:TCH], lhsT=k_w1a[0:2, mg * 128:(mg + 1) * 128],
                        rhs=compb[0:2, nc0:nc0 + TCH], start=True, stop=True), waits=[ev_cp])
                    evr = onac(lambda e, mg=mg: e.activation(out=wk1[:, 0:TCH], in_=p4[:, 0:TCH],
                                                             func=AF.Relu, bias=k_b1f[:, mg, :]),
                               waits=[evh])
                    evw2 = ondv(lambda e, mg=mg: e.tensor_scalar(
                        out=wkb2[:, 0:TCH], in0=wk1[:, 0:TCH], scalar1=k_w2f[:, mg, :],
                        scalar2=None, op0=OP.mult), waits=[evr])
                    onpe(lambda e, mg=mg: e.matmul(out=p3[0:64, 0:TCH], lhsT=k_blk8[:, mg, :],
                                                   rhs=wkb2[:, 0:TCH], start=(mg == 0),
                                                   stop=(mg == PGM - 1)), waits=[evw2])
                def magf(e, nc0=nc0):
                    e.tensor_scalar(out=wk1[0:64, 0:TCH], in0=p3[0:64, 0:TCH],
                                    scalar1=k_b2p[0:64, 0:1], scalar2=None, op0=OP.add)
                    return e.tensor_tensor(out=wkb[0:64, 0:TCH], in0=wk1[0:64, 0:TCH],
                                           in1=ohT[0:64, nc0:nc0 + TCH], op=OP.mult)
                evm = ondv(magf, waits=[sch.ev(e_pe), ev_ohT])
                evms = onpe(lambda e: e.matmul(out=p4[:, 0:TCH], lhsT=k_ones[0:64, :],
                                               rhs=wkb[0:64, 0:TCH], start=True, stop=True),
                            waits=[evm])
                evmsel = ondv(lambda e, nc0=nc0: e.tensor_copy(out=st_box["ri"][:, nc0:nc0 + TCH],
                                                                in_=p4[:, 0:TCH]), waits=[evms])
                for mq in range(2):
                    def dsq(e, mq=mq, nc0=nc0):
                        r = None
                        for mi in range(4):
                            mb = mq * 4 + mi
                            r = e.matmul(out=pbig[:, mi, 0:TCH],
                                         lhsT=k_dirs[0:64, mb * 128:(mb + 1) * 128],
                                         rhs=ohT[0:64, nc0:nc0 + TCH], start=True, stop=True)
                        return r
                    last = onpe(dsq, waits=[ev_ohT])
                    def ffin(e, mq=mq, nc0=nc0):
                        r = None
                        for mi in range(4):
                            mb = mq * 4 + mi
                            e.tensor_scalar(out=wk1[:, 0:TCH], in0=pbig[:, mi, 0:TCH],
                                            scalar1=k_osc[:, 0:1], scalar2=None, op0=OP.mult)
                            e.tensor_mul(out=wk1[:, 0:TCH], in0=wk1[:, 0:TCH],
                                         in1=st_box["ri"][:, nc0:nc0 + TCH])
                            r = e.tensor_add(out=xf[:, mb, HALO + nc0:HALO + nc0 + TCH],
                                             in0=xf[:, mb, HALO + nc0:HALO + nc0 + TCH],
                                             in1=wk1[:, 0:TCH])
                        return r
                    ondv(ffin, waits=[last, evmsel])
            ev_y = sch.ev(e_dv)
            tap("ffny", sA[:, :, HALO:NT], [ev_y])

            sch.phase = "ffn_final"
            ev_lnf = emit_ln(sA, sA, None, False, NT, gb=k_gam, bb=k_bet)
            for inch, nc0 in enumerate(range(0, TLOC, TCH)):
                for mb in range(PGM):
                    evr = dma("sync", q_ld, wk1[:, 0:TCH],
                              dram["s_s1"][:, mb, HALO + nc0:HALO + nc0 + TCH], waits=[ev_xasp])
                    ondv(lambda e, mb=mb, nc0=nc0: e.tensor_add(
                        out=sA[:, mb, HALO + nc0:HALO + nc0 + TCH],
                        in0=sA[:, mb, HALO + nc0:HALO + nc0 + TCH], in1=wk1[:, 0:TCH]),
                        waits=[evr, ev_lnf])
        for c in reversed(stB_cm):
            c.__exit__(None, None, None)
        ev_fin = sch.ev(e_dv)

        sch.phase = "out_store"
        sch.enabled = True
        bar_out = barrier()
        nst = 0
        for tg in range(16):
            evv = None
            xo = (tg % 2) * DM
            for dq in range(2):
                def otrq(e, tg=tg, dq=dq):
                    r = None
                    for di in range(4):
                        dg = dq * 4 + di
                        r = e.transpose(
                            out=p1[:, di * 128:(di + 1) * 128],
                            in_=sA[:, dg, HALO + tg * 128:HALO + (tg + 1) * 128],
                            identity=k_eyef[:])
                    return r
                evt = onpe(otrq, waits=bar_out if tg == 0 and dq == 0 else [])
                evv = ondv(lambda e, dq=dq, xo=xo: e.tensor_copy(
                    out=wk2[:, xo + dq * 512:xo + (dq + 1) * 512],
                    in_=p1[0:128, 0:512]), waits=[evt])
            dma("gpsimd", q_st, dram["out"][tg * 128:(tg + 1) * 128, :],
                wk2[:, xo:xo + DM], waits=[evv])
            nst += 16
        sch.emit("gpsimd", lambda e: e.engine_nop(), waits=[(q_st, nst)])

        sch.build(blockctx)
    return nc, dram


def _pack_inputs(inputs):
    from ml_dtypes import bfloat16
    x = np.asarray(inputs["L0_attn_x"], np.float32).reshape(NCORE * TLOC, DM)
    eye = np.eye(128, dtype=np.float32)
    packP = lambda a, pgs: np.ascontiguousarray(
        np.asarray(a, np.float32).reshape(pgs, 128, -1).transpose(1, 0, 2))
    rep = lambda v: np.full((128, 1), np.float32(v), np.float32)
    blk8 = np.zeros((128, 8, 64), np.float32)
    for p in range(128):
        for pg in range(8):
            blk8[p, pg, (pg * 128 + p) // 16] = 1.0
    base = {
        "w_inT": np.ascontiguousarray(np.asarray(inputs["L0_attn_in_proj_w"], np.float32).T),
        "w_outT": np.ascontiguousarray(np.asarray(inputs["L0_attn_out_proj_w"], np.float32).T),
        "xp_wT": np.ascontiguousarray(np.asarray(inputs["L0_attn_x_proj_w"], np.float32).T),
        "dt_wT": np.ascontiguousarray(np.asarray(inputs["L0_attn_dt_proj_w"], np.float32).T),
        "ww_T": np.ascontiguousarray(np.asarray(inputs["L0_attn_proj_write_w"], np.float32).T),
        "wr_T": np.ascontiguousarray(np.asarray(inputs["L0_attn_proj_read_w"], np.float32).T),
        "in_b": packP(inputs["L0_attn_in_proj_b"], 32),
        "out_b": packP(inputs["L0_attn_out_proj_b"], 8),
        "conv_w": packP(np.asarray(inputs["L0_attn_conv_w"], np.float32).reshape(DI, 4), 16),
        "conv_b": packP(inputs["L0_attn_conv_b"], 16),
        "dt_b": packP(inputs["L0_attn_dt_proj_b"], 16),
        "A_log": packP(inputs["L0_attn_A_log"], 16),
        "D_ssm": packP(inputs["L0_attn_D_ssm"], 16),
        "n1w": packP(inputs["L0_attn_norm1_w"], 8),
        "n2w": packP(inputs["L0_attn_norm2_w"], 8),
        "decay_r": rep(np.asarray(inputs["L0_attn_decay"], np.float32)),
        "sig_T": np.ascontiguousarray(np.asarray(inputs["L0_ffn_signatures_raw"], np.float32).T),
        "knots_r": np.ascontiguousarray(np.broadcast_to(
            np.asarray(inputs["L0_ffn_knot_values"], np.float32), (128, 8))),
        "temp_r": rep(np.asarray(inputs["L0_ffn_temperature"], np.float32)[0]),
        "cw1": np.asarray(inputs["L0_ffn_compress_w1"], np.float32),
        "cb1": packP(inputs["L0_ffn_compress_b1"], 2),
        "cw2": np.asarray(inputs["L0_ffn_compress_w2"], np.float32),
        "w1a": np.ascontiguousarray(np.asarray(inputs["L0_ffn_spline_w1"], np.float32)
                                    .transpose(1, 0, 2).reshape(2, 1024)),
        "b1f": packP(np.asarray(inputs["L0_ffn_spline_b1"], np.float32).reshape(-1), 8),
        "w2f": packP(np.asarray(inputs["L0_ffn_spline_w2"], np.float32).reshape(-1), 8),
        "b2p": np.asarray(inputs["L0_ffn_spline_b2"], np.float32).reshape(64, 1),
        "dirs": np.asarray(inputs["L0_ffn_directions"], np.float32),
        "gam": packP(inputs["L0_ffn_norm_gamma"], 8),
        "bet": packP(inputs["L0_ffn_norm_beta"], 8),
        "oscale_r": rep(np.asarray(inputs["L0_ffn_output_scale"], np.float32)[0]),
        "eye_bf": eye.astype(bfloat16),
        "eye_f32": eye,
        "ones_bf": np.ones((128, 128), np.float32).astype(bfloat16),
        "iota64": np.ascontiguousarray(np.broadcast_to(np.arange(64, dtype=np.float32), (128, 64))),
        "iotam": np.ascontiguousarray(np.broadcast_to(
            np.arange(8, dtype=np.float32) - 999.0, (128, 8))),
        "blk8": blk8.astype(bfloat16),
        "iota256": np.arange(SC, dtype=np.float32)[None, :],
        "iotaT": np.arange(TLOC, dtype=np.float32)[None, :],
        "consts8": np.ascontiguousarray(np.broadcast_to(np.array(
            [1e-5, 1.0, -1.0, -0.5, 256.0, 2048.0, 0.5, 0.0], np.float32), (128, 8))),
    }
    cb2 = np.zeros((128, 1), np.float32)
    cb2[0:2, 0] = np.asarray(inputs["L0_ffn_compress_b2"], np.float32)
    base["cb2"] = cb2
    maps = []
    for i in range(NCORE):
        m = dict(base)
        sl = np.zeros((NT, DM), np.float32)
        lo = i * TLOC - HALO
        s0 = max(lo, 0)
        sl[s0 - lo:, :] = x[s0:(i + 1) * TLOC, :]
        m["x_slab"] = sl
        m["c0mask"] = rep(0.0 if i == 0 else 1.0)
        chm = np.zeros((128, 7), np.float32)
        chm[:, :i] = 1.0
        m["chainmask"] = chm
        m["chaininv"] = np.ascontiguousarray(1.0 - chm)
        psel = np.zeros((128, 8), np.float32)
        if i > 0:
            psel[:, i - 1] = 1.0
        m["prevsel"] = psel
        maps.append(m)
    return maps


def kernel(**inputs):
    maps = _pack_inputs(inputs)
    nc, dram = build_nc()
    res = run_bass_kernel_spmd(nc, maps, core_ids=list(range(NCORE)))
    out = np.concatenate([res.results[i]["out"] for i in range(NCORE)], axis=0)
    return out.reshape(1, NCORE * TLOC, DM).astype(np.float32)


# version 18 (final)
# speedup vs baseline: 1.0550x; 1.0082x over previous
import contextlib
import numpy as np

import concourse.bass as bass
import concourse.mybir as mybir
from concourse.bass_utils import run_bass_kernel_spmd

F32 = mybir.dt.float32
BF16 = mybir.dt.bfloat16
AF = mybir.ActivationFunctionType
OP = mybir.AluOpType
AX = mybir.AxisListType

NCORE = 8
DM = 1024
DI = 2048
NS = 16
DTR = 64
TLOC = 2048
HALO = 3
NT = TLOC + HALO
PGM = DM // 128
PGI = DI // 128
TCH = 512
NTC = TLOC // TCH
SC = 256
NSC = TLOC // SC
EPS = 1e-5
ALPHA = 0.1


_RD_KEYS = ("in_", "in0", "in1", "lhsT", "rhs", "identity", "initial",
            "data0", "data1", "scalar", "scalar1", "scalar2", "scale", "bias")


class _Rec:

    def __init__(self):
        self.reads = []
        self.writes = []
        self.ops = []

    def __getattr__(self, name):
        def call(*a, **kw):
            if name in ("iota", "memset") and a and isinstance(a[0], bass.AP):
                self.writes.append(a[0])
            for k, v in kw.items():
                if k == "out":
                    if isinstance(v, bass.AP):
                        self.writes.append(v)
                elif k == "outs":
                    self.writes.extend(x for x in v if isinstance(x, bass.AP))
                elif k == "ins":
                    self.reads.extend(x for x in v if isinstance(x, bass.AP))
                elif k in _RD_KEYS and isinstance(v, bass.AP):
                    self.reads.append(v)
            self.ops.append((name, a, kw))
            return None
        return call


def _rng(ap):
    off = int(ap.offset)
    span = 1
    for d in ap.ap:
        span += abs(int(d[0])) * (int(d[1]) - 1)
    return (ap.tensor.name, off, off + span)


def _cost_ns(eng, dma, rw):
    ops = getattr(rw, "ops", [])
    if dma:
        b = 0
        for name, a, kw in ops:
            o = kw.get("out")
            if isinstance(o, bass.AP):
                el = 1
                for d in o.ap:
                    el *= int(d[1])
                es = 2 if "bf16" in str(o.tensor.dtype) else 4
                b += el * es
        return 900 + b * 0.0035
    tot = 0.0
    for name, a, kw in ops:
        if name == "collective_compute":
            b = 0
            for o in kw.get("outs", []):
                if isinstance(o, bass.AP):
                    el = 1
                    for d in o.ap:
                        el *= int(d[1])
                    b += el * 2
            tot += 3000 + b * 0.006
            continue
        ap = None
        for k in ("out", "in_", "in0"):
            v = kw.get(k)
            if isinstance(v, bass.AP):
                ap = v
                break
        if ap is None and a and isinstance(a[0], bass.AP):
            ap = a[0]
        el = 1
        if ap is not None:
            for d in list(ap.ap)[1:]:
                el *= int(d[1])
        if eng == "tensor":
            if name == "matmul":
                rhs = kw.get("rhs")
                n = 1
                if isinstance(rhs, bass.AP):
                    for d in list(rhs.ap)[1:]:
                        n *= int(d[1])
                tot += (n + 60) * 0.42
            else:
                tot += (el + 60) * 0.42
        elif eng == "vector":
            tot += (58 + el) * 1.05
        elif eng == "scalar":
            tot += (222 + el) * 0.84
        elif eng == "gpsimd":
            tot += (250 + el * 2.4) * 0.72
        else:
            tot += 100.0
    return tot


class Sched:
    ENGINES = ("sync", "tensor", "vector", "scalar", "gpsimd")
    RINGS = {"sync": "dR_sync", "scalar": "dR_scal", "gpsimd": "dR_gp"}

    def __init__(self, nc):
        self.nc = nc
        self.streams = {e: [] for e in self.ENGINES}
        self.cnt = {}
        self.enabled = True
        self.pos = {}
        self.emrec = {}
        self.lastw = {}
        self.lastr = {}
        self.carry = {e: {} for e in self.ENGINES}
        self.act_const = None
        self.phase = "init"
        self.trace = []

    def mksem(self, es, name):
        h = es.enter_context(self.nc.semaphore(name))
        self.cnt[name] = 0
        return h

    def emit(self, eng, fn, waits=(), inc=None, amt=1, dma=False):
        if not self.enabled:
            return None
        rw = _Rec()
        fn(rw)
        stream = ("dR_" + inc.name) if dma else eng
        pos = self.pos.get(stream, 0)
        know = self.carry[eng]
        deps = {}

        def add_dep(rec, p, raw_same=False):
            st = rec["stream"]
            if st == stream and not raw_same:
                return
            cur = deps.get(st)
            if cur is None or p > cur[0]:
                deps[st] = (p, rec)

        raw = []
        for w in waits:
            if not w:
                continue
            sem, val = w
            if val <= 0:
                continue
            r = self.emrec.get((sem.name, val))
            if r is not None:
                add_dep(r, r["pos"])
            else:
                raw.append((sem, val))
        reads = [_rng(a) for a in rw.reads]
        if eng == "scalar" and not dma and self.act_const is not None:
            reads.append(self.act_const)
        writes = [_rng(a) for a in rw.writes]
        for n, lo, hi in reads:
            for st, ent in self.lastw.get(n, {}).items():
                if ent[1] < hi and lo < ent[2]:
                    add_dep(ent[3], ent[0], raw_same=not dma)
        for n, lo, hi in writes:
            for dct in (self.lastw, self.lastr):
                for st, ent in dct.get(n, {}).items():
                    if ent[1] < hi and lo < ent[2]:
                        add_dep(ent[3], ent[0])
        wl = list(raw)
        if dma and pos > 0:
            prev = self.emrec.get((inc.name, self.cnt[inc.name]))
            if prev is not None and know.get(stream, -1) < prev["pos"]:
                wl.append((inc, prev["semval"]))
                for s2, p2 in prev["full"].items():
                    if know.get(s2, -1) < p2:
                        know[s2] = p2
        for st, (p, r) in sorted(deps.items(), key=lambda kv: -kv[1][0]):
            if know.get(st, -1) >= p:
                continue
            wl.append((r["sem"], r["semval"]))
            for s2, p2 in r["full"].items():
                if know.get(s2, -1) < p2:
                    know[s2] = p2
        full = dict(know)
        full[stream] = pos
        self.trace.append((stream, pos, [(r2["stream"], p2) for _, (p2, r2)
                                         in deps.items()],
                           _cost_ns(eng, dma, rw), self.phase))
        semval = None
        if inc is not None:
            self.cnt[inc.name] += amt
            semval = self.cnt[inc.name]
        rec = {"stream": stream, "pos": pos, "sem": inc, "semval": semval,
               "full": full}
        if inc is not None:
            self.emrec[(inc.name, semval)] = rec
        for n, lo, hi in writes:
            d = self.lastw.setdefault(n, {})
            old = d.get(stream)
            if old is not None:
                lo, hi = min(lo, old[1]), max(hi, old[2])
            d[stream] = (pos, lo, hi, rec)
        for n, lo, hi in reads:
            d = self.lastr.setdefault(n, {})
            old = d.get(stream)
            if old is not None:
                lo, hi = min(lo, old[1]), max(hi, old[2])
            d[stream] = (pos, lo, hi, rec)
        self.pos[stream] = pos + 1
        self.streams[eng].append((fn, wl, inc, amt))
        if inc is not None:
            return (inc, semval)
        return None

    def ev(self, sem):
        return (sem, self.cnt[sem.name])

    def build(self, block):
        def mk(stream):
            def body(eng):
                for fn, waits, inc, amt in stream:
                    for s, v in waits:
                        eng.wait_ge(s, v)
                    ins = fn(eng)
                    if inc is not None:
                        ins.then_inc(inc, amt)
            return body
        for name in self.ENGINES:
            if self.streams[name]:
                getattr(block, name)(mk(self.streams[name]))


def ap3(t, off, dims):
    base = t[:]
    part = list(base.ap[0])
    return bass.AP(base.tensor, base.offset + off, [part] + [list(d) for d in dims])


def build_nc(debug_names=(), stage="full"):
    nc = bass.Bass("TRN2", target_bir_lowering=False, debug=False)
    dram = {}

    def din(name, shape, dt=F32):
        dram[name] = nc.dram_tensor(name, shape, dt, kind="ExternalInput").ap()

    def dout(name, shape, dt=F32):
        dram[name] = nc.dram_tensor(name, shape, dt, kind="ExternalOutput").ap()

    def dscr(name, shape, dt=F32):
        dram[name] = nc.dram_tensor(name, shape, dt).ap()

    din("x_slab", [NT, DM])
    din("w_inT", [DM, 2 * DI]); din("w_outT", [DI, DM])
    din("xp_wT", [DI, DTR + 2 * NS]); din("dt_wT", [DTR, DI])
    din("ww_T", [DM, DM]); din("wr_T", [DM, DM])
    din("in_b", [128, 2 * PGI, 1]); din("out_b", [128, PGM, 1])
    din("conv_w", [128, PGI, 4]); din("conv_b", [128, PGI, 1])
    din("dt_b", [128, PGI, 1]); din("A_log", [128, PGI, NS]); din("D_ssm", [128, PGI, 1])
    din("n1w", [128, PGM, 1]); din("n2w", [128, PGM, 1])
    din("decay_r", [128, 1])
    din("sig_T", [DM, 64]); din("knots_r", [128, 8]); din("temp_r", [128, 1])
    din("cw1", [DM, 256]); din("cb1", [128, 2, 1]); din("cw2", [256, 2]); din("cb2", [128, 1])
    din("w1a", [2, 1024]); din("b1f", [128, 8, 1]); din("w2f", [128, 8, 1]); din("b2p", [64, 1])
    din("dirs", [64, DM]); din("gam", [128, PGM, 1]); din("bet", [128, PGM, 1]); din("oscale_r", [128, 1])
    din("eye_bf", [128, 128], BF16); din("eye_f32", [128, 128]); din("ones_bf", [128, 128], BF16)
    din("iota64", [128, 64]); din("iotam", [128, 8]); din("blk8", [128, 8, 64], BF16)
    din("c0mask", [128, 1]); din("chainmask", [128, 7]); din("chaininv", [128, 7]); din("prevsel", [128, 8])
    din("iota256", [1, SC]); din("iotaT", [1, TLOC]); din("consts8", [128, 8])
    dout("out", [TLOC, DM])

    dscr("b_winT", [DM, 2 * DI], BF16); dscr("b_woutT", [DI, DM], BF16)
    dscr("b_xpT", [DI, 96], BF16); dscr("b_dtT", [DTR, DI], BF16)
    dscr("b_wwT", [DM, DM], BF16); dscr("b_wrT", [DM, DM], BF16); dscr("b_cw1", [DM, 256], BF16)
    dscr("b_wwTlo", [DM, DM], BF16); dscr("b_wrTlo", [DM, DM], BF16)
    dscr("s_xfm", [128, PGM, TLOC])
    dscr("s_s1", [128, PGM, NT])
    dscr("s_o2", [128, PGM, TLOC])
    dscr("s_h1", [128, PGM, NT], BF16)
    dscr("s_xm", [128, PGI, TLOC], BF16)
    dscr("s_res", [128, PGI, TLOC])
    dscr("s_g", [128, PGI, TLOC], BF16)
    dscr("s_bc", [2 * NS, TLOC], BF16)
    dscr("s_delta", [128, PGI, TLOC])
    dscr("s_cdl", [128, PGI, TLOC], BF16)
    dscr("s_y", [128, PGI, TLOC])
    dscr("s_reads", [128, PGM, TLOC])
    dscr("s_row", [4, TLOC])
    dscr("cc_ch_i", [2, 128, PGI, NS])
    dscr("cc_ch_o", [NCORE, 2, 128, PGI, NS])
    dscr("cc_h1_i", [128, PGM, HALO])
    dscr("cc_h1_o", [NCORE, 128, PGM, HALO])
    dscr("cc_wk_i", [128, PGM, 1])
    dscr("cc_wk_o", [NCORE, 128, PGM, 1])
    dscr("cc_ag_i", [128, PGM, DM], BF16)
    dscr("cc_ag_o", [NCORE, 128, PGM, DM], BF16)

    with contextlib.ExitStack() as es:
        sch = Sched(nc)
        globals()["LAST_SCHED"] = sch
        q_ld = sch.mksem(es, "q_ld")
        q_g = sch.mksem(es, "q_g")
        q_s = sch.mksem(es, "q_s")
        q_st = sch.mksem(es, "q_st")
        e_pe = sch.mksem(es, "e_pe")
        e_dv = sch.mksem(es, "e_dv")
        e_ac = sch.mksem(es, "e_ac")
        e_gp = sch.mksem(es, "e_gp")
        q_l2 = sch.mksem(es, "q_l2")
        q_l3 = sch.mksem(es, "q_l3")
        q_l4 = sch.mksem(es, "q_l4")

        rot = {"on": False, "i": 0}

        def dma(qeng, sem, out, in_, waits=(), slow=False):
            if rot["on"] and sem is q_ld:
                rot["i"] = (rot["i"] + 1) % 4
                sem = (q_ld, q_l2, q_l3, q_l4)[rot["i"]]
            return sch.emit(qeng, lambda e, o=out, i=in_, s=slow: e.dma_start(
                out=o, in_=i, allow_slow_non_contiguous=s), waits=waits, inc=sem,
                amt=16, dma=True)

        def onpe(fn, waits=()):
            return sch.emit("tensor", fn, waits=waits, inc=e_pe)

        def ondv(fn, waits=()):
            return sch.emit("vector", fn, waits=waits, inc=e_dv)

        def onac(fn, waits=()):
            return sch.emit("scalar", fn, waits=waits, inc=e_ac)

        def ongp(fn, waits=()):
            return sch.emit("gpsimd", fn, waits=waits, inc=e_gp)

        def barrier():
            return [sch.ev(s) for s in (q_ld, q_l2, q_l3, q_l4, q_g, q_s, q_st,
                                        e_pe, e_dv, e_ac, e_gp)]

        def dvfence():
            return ondv(lambda e: e.tensor_reduce(out=k_fence[:, 1:2],
                                                  in_=ap3(k_fence, 0, [[0, 256]]),
                                                  axis=AX.X, op=OP.add))

        def tap(name, ap, waits):
            if name in debug_names:
                dout("tap_" + name, list(ap.shape), ap.dtype)
                dma("gpsimd", q_g, dram["tap_" + name], ap, waits=waits)

        S = lambda n, sh, dt=F32: es.enter_context(nc.sbuf_tensor(n, sh, dt))
        P = lambda n, sh: es.enter_context(nc.psum_tensor(n, sh, F32))

        k_eye = S("k_eye", [128, 128], BF16)
        k_eyef = S("k_eyef", [128, 128])
        k_ones = S("k_ones", [128, 128], BF16)
        k_inb = S("k_inb", [128, 2 * PGI, 1])
        k_outb = S("k_outb", [128, PGM, 1])
        k_cw = S("k_cw", [128, PGI, 4])
        k_cb = S("k_cb", [128, PGI, 1])
        k_dtb = S("k_dtb", [128, PGI, 1])
        k_dssm = S("k_dssm", [128, PGI, 1])
        k_n1 = S("k_n1", [128, PGM, 1])
        k_n2 = S("k_n2", [128, PGM, 1])
        k_gam = S("k_gam", [128, PGM, 1])
        k_bet = S("k_bet", [128, PGM, 1])
        k_knots = S("k_knots", [128, 8])
        k_kA = S("k_kA", [128, 7])
        k_kB = S("k_kB", [128, 7])
        k_io64 = S("k_io64", [128, 64])
        k_iom = S("k_iom", [128, 8])
        k_blk8 = S("k_blk8", [128, 8, 64], BF16)
        k_c0 = S("k_c0", [128, 1])
        k_chm = S("k_chm", [128, 7])
        k_chi = S("k_chi", [128, 7])
        k_psel = S("k_psel", [128, 8])
        k_osc = S("k_osc", [128, 1])
        k_cb1 = S("k_cb1", [128, 2, 1])
        k_cb2 = S("k_cb2", [128, 1])
        k_b1f = S("k_b1f", [128, 8, 1])
        k_w2f = S("k_w2f", [128, 8, 1])
        k_b2p = S("k_b2p", [64, 1])
        k_cw2 = S("k_cw2", [128, 2, 2], BF16)
        k_w1a = S("k_w1a", [2, 1024], BF16)
        k_dirs = S("k_dirs", [64, DM], BF16)
        k_qT = S("k_qT", [128, PGM, 64], BF16)
        k_csT = S("k_csT", [128, PGM, 8], BF16)
        k_misc = S("k_misc", [128, 80])
        k_fence = S("k_fence", [128, 8])
        k_tmp8 = S("k_tmp8", [128, 8])
        k_c8 = S("k_c8", [128, 8])
        sch.act_const = ("k_c8", 0, 9)
        k_MT = S("k_MT", [128, 2, SC])
        k_gw = S("k_gw", [128, 2, 1])
        k_gwi = S("k_gwi", [128, 2, 1])
        k_g256 = S("k_g256", [128, 1])
        k_g2048 = S("k_g2048", [128, 1])
        Apk = S("Apk", [128, PGI, NS])
        states = S("states", [128, PGI, NS])
        hend = S("hend", [128, PGI, NS])
        pend = S("pend", [128, PGI, NS])
        cdend = S("cdend", [128, PGI, 1])
        hin = S("hin", [128, PGI, NS])
        lnh = S("lnh", [128, PGI, NS])
        sgh = S("sgh", [128, PGI, NS])
        wk1 = S("wk1", [128, 2048])
        wk2 = S("wk2", [128, 2048])
        wkb = S("wkb", [128, 2048], BF16)
        wkb2 = S("wkb2", [128, 2048], BF16)
        sqb = S("sqb", [128, PGM, TCH], BF16)
        sA = S("sA", [128, PGM, NT])
        st_box = {}

        p1 = P("p1", [128, 512])
        p2 = P("p2", [128, 512])
        p3 = P("p3", [128, 512])
        p4 = P("p4", [128, 512])
        pbig = P("pbig", [128, 4, 512])

        blockctx = es.enter_context(nc.Block())
        STAGES = ["p0", "tr", "ln0", "m1", "m2", "att", "full"]
        SN_OFF = 2
        SN = STAGES.index(stage)

        sch.phase = "p0"
        for t, n in [(k_eye, "eye_bf"), (k_eyef, "eye_f32"), (k_ones, "ones_bf"),
                     (k_inb, "in_b"), (k_outb, "out_b"), (k_cw, "conv_w"), (k_cb, "conv_b"),
                     (k_dtb, "dt_b"), (k_dssm, "D_ssm"), (k_n1, "n1w"), (k_n2, "n2w"),
                     (k_gam, "gam"), (k_bet, "bet"), (k_knots, "knots_r"),
                     (k_io64, "iota64"), (k_iom, "iotam"), (k_blk8, "blk8"),
                     (k_c0, "c0mask"), (k_chm, "chainmask"), (k_chi, "chaininv"), (k_psel, "prevsel"),
                     (k_osc, "oscale_r"), (k_cb1, "cb1"), (k_cb2, "cb2"),
                     (k_b1f, "b1f"), (k_w2f, "w2f"), (k_b2p, "b2p"), (k_c8, "consts8")]:
            dma("sync", q_ld, t[:], dram[n])
        ld_const = sch.ev(q_ld)
        ondv(lambda e: e.memset(k_fence[:], 0.0))
        rot["on"] = True
        for i, v in enumerate([EPS, 1.0, -1.0, -0.5, 256.0, 2048.0, 0.5, 0.0]):
            nc.const_aps.register(F32, v, k_c8[:, i:i + 1])

        ev = dma("sync", q_ld, wk1[:, 0:PGI * NS], dram["A_log"].rearrange("p a b -> p (a b)"))
        onac(lambda e: e.activation(out=wk2[:, 0:PGI * NS], in_=wk1[:, 0:PGI * NS], func=AF.Exp),
             waits=[ev])
        ondv(lambda e: e.tensor_scalar_mul(out=Apk[:].rearrange("p a b -> p (a b)"),
                                           in0=wk2[:, 0:PGI * NS], scalar1=-1.0),
             waits=[sch.ev(e_ac)])

        def prep_w(src, dst, rows, cols, lo_dst=None, ping=[0]):
            for r0 in range(0, rows, 128):
                rr = min(128, rows - r0)
                for c0 in range(0, cols, 2048):
                    cc = min(2048, cols - c0)
                    if lo_dst is None:
                        wsrc = wk1 if ping[0] == 0 else wk2
                        wdst = wkb if ping[0] == 0 else wkb2
                        ping[0] ^= 1
                    else:
                        wsrc, wdst = wk1, wkb
                    ev1 = dma("sync", q_ld, wsrc[0:rr, 0:cc], dram[src][r0:r0 + rr, c0:c0 + cc])
                    ev2 = ondv(lambda e, rr=rr, cc=cc, wsrc=wsrc, wdst=wdst: e.tensor_copy(
                        out=wdst[0:rr, 0:cc], in_=wsrc[0:rr, 0:cc]), waits=[ev1])
                    dma("scalar", q_s, dram[dst][r0:r0 + rr, c0:c0 + cc], wdst[0:rr, 0:cc],
                        waits=[ev2])
                    if lo_dst is not None:
                        ondv(lambda e, rr=rr, cc=cc: e.tensor_copy(out=wk2[0:rr, 0:cc],
                                                                   in_=wkb[0:rr, 0:cc]))
                        ev3 = ondv(lambda e, rr=rr, cc=cc: e.tensor_sub(out=wkb2[0:rr, 0:cc],
                                                                        in0=wk1[0:rr, 0:cc],
                                                                        in1=wk2[0:rr, 0:cc]))
                        dma("scalar", q_s, dram[lo_dst][r0:r0 + rr, c0:c0 + cc], wkb2[0:rr, 0:cc],
                            waits=[ev3])

        for a in [("w_inT", "b_winT", DM, 2 * DI), ("w_outT", "b_woutT", DI, DM),
                  ("xp_wT", "b_xpT", DI, 96), ("dt_wT", "b_dtT", DTR, DI),
                  ("ww_T", "b_wwT", DM, DM, "b_wwTlo"), ("wr_T", "b_wrT", DM, DM, "b_wrTlo"),
                  ("cw1", "b_cw1", DM, 256)]:
            prep_w(*a)
        ev_wprep = sch.ev(q_s)

        ev1 = dma("sync", q_ld, wk1[:, 0:4].rearrange("p (a c) -> p a c", a=2),
                  dram["cw2"].rearrange("(a p) c -> p a c", p=128),
                  waits=[sch.ev(e_dv)])
        ondv(lambda e: e.tensor_copy(out=k_cw2[:].rearrange("p a b -> p (a b)"), in_=wk1[:, 0:4]),
             waits=[ev1])
        ev1 = dma("sync", q_ld, wk1[0:2, 0:1024], dram["w1a"], waits=[sch.ev(e_dv)])
        ondv(lambda e: e.tensor_copy(out=k_w1a[:], in_=wk1[0:2, 0:1024]), waits=[ev1])
        ev1 = dma("sync", q_ld, wk1[0:64, 0:DM], dram["dirs"], waits=[sch.ev(e_dv)])
        ondv(lambda e: e.tensor_copy(out=k_dirs[:], in_=wk1[0:64, 0:DM]), waits=[ev1])

        ondv(lambda e: e.tensor_sub(out=k_misc[:, 0:7], in0=k_knots[:, 1:8],
                                    in1=k_knots[:, 0:7]), waits=[ld_const])
        dvfence()
        ondv(lambda e: e.tensor_mul(out=k_misc[:, 8:15], in0=k_misc[:, 0:7],
                                    in1=k_io64[:, 0:7]))
        dvfence()
        ondv(lambda e: e.tensor_sub(out=k_misc[:, 8:15], in0=k_knots[:, 0:7],
                                    in1=k_misc[:, 8:15]))
        dvfence()
        def build_calib(e):
            e.tensor_copy(out=k_kA[:, 0:7], in_=k_misc[:, 8:15])
            return e.tensor_copy(out=k_kB[:, 0:7], in_=k_misc[:, 0:7])
        ondv(build_calib)
        dvfence()
        def build_calib2(e):
            e.tensor_sub(out=k_kA[:, 1:7], in0=k_kA[:, 1:7], in1=k_misc[:, 8:14])
            return e.tensor_sub(out=k_kB[:, 1:7], in0=k_kB[:, 1:7], in1=k_misc[:, 0:6])
        ondv(build_calib2)

        ev1 = dma("sync", q_ld, k_misc[:, 16:17], dram["temp_r"])
        ondv(lambda e: e.tensor_scalar_add(out=k_misc[:, 64:72],
                                           in0=ap3(k_misc, 16, [[0, 8]]), scalar1=1e-6),
             waits=[ev1])
        dvfence()
        ondv(lambda e: e.reciprocal(out=k_misc[:, 64:72], in_=k_misc[:, 64:72]))

        ev1 = dma("sync", q_ld, k_misc[:, 17:18], dram["decay_r"])
        ondv(lambda e: e.tensor_copy(out=k_misc[:, 32:40],
                                     in_=ap3(k_misc, 17, [[0, 8]])), waits=[ev1, ld_const])
        def g_a(e):
            e.activation(out=k_misc[:, 40:48], in_=k_misc[:, 32:40], func=AF.Abs)
            e.activation(out=k_misc[:, 40:48], in_=k_misc[:, 40:48], func=AF.Exp, scale=-1.0)
            e.activation(out=k_misc[:, 40:48], in_=k_misc[:, 40:48], func=AF.Ln, bias=1.0)
            return e.activation(out=k_misc[:, 48:56], in_=k_misc[:, 32:40], func=AF.Relu,
                                scale=-1.0)
        onac(g_a, waits=[sch.ev(e_dv)])
        ondv(lambda e: e.scalar_tensor_tensor(out=k_misc[:, 56:64], in0=k_misc[:, 40:48],
                                              scalar=-1.0, in1=k_misc[:, 48:56],
                                              op0=OP.mult, op1=OP.subtract),
             waits=[sch.ev(e_ac)])
        onac(lambda e: e.activation(out=k_misc[:, 24:32], in_=k_misc[:, 56:64],
                                    func=AF.Relu, scale=-1.0), waits=[sch.ev(e_dv)])
        ev_lg = sch.ev(e_ac)
        def g_c(e):
            e.activation(out=k_g256[:], in_=k_misc[:, 56:57], func=AF.Exp, scale=float(SC))
            return e.activation(out=k_g2048[:], in_=k_misc[:, 56:57], func=AF.Exp,
                                scale=float(TLOC))
        onac(g_c, waits=[sch.ev(e_dv)])
        def g_iotas(e):
            e.iota(k_misc[:, 72:73], [[0, 1]], base=SC - 1, channel_multiplier=-1,
                   allow_small_or_imprecise_dtypes=True)
            return e.iota(k_misc[:, 73:74], [[0, 1]], base=SC - 129, channel_multiplier=-1,
                          allow_small_or_imprecise_dtypes=True)
        ongp(g_iotas, waits=[ld_const])
        def g_d(e):
            e.tensor_scalar(out=k_misc[:, 74:75], in0=k_misc[:, 72:73],
                            scalar1=k_misc[:, 24:25], scalar2=-1.0, op0=OP.mult, op1=OP.mult)
            return e.tensor_scalar(out=k_misc[:, 75:76], in0=k_misc[:, 73:74],
                                   scalar1=k_misc[:, 24:25], scalar2=-1.0, op0=OP.mult,
                                   op1=OP.mult)
        ondv(g_d, waits=[sch.ev(e_gp), ev_lg])
        def g_e(e):
            e.activation(out=k_gw[:, 0, :], in_=k_misc[:, 74:75], func=AF.Exp)
            e.activation(out=k_gw[:, 1, :], in_=k_misc[:, 75:76], func=AF.Exp)
            e.activation(out=k_gwi[:, 0, :], in_=k_misc[:, 74:75], func=AF.Exp, scale=-1.0)
            return e.activation(out=k_gwi[:, 1, :], in_=k_misc[:, 75:76], func=AF.Exp,
                                scale=-1.0)
        onac(g_e, waits=[sch.ev(e_dv)])

        ev_i = dma("sync", q_ld, wk1[0:1, 0:SC], dram["iota256"])
        ev_j = dma("sync", q_ld, wk2[0:1, 0:TLOC], dram["iotaT"])
        def g_rows(e):
            e.tensor_scalar(out=wk1[0:1, 0:SC], in0=wk1[0:1, 0:SC], scalar1=k_misc[0:1, 24:25],
                            scalar2=-1.0, op0=OP.mult, op1=OP.mult)
            return e.tensor_scalar(out=wk2[0:1, 0:TLOC], in0=wk2[0:1, 0:TLOC],
                                   scalar1=k_misc[0:1, 24:25], scalar2=-1.0, op0=OP.mult,
                                   op1=OP.mult)
        ondv(g_rows, waits=[ev_i, ev_j, ev_lg])
        def g_rows2(e):
            e.activation(out=wk1[0:1, 0:SC], in_=wk1[0:1, 0:SC], func=AF.Exp)
            return e.activation(out=wk2[0:1, 0:TLOC], in_=wk2[0:1, 0:TLOC], func=AF.Exp)
        onac(g_rows2, waits=[sch.ev(e_dv)])
        ev1 = dma("scalar", q_s, dram["s_row"][1:2, 0:SC], wk1[0:1, 0:SC],
                  waits=[sch.ev(e_ac), ev_wprep])
        ev2 = dma("scalar", q_s, dram["s_row"][0:1, 0:TLOC], wk2[0:1, 0:TLOC], waits=[sch.ev(e_ac)])
        ev_grow = [ev1, ev2]

        def mt_iota(e):
            e.iota(wk1[:, 0:SC], [[1, SC]], base=-1, channel_multiplier=-1,
                   allow_small_or_imprecise_dtypes=True)
            return e.iota(wk1[:, SC:2 * SC], [[1, SC]], base=-129, channel_multiplier=-1,
                          allow_small_or_imprecise_dtypes=True)
        ongp(mt_iota, waits=[sch.ev(e_ac), sch.ev(q_s)])
        ondv(lambda e: e.tensor_scalar(out=wk2[:, 0:2 * SC], in0=wk1[:, 0:2 * SC],
                                       scalar1=k_misc[:, 24:25], scalar2=-1.0, op0=OP.mult,
                                       op1=OP.mult),
             waits=[sch.ev(e_gp)])
        onac(lambda e: e.activation(out=wk2[:, 0:2 * SC], in_=wk2[:, 0:2 * SC], func=AF.Exp),
             waits=[sch.ev(e_dv)])
        def mt_mask(e):
            e.affine_select(out=wk1[:, 0:SC], in_=wk2[:, 0:SC], pattern=[[1, SC]],
                            compare_op=OP.is_ge, fill=0.0, base=-1, channel_multiplier=-1)
            return e.affine_select(out=wk1[:, SC:2 * SC], in_=wk2[:, SC:2 * SC], pattern=[[1, SC]],
                                   compare_op=OP.is_ge, fill=0.0, base=-129, channel_multiplier=-1)
        ongp(mt_mask, waits=[sch.ev(e_ac)])
        ondv(lambda e: e.tensor_copy(out=k_MT[:].rearrange("p a b -> p (a b)"), in_=wk1[:, 0:2 * SC]),
             waits=[sch.ev(e_gp)])

        for pg in range(PGM):
            ev1 = dma("sync", q_ld, wk1[:, 0:64], dram["sig_T"][pg * 128:(pg + 1) * 128, :],
                      waits=[sch.ev(e_dv)])
            def qf(e, pg=pg):
                e.tensor_single_scalar(out=wk2[:, 0:64], in_=wk1[:, 0:64], scalar=0.3, op=OP.is_gt)
                return e.tensor_single_scalar(out=wk2[:, 64:128], in_=wk1[:, 0:64], scalar=-0.3,
                                              op=OP.is_ge)
            ondv(qf, waits=[ev1])
            dvfence()
            ondv(lambda e: e.scalar_tensor_tensor(out=wk2[:, 0:64], in0=wk2[:, 0:64],
                                                  scalar=-1.0, in1=wk2[:, 64:128],
                                                  op0=OP.add, op1=OP.add))
            dvfence()
            def qf2(e, pg=pg):
                e.tensor_copy(out=k_qT[:, pg, :], in_=wk2[:, 0:64])
                return e.tensor_reduce(out=wk2[:, 128:136],
                                       in_=wk2[:, 0:64].rearrange("p (c j) -> p c j", c=8),
                                       axis=AX.X, op=OP.add)
            ondv(qf2)
            onac(lambda e: e.activation(out=wk2[:, 136:144], in_=wk2[:, 128:136], func=AF.Sign),
                 waits=[sch.ev(e_dv)])
            ondv(lambda e, pg=pg: e.tensor_copy(out=k_csT[:, pg, :], in_=wk2[:, 136:144]),
                 waits=[sch.ev(e_ac)])

        sch.phase = "p1_transpose"
        stA_cm = [nc.sbuf_tensor("stA%d" % i, [128, NT], F32) for i in range(3)]
        st_box["mu"], st_box["ri"], st_box["t"] = [c.__enter__() for c in stA_cm]
        if SN < 1:
            sch.enabled = False
        st_mu0 = st_box["mu"]
        NTT = (NT + 127) // 128
        for tt in range(NTT):
            r0 = tt * 128
            rr = min(128, NT - r0)
            xo = (tt % 2) * DM
            evm = None
            if rr < 128:
                evm = ondv(lambda e, xo=xo: e.memset(st_mu0[:, xo:xo + DM], 0.0))
            evl = dma("sync", q_ld, st_mu0[0:rr, xo:xo + DM], dram["x_slab"][r0:r0 + rr, :],
                      waits=[evm] if evm else [])
            for dg in range(PGM):
                evt = onpe(lambda e, dg=dg, xo=xo: e.transpose(
                    out=p1[:, (dg % 4) * 128:(dg % 4) * 128 + 128],
                    in_=st_mu0[:, xo + dg * 128:xo + (dg + 1) * 128],
                    identity=k_eyef[:]), waits=[evl, ld_const])
                ondv(lambda e, dg=dg, r0=r0, rr=rr: e.tensor_copy(
                    out=sA[:, dg, r0:r0 + rr],
                    in_=p1[0:128, (dg % 4) * 128:(dg % 4) * 128 + rr]), waits=[evt])
        ev_xfm = sch.ev(e_dv)
        for pg in range(PGM):
            dma("scalar", q_s, dram["s_xfm"][:, pg, :], sA[:, pg, HALO:NT], waits=[ev_xfm])

        if SN < 2:
            sch.enabled = False
        def emit_ln(src, dst, normw, rms_only, width, cast_bf=None, gb=None, bb=None):
            st_mu, st_ri, st_t = st_box["mu"], st_box["ri"], st_box["t"]
            for nc0 in range(0, width, TCH):
                w = min(TCH, width - nc0)
                eva = onac(lambda e, nc0=nc0, w=w: e.activation(out=sqb[:, :, 0:w],
                                                                in_=src[:, :, nc0:nc0 + w],
                                                                func=AF.Square),
                           waits=[])
                last = None
                for kg in range(PGM):
                    last = onpe(lambda e, kg=kg, w=w: e.matmul(out=p3[:, 0:w], lhsT=k_ones[:],
                                                               rhs=sqb[:, kg, 0:w],
                                                               start=(kg == 0), stop=(kg == PGM - 1)),
                                waits=[eva])
                ondv(lambda e, nc0=nc0, w=w: e.tensor_scalar_mul(out=st_t[:, nc0:nc0 + w],
                                                                 in0=p3[:, 0:w], scalar1=1.0 / DM),
                     waits=[last])
                if not rms_only:
                    evc = onac(lambda e, nc0=nc0, w=w: e.activation(out=sqb[:, :, 0:w],
                                                                    in_=src[:, :, nc0:nc0 + w],
                                                                    func=AF.Identity),
                               waits=[last])
                    last = None
                    for kg in range(PGM):
                        last = onpe(lambda e, kg=kg, w=w: e.matmul(out=p4[:, 0:w], lhsT=k_ones[:],
                                                                   rhs=sqb[:, kg, 0:w],
                                                                   start=(kg == 0), stop=(kg == PGM - 1)),
                                    waits=[evc])
                    ondv(lambda e, nc0=nc0, w=w: e.tensor_scalar_mul(out=st_mu[:, nc0:nc0 + w],
                                                                     in0=p4[:, 0:w], scalar1=1.0 / DM),
                         waits=[last])
            ev_red = sch.ev(e_dv)
            def fin1(e):
                if not rms_only:
                    e.tensor_mul(out=st_ri[:, 0:width], in0=st_mu[:, 0:width], in1=st_mu[:, 0:width])
                    return e.tensor_sub(out=st_t[:, 0:width], in0=st_t[:, 0:width],
                                        in1=st_ri[:, 0:width])
                return e.tensor_copy(out=st_ri[:, 0:width], in_=st_t[:, 0:width])
            ondv(fin1, waits=[ev_red])
            def fin2(e):
                e.activation(out=st_ri[:, 0:width], in_=st_t[:, 0:width], func=AF.Ln, bias=EPS)
                return e.activation(out=st_ri[:, 0:width], in_=st_ri[:, 0:width], func=AF.Exp,
                                    scale=-0.5)
            onac(fin2, waits=[sch.ev(e_dv)])
            ev_r = sch.ev(e_ac)
            def app(e):
                last = None
                for pg in range(PGM):
                    if rms_only:
                        last = e.scalar_tensor_tensor(out=dst[:, pg, 0:width], in0=src[:, pg, 0:width],
                                                      scalar=normw[:, pg, :], in1=st_ri[:, 0:width],
                                                      op0=OP.mult, op1=OP.mult)
                    else:
                        e.tensor_sub(out=dst[:, pg, 0:width], in0=src[:, pg, 0:width],
                                     in1=st_mu[:, 0:width])
                        last = e.tensor_mul(out=dst[:, pg, 0:width], in0=dst[:, pg, 0:width],
                                            in1=st_ri[:, 0:width])
                    if gb is not None:
                        last = e.tensor_scalar(out=dst[:, pg, 0:width], in0=dst[:, pg, 0:width],
                                               scalar1=gb[:, pg, :], scalar2=bb[:, pg, :],
                                               op0=OP.mult, op1=OP.add)
                    if cast_bf is not None:
                        last = e.tensor_copy(out=cast_bf[:, pg, 0:width], in_=dst[:, pg, 0:width])
                return last
            ondv(app, waits=[ev_r])
            return sch.ev(e_dv)

        ev_ln0 = emit_ln(sA, sA, None, False, NT)
        tap("ln0", sA[:, :, :], [ev_ln0])
        if SN < 3:
            sch.enabled = False
        for pg in range(PGM):
            dma("scalar", q_s, dram["s_s1"][:, pg, :], sA[:, pg, :], waits=[ev_ln0])
        ev_s1sp = sch.ev(q_s)

        def emit_mamba(which, arh, wsl):
            normw = k_n1 if which == 1 else k_n2
            ev_rms = emit_ln(sA, sA, normw, True, NT, cast_bf=arh)
            tap("rms%d" % which, arh[:, :, :], [ev_rms])
            for pg in range(PGM):
                dma("scalar", q_s, dram["s_h1"][:, pg, :], arh[:, pg, :], waits=[ev_rms])
            ev_h1sp = sch.ev(q_s)

            sch.phase = "m_inproj"
            rowb = wkb
            with nc.sbuf_tensor("rowb%d" % which, [128, NT], BF16) as rowb2:
                for mg in range(PGI):
                    evw = dma("sync", q_ld, wsl[:].rearrange("p a b -> p a b"),
                              dram["b_winT"][:, mg * 128:(mg + 1) * 128]
                              .rearrange("(a p) c -> p a c", p=128),
                              waits=[])
                    for inch, nc0 in enumerate(range(0, NT, TCH)):
                        w = min(TCH, NT - nc0)
                        pb = p1 if inch % 2 == 0 else p2
                        def ipq(e, nc0=nc0, w=w, pb=pb):
                            r = None
                            for kg in range(PGM):
                                r = e.matmul(out=pb[:, 0:w], lhsT=wsl[:, kg, 0:128],
                                             rhs=arh[:, kg, nc0:nc0 + w], start=(kg == 0),
                                             stop=(kg == PGM - 1))
                            return r
                        last = onpe(ipq, waits=[evw, ev_rms])
                        ondv(lambda e, mg=mg, nc0=nc0, w=w, pb=pb: e.tensor_scalar(
                            out=rowb2[:, nc0:nc0 + w], in0=pb[:, 0:w], scalar1=k_inb[:, mg, :],
                            scalar2=None, op0=OP.add), waits=[last])
                    evrow = ondv(lambda e: e.tensor_scalar(out=rowb2[:, 0:HALO], in0=rowb2[:, 0:HALO],
                                                           scalar1=k_c0[:, 0:1], scalar2=None,
                                                           op0=OP.mult))
                    evd = ondv(lambda e, mg=mg: [e.tensor_scalar(
                        out=wkb2[:, 128 * k:128 * (k + 1)], in0=k_eye[:],
                        scalar1=k_cw[:, mg, k:k + 1], scalar2=None, op0=OP.mult)
                        for k in range(4)][-1], waits=[evrow])
                    for inch, nc0 in enumerate(range(0, TLOC, TCH)):
                        pb = p3 if inch % 2 == 0 else p4
                        def cvq(e, nc0=nc0, pb=pb):
                            r = None
                            for k in range(4):
                                r = e.matmul(out=pb[:, 0:TCH], lhsT=wkb2[:, 128 * k:128 * (k + 1)],
                                             rhs=rowb2[:, nc0 + k:nc0 + k + TCH], start=(k == 0),
                                             stop=(k == 3))
                            return r
                        last = onpe(cvq, waits=[evd])
                        eva = onac(lambda e, mg=mg, pb=pb: e.activation(
                            out=wkb[:, 0:TCH], in_=pb[:, 0:TCH], func=AF.Silu, bias=k_cb[:, mg, :]),
                            waits=[last])
                        dma("scalar", q_s, dram["s_xm"][:, mg, nc0:nc0 + TCH], wkb[:, 0:TCH],
                            waits=[eva])
            ev_xm = sch.ev(q_s)
            tap("xm%d" % which, wkb[:, 0:TCH], [ev_xm])

            sch.phase = "m_xdbl"
            xdb = st_box["t"]
            evw = dma("sync", q_ld, wkb2[:, 0:PGI * 96].rearrange("p (a c) -> p a c", a=PGI),
                      dram["b_xpT"].rearrange("(a p) c -> p a c", p=128),
                      waits=barrier())
            for inch, nc0 in enumerate(range(0, TLOC, TCH)):
                pb = p1 if inch % 2 == 0 else p2
                last = None
                for kh in range(2):
                    evxs = [dma("sync", q_ld, sqb[:, kg % PGM, 0:TCH],
                                dram["s_xm"][:, kg, nc0:nc0 + TCH], waits=[ev_xm])
                            for kg in range(kh * 8, kh * 8 + 8)]
                    def xdq(e, kh=kh, pb=pb):
                        r = None
                        for kg in range(kh * 8, kh * 8 + 8):
                            r = e.matmul(
                                out=pb[0:96, 0:TCH], lhsT=wkb2[:, kg * 96:(kg + 1) * 96],
                                rhs=sqb[:, kg % PGM, 0:TCH], start=(kg == 0),
                                stop=(kg == PGI - 1))
                        return r
                    last = onpe(xdq, waits=evxs + [evw])
                ondv(lambda e, nc0=nc0, pb=pb: e.tensor_copy(out=xdb[0:96, nc0:nc0 + TCH],
                                                             in_=pb[0:96, 0:TCH]), waits=[last])
            ev_xdb = sch.ev(e_dv)
            tap("xdb%d" % which, xdb[0:96, 0:TLOC], [ev_xdb])
            ondv(lambda e: e.tensor_copy(out=wkb[0:32, 0:TLOC], in_=xdb[64:96, 0:TLOC]),
                 waits=[ev_xdb])
            ev_bc = dma("scalar", q_s, dram["s_bc"], wkb[0:32, 0:TLOC], waits=[])

            sch.phase = "m_delta"
            evw = dma("sync", q_ld, wkb2[0:64, 0:DI], dram["b_dtT"], waits=barrier())
            sqf = sqb[:].rearrange("p a b -> p (a b)")
            ev_x64 = ondv(lambda e: e.tensor_copy(out=sqf[0:64, 0:TLOC], in_=xdb[0:64, 0:TLOC]),
                          waits=[ev_xdb, evw])
            ondv(lambda e: e.memset(states[:].rearrange("p a b -> p (a b)"), 0.0))
            ondv(lambda e: e.memset(cdend[:].rearrange("p a b -> p (a b)"), 0.0))
            ondv(lambda e: e.memset(wk2[:, 0:TLOC], 1.0))
            ev_ones = sch.ev(e_dv)
            for pg in range(PGI):
                for itc in range(NTC):
                    nc0 = itc * TCH
                    pb = p3 if itc % 2 == 0 else p4
                    evq = onpe(lambda e, pg=pg, nc0=nc0, pb=pb: e.matmul(
                        out=pb[:, 0:TCH], lhsT=wkb2[0:64, pg * 128:(pg + 1) * 128],
                        rhs=sqf[0:64, nc0:nc0 + TCH], start=True, stop=True),
                        waits=[evw, ev_x64])
                    def sp(e, pg=pg, pb=pb, nc0=nc0):
                        smu = st_box["mu"]
                        e.activation(out=wk1[:, nc0:nc0 + TCH], in_=pb[:, 0:TCH], func=AF.Relu,
                                     bias=k_dtb[:, pg, :])
                        e.activation(out=smu[:, 0:TCH], in_=pb[:, 0:TCH], func=AF.Abs,
                                     bias=k_dtb[:, pg, :])
                        e.activation(out=smu[:, 0:TCH], in_=smu[:, 0:TCH], func=AF.Exp,
                                     scale=-1.0)
                        return e.activation(out=smu[:, 0:TCH], in_=smu[:, 0:TCH],
                                            func=AF.Ln, bias=1.0)
                    evsp = onac(sp, waits=[evq])
                    ondv(lambda e, nc0=nc0: e.tensor_add(out=wk1[:, nc0:nc0 + TCH],
                                                         in0=wk1[:, nc0:nc0 + TCH],
                                                         in1=st_box["mu"][:, 0:TCH]),
                         waits=[evsp, ev_ones, ev_bc])
                def cds(e, pg=pg):
                    e.tensor_tensor_scan(out=wkb[:, 0:TLOC], data0=wk2[:, 0:TLOC],
                                         data1=wk1[:, 0:TLOC], initial=0.0,
                                         op0=OP.mult, op1=OP.add)
                    return e.tensor_reduce(out=cdend[:, pg, :], in_=wk1[:, 0:TLOC], axis=AX.X,
                                           op=OP.add)
                ondv(cds)
                ev2 = sch.ev(e_dv)
                dma("scalar", q_s, dram["s_delta"][:, pg, :], wk1[:, 0:TLOC], waits=[ev2])
                dma("scalar", q_s, dram["s_cdl"][:, pg, :], wkb[:, 0:TLOC], waits=[ev2])
            ev_delta = sch.ev(q_s)

            sch.phase = "m_scan"
            sbc = dram["s_bc"]
            NH = NS // 2
            with nc.sbuf_tensor("bcc%d" % which, [128, 2 * NS, TCH], BF16) as bcc:
                for itc in range(NTC):
                    nc0 = itc * TCH
                    evbc = dma("gpsimd", q_g, bcc[:],
                               bass.AP(sbc.tensor, sbc.offset + nc0,
                                       [[0, 128], [TLOC, 2 * NS], [1, TCH]]),
                               waits=[ev_bc, ev_delta])
                    for pg in range(PGI):
                        po = (pg % 2) * TCH
                        if pg % 2 == 0:
                            evd2 = dma("sync", q_ld,
                                       ap3(wk1, 0, [[TCH, 2], [1, TCH]]),
                                       dram["s_delta"][:, pg:pg + 2, nc0:nc0 + TCH],
                                       waits=[ev_delta])
                            evx2 = dma("sync", q_ld,
                                       ap3(wkb2, 0, [[TCH, 2], [1, TCH]]),
                                       dram["s_xm"][:, pg:pg + 2, nc0:nc0 + TCH],
                                       waits=[ev_xm])
                        evd, evx = evd2, evx2
                        evwm = ondv(lambda e, po=po: e.tensor_tensor(
                            out=wkb[:, po:po + TCH], in0=wk1[:, po:po + TCH],
                            in1=wkb2[:, po:po + TCH], op=OP.mult),
                            waits=[evd, evx])
                        py = p1 if pg % 2 == 0 else p2
                        for h in range(2):
                            s0 = h * NH
                            evbb = ondv(lambda e, s0=s0, po=po: e.tensor_tensor(
                                out=sqb[:].rearrange("p a b -> p (a b)"),
                                in0=ap3(wkb, po, [[0, NH], [1, TCH]]),
                                in1=bcc[:, s0:s0 + NH, :], op=OP.mult),
                                waits=[evwm, evbc])
                            for q in range(2):
                                ebuf = wk2 if q == 0 else st_box["t"]
                                def evaq(e, pg=pg, s0=s0, q=q, ebuf=ebuf, po=po):
                                    r = None
                                    for si in range(q * 4, q * 4 + 4):
                                        r = e.activation(
                                            out=ebuf[:, (si % 4) * TCH:(si % 4) * TCH + TCH],
                                            in_=wk1[:, po:po + TCH], func=AF.Exp,
                                            scale=Apk[:, pg, s0 + si:s0 + si + 1])
                                    return r
                                eva = onac(evaq, waits=[evd])
                                def scnq(e, pg=pg, s0=s0, q=q, ebuf=ebuf):
                                    r = None
                                    for si in range(q * 4, q * 4 + 4):
                                        r = e.tensor_tensor_scan(
                                            out=sqb[:, si, :],
                                            data0=ebuf[:, (si % 4) * TCH:(si % 4) * TCH + TCH],
                                            data1=sqb[:, si, :],
                                            initial=states[:, pg, s0 + si:s0 + si + 1],
                                            op0=OP.mult, op1=OP.add)
                                    return r
                                ondv(scnq, waits=[eva, evbb])
                            dvfence()
                            ondv(lambda e, pg=pg, s0=s0: e.tensor_copy(
                                out=states[:, pg, s0:s0 + NH],
                                in_=ap3(sqb, TCH - 1, [[TCH, NH]])))
                            evu = ondv(lambda e, s0=s0: e.tensor_tensor(
                                out=sqb[:].rearrange("p a b -> p (a b)"),
                                in0=sqb[:].rearrange("p a b -> p (a b)"),
                                in1=bcc[:, NS + s0:NS + s0 + NH, :], op=OP.mult))
                            def yredq(e, py=py, h=h):
                                r = None
                                for si in range(NH):
                                    r = e.matmul(out=py[:, 0:TCH], lhsT=k_eye[:],
                                                 rhs=sqb[:, si, :],
                                                 start=(h == 0 and si == 0),
                                                 stop=(h == 1 and si == NH - 1))
                                return r
                            onpe(yredq, waits=[evu])
                        evy = ondv(lambda e, pg=pg, py=py, po=po: e.scalar_tensor_tensor(
                            out=st_box["ri"][:, po:po + TCH], in0=wkb2[:, po:po + TCH],
                            scalar=k_dssm[:, pg, :],
                            in1=py[:, 0:TCH], op0=OP.mult, op1=OP.add), waits=[])
                        dma("scalar", q_s, dram["s_y"][:, pg, nc0:nc0 + TCH],
                            st_box["ri"][:, po:po + TCH], waits=[evy])
            ev_scan = sch.ev(q_s)

            ondv(lambda e: e.tensor_copy(out=hend[:].rearrange("p a b -> p (a b)"),
                                         in_=states[:].rearrange("p a b -> p (a b)")),
                 waits=[sch.ev(e_dv)])
            def pendf(e):
                cb = ap3(cdend, 0, [[1, PGI], [0, NS]])
                return e.tensor_tensor(out=pend[:], in0=cb, in1=Apk[:], op=OP.mult)
            ondv(pendf)
            onac(lambda e: e.activation(out=pend[:].rearrange("p a b -> p (a b)"),
                                        in_=pend[:].rearrange("p a b -> p (a b)"), func=AF.Exp),
                 waits=[sch.ev(e_dv)])
            ev_pend = sch.ev(e_ac)

            ev1 = dma("gpsimd", q_g, dram["cc_ch_i"][0], hend[:], waits=[sch.ev(e_dv)])
            ev2 = dma("gpsimd", q_g, dram["cc_ch_i"][1], pend[:], waits=[ev_pend])
            evc = ongp(lambda e: e.collective_compute(
                "AllGather", OP.bypass, replica_groups=[list(range(NCORE))],
                ins=[dram["cc_ch_i"].opt()], outs=[dram["cc_ch_o"].opt()]), waits=[ev1, ev2])
            ondv(lambda e: e.memset(hin[:].rearrange("p a b -> p (a b)"), 0.0))
            for j in range(NCORE - 1):
                evl1 = dma("sync", q_ld, wk1[:, 0:PGI * NS],
                           dram["cc_ch_o"][j, 0].rearrange("p a b -> p (a b)"), waits=[evc])
                evl2 = dma("sync", q_ld, wk2[:, 0:PGI * NS],
                           dram["cc_ch_o"][j, 1].rearrange("p a b -> p (a b)"), waits=[evc])
                def chj(e, j=j):
                    e.tensor_scalar(out=wk2[:, 0:PGI * NS], in0=wk2[:, 0:PGI * NS],
                                    scalar1=k_chm[:, j:j + 1], scalar2=None, op0=OP.mult)
                    e.tensor_scalar(out=wk2[:, 0:PGI * NS], in0=wk2[:, 0:PGI * NS],
                                    scalar1=k_chi[:, j:j + 1], scalar2=None, op0=OP.add)
                    e.tensor_scalar(out=wk1[:, 0:PGI * NS], in0=wk1[:, 0:PGI * NS],
                                    scalar1=k_chm[:, j:j + 1], scalar2=None, op0=OP.mult)
                    e.tensor_tensor(out=hin[:].rearrange("p a b -> p (a b)"),
                                    in0=hin[:].rearrange("p a b -> p (a b)"),
                                    in1=wk2[:, 0:PGI * NS], op=OP.mult)
                    return e.tensor_add(out=hin[:].rearrange("p a b -> p (a b)"),
                                        in0=hin[:].rearrange("p a b -> p (a b)"),
                                        in1=wk1[:, 0:PGI * NS])
                ondv(chj, waits=[evl1, evl2])
            tap("hend%d" % which, hend[:, :, :], [sch.ev(e_dv), sch.ev(e_gp)])
            tap("pend%d" % which, pend[:, :, :], [sch.ev(e_ac), sch.ev(e_gp)])
            tap("cco%d" % which, dram["cc_ch_o"], [sch.ev(q_ld)])
            def lns(e):
                e.activation(out=lnh[:].rearrange("p a b -> p (a b)"),
                             in_=hin[:].rearrange("p a b -> p (a b)"), func=AF.Abs)
                return e.activation(out=sgh[:].rearrange("p a b -> p (a b)"),
                                    in_=hin[:].rearrange("p a b -> p (a b)"), func=AF.Sign)
            onac(lns, waits=[sch.ev(e_dv)])
            ondv(lambda e: e.tensor_scalar_max(out=lnh[:].rearrange("p a b -> p (a b)"),
                                               in0=lnh[:].rearrange("p a b -> p (a b)"),
                                               scalar1=1e-38), waits=[sch.ev(e_ac)])
            onac(lambda e: e.activation(out=lnh[:].rearrange("p a b -> p (a b)"),
                                        in_=lnh[:].rearrange("p a b -> p (a b)"), func=AF.Ln),
                 waits=[sch.ev(e_dv)])
            ev_lnh = sch.ev(e_ac)
            tap("hin%d" % which, hin[:, :, :], [sch.ev(e_dv)])

            sch.phase = "m_res"
            with nc.sbuf_tensor("rwsl%d" % which, [128, 8, 128], BF16) as rwsl:
                for mg in range(PGI):
                    evw = dma("sync", q_ld, rwsl[:],
                              dram["b_winT"][:, DI + mg * 128:DI + (mg + 1) * 128]
                              .rearrange("(a p) c -> p a c", p=128),
                              waits=[sch.ev(e_pe)] if mg else barrier())
                    for inch, nc0 in enumerate(range(0, TLOC, TCH)):
                        pb = p1 if inch % 2 == 0 else p2
                        def resq(e, pb=pb, nc0=nc0):
                            r = None
                            for kg in range(PGM):
                                r = e.matmul(out=pb[:, 0:TCH], lhsT=rwsl[:, kg, :],
                                             rhs=arh[:, kg, HALO + nc0:HALO + nc0 + TCH],
                                             start=(kg == 0), stop=(kg == PGM - 1))
                            return r
                        last = onpe(resq, waits=[evw])
                        eva = onac(lambda e, mg=mg, pb=pb: e.activation(
                            out=wk1[:, 0:TCH], in_=pb[:, 0:TCH], func=AF.Silu,
                            bias=k_inb[:, PGI + mg, :]), waits=[last])
                        dma("scalar", q_s, dram["s_res"][:, mg, nc0:nc0 + TCH], wk1[:, 0:TCH],
                            waits=[eva])
            ev_res = sch.ev(q_s)

            sch.phase = "m_fix"
            with nc.sbuf_tensor("ccb%d" % which, [128, NS, TCH], BF16) as ccb:
                for itc in range(NTC):
                    nc0 = itc * TCH
                    evcc = dma("gpsimd", q_g, ccb[:],
                               bass.AP(sbc.tensor, sbc.offset + NS * TLOC + nc0,
                                       [[0, 128], [TLOC, NS], [1, TCH]]),
                               waits=[ev_bc, ev_scan])
                    for pg in range(PGI):
                        po = (pg % 2) * TCH
                        if pg % 2 == 0:
                            evy2 = dma("sync", q_ld, ap3(wk1, 0, [[TCH, 2], [1, TCH]]),
                                       dram["s_y"][:, pg:pg + 2, nc0:nc0 + TCH],
                                       waits=[ev_scan])
                            evcd2 = dma("sync", q_ld, ap3(wkb2, 0, [[TCH, 2], [1, TCH]]),
                                        dram["s_cdl"][:, pg:pg + 2, nc0:nc0 + TCH],
                                        waits=[ev_scan])
                            evrs2 = dma("sync", q_ld, ap3(wk2, 0, [[TCH, 2], [1, TCH]]),
                                        dram["s_res"][:, pg:pg + 2, nc0:nc0 + TCH],
                                        waits=[ev_res])
                        evy, evcd, evrs = evy2, evcd2, evrs2
                        py = [p3[:, 0:TCH], p4[:, 0:TCH], pbig[:, 2, 0:TCH],
                              pbig[:, 3, 0:TCH]][pg % 4]
                        qv2 = ap3(arh, 0, [[NT, NH], [1, TCH]])
                        for h2 in range(2):
                            s0 = h2 * NH
                            dgo = h2 * 1024
                            evdg = ondv(lambda e, pg=pg, s0=s0, dgo=dgo: e.tensor_tensor(
                                out=ap3(wkb, dgo, [[128, NH], [1, 128]]),
                                in0=ap3(k_eye, 0, [[0, NH], [1, 128]]),
                                in1=ap3(sgh, pg * NS + s0, [[1, NH], [0, 128]]),
                                op=OP.mult), waits=[ev_lnh] if h2 == 0 else [])
                            def qexpq(e, pg=pg, s0=s0, h2=h2, po=po):
                                r = None
                                for si in range(NH):
                                    r = e.activation(
                                        out=sqb[:, si, :] if h2 == 0 else arh[:, si, 0:TCH],
                                        in_=wkb2[:, po:po + TCH], func=AF.Exp,
                                        scale=Apk[:, pg, s0 + si:s0 + si + 1],
                                        bias=lnh[:, pg, s0 + si:s0 + si + 1])
                                return r
                            onac(qexpq, waits=[evcd, ev_lnh] if h2 == 0 else [])
                            evu2 = ondv(lambda e, s0=s0, h2=h2, qv2=qv2: e.tensor_tensor(
                                out=sqb[:].rearrange("p a b -> p (a b)") if h2 == 0 else qv2,
                                in0=sqb[:].rearrange("p a b -> p (a b)") if h2 == 0 else qv2,
                                in1=ccb[:, s0:s0 + NH, :], op=OP.mult),
                                waits=[evcc])
                            def fredq(e, py=py, h2=h2, dgo=dgo):
                                r = None
                                for si in range(NH):
                                    r = e.matmul(out=py,
                                                 lhsT=wkb[:, dgo + si * 128:dgo + (si + 1) * 128],
                                                 rhs=sqb[:, si, :] if h2 == 0
                                                 else arh[:, si, 0:TCH],
                                                 start=(h2 == 0 and si == 0),
                                                 stop=(h2 == 1 and si == NH - 1))
                                return r
                            onpe(fredq, waits=[evu2, evdg])
                        def gat(e, py=py, po=po):
                            e.tensor_add(out=wk1[:, po:po + TCH], in0=wk1[:, po:po + TCH],
                                         in1=py)
                            return e.tensor_tensor(out=wkb2[:, 1024 + po:1024 + po + TCH],
                                                   in0=wk1[:, po:po + TCH],
                                                   in1=wk2[:, po:po + TCH], op=OP.mult)
                        evg = ondv(gat, waits=[evy, evrs])
                        dma("scalar", q_s, dram["s_g"][:, pg, nc0:nc0 + TCH],
                            wkb2[:, 1024 + po:1024 + po + TCH], waits=[evg])
            ev_g = sch.ev(q_s)
            tap("gated%d" % which, wkb[:, TCH:2 * TCH], [ev_g])

            sch.phase = "m_outproj"
            def obank(mb):
                return pbig[:, mb, 0:TCH] if mb < 4 else [p1, p2, p3, p4][mb - 4][:, 0:TCH]
            for inch, nc0 in enumerate(range(0, TLOC, TCH)):
                for kg in range(PGI):
                    evwo = dma("sync", q_ld, wkb2[:, (kg % 2) * 1024:(kg % 2) * 1024 + 1024],
                               dram["b_woutT"][kg * 128:(kg + 1) * 128, :],
                               waits=[ev_g] if inch == 0 and kg == 0 else [])
                    evgx = dma("sync", q_ld, sqb[:, kg % 8, 0:TCH],
                               dram["s_g"][:, kg, nc0:nc0 + TCH],
                               waits=[ev_g] if inch == 0 and kg == 0 else [])
                    def opq(e, kg=kg):
                        r = None
                        for mb in range(PGM):
                            r = e.matmul(
                                out=obank(mb),
                                lhsT=wkb2[:, (kg % 2) * 1024 + mb * 128:
                                          (kg % 2) * 1024 + mb * 128 + 128],
                                rhs=sqb[:, kg % 8, 0:TCH], start=(kg == 0),
                                stop=(kg == PGI - 1))
                        return r
                    onpe(opq, waits=[evwo, evgx])
                for mb in range(PGM):
                    evrl = None
                    if which == 1:
                        evrl = dma("sync", q_ld, wk1[:, (mb % 2) * TCH:(mb % 2) * TCH + TCH],
                                   dram["s_s1"][:, mb, HALO + nc0:HALO + nc0 + TCH],
                                   waits=[ev_s1sp])
                    def evo(e, mb=mb, nc0=nc0, which=which):
                        if which == 1:
                            return e.scalar_tensor_tensor(
                                out=sA[:, mb, HALO + nc0:HALO + nc0 + TCH], in0=obank(mb),
                                scalar=k_outb[:, mb, :],
                                in1=wk1[:, (mb % 2) * TCH:(mb % 2) * TCH + TCH],
                                op0=OP.add, op1=OP.add)
                        return e.tensor_scalar(out=sA[:, mb, HALO + nc0:HALO + nc0 + TCH],
                                               in0=obank(mb),
                                               scalar1=k_outb[:, mb, :], scalar2=None,
                                               op0=OP.add)
                    ondv(evo, waits=[evrl])
            return sch.ev(e_dv)

        with nc.sbuf_tensor("arh1", [128, PGM, NT], BF16) as arh1, \
             nc.sbuf_tensor("wsl1", [128, 8, 128], BF16) as wsl1:
            ev_m1 = emit_mamba(1, arh1, wsl1)
        for nm, buf in [("sxm", "s_xm"), ("sdelta", "s_delta"), ("scdl", "s_cdl"),
                        ("sy", "s_y"), ("sres", "s_res"), ("sg", "s_g")]:
            tap(nm, dram[buf], [ev_m1] + barrier())
        tap("x1", sA[:, :, :], [ev_m1])
        if SN < 4:
            sch.enabled = False

        sch.phase = "x1halo"
        ev1 = dma("gpsimd", q_g, dram["cc_h1_i"], sA[:, :, NT - HALO:NT], waits=[ev_m1], slow=True)
        evc = ongp(lambda e: e.collective_compute(
            "AllGather", OP.bypass, replica_groups=[list(range(NCORE))],
            ins=[dram["cc_h1_i"].opt()], outs=[dram["cc_h1_o"].opt()]), waits=[ev1])
        ondv(lambda e: e.memset(sA[:, :, 0:HALO], 0.0), waits=[ev_m1])
        dvfence()
        for j in range(NCORE):
            evl = dma("sync", q_ld, wk1[:, 0:PGM * HALO],
                      dram["cc_h1_o"][j].rearrange("p a b -> p (a b)"), waits=[evc])
            def hj(e, j=j):
                return e.scalar_tensor_tensor(
                    out=sA[:, :, 0:HALO],
                    in0=wk1[:, 0:PGM * HALO].rearrange("p (a b) -> p a b", a=PGM),
                    scalar=k_psel[:, j:j + 1], in1=sA[:, :, 0:HALO],
                    op0=OP.mult, op1=OP.add)
            ondv(hj, waits=[evl])
            dvfence()
        for pg in range(PGM):
            dma("scalar", q_s, dram["s_s1"][:, pg, :], sA[:, pg, :], waits=[])
        ev_s1sp = sch.ev(q_s)

        with nc.sbuf_tensor("arh2", [128, PGM, NT], BF16) as arh2, \
             nc.sbuf_tensor("wsl2", [128, 8, 128], BF16) as wsl2:
            ev_m2 = emit_mamba(2, arh2, wsl2)
        tap("out2", sA[:, :, HALO:NT], [ev_m2])
        if SN < 5:
            sch.enabled = False
        for pg in range(PGM):
            dma("scalar", q_s, dram["s_o2"][:, pg, :], sA[:, pg, HALO:NT], waits=[ev_m2])
        ev_o2sp = sch.ev(q_s)
        for c in reversed(stA_cm):
            c.__exit__(None, None, None)

        with nc.sbuf_tensor("Vq", [128, PGM, DM], F32) as V, \
             nc.sbuf_tensor("Vbf", [128, PGM, DM], BF16) as Vbf, \
             nc.sbuf_tensor("Vlo", [128, PGM, DM], BF16) as Vlo, \
             nc.sbuf_tensor("o2hw", [128, PGM, 264], BF16) as o2hw, \
             nc.sbuf_tensor("o2lw", [128, PGM, 264], BF16) as o2lw, \
             nc.sbuf_tensor("ginf", [128, SC], F32) as ginf:
            rowd = dram["s_row"]
            ev_gin = dma("gpsimd", q_g, ginf[:],
                         bass.AP(rowd.tensor, rowd.offset + TLOC, [[0, 128], [1, SC]]),
                         waits=ev_grow)

            ev1 = dma("gpsimd", q_g, dram["cc_wk_i"], sA[:, :, NT - 1:NT], waits=[ev_m2], slow=True)
            evc = ongp(lambda e: e.collective_compute(
                "AllGather", OP.bypass, replica_groups=[list(range(NCORE))],
                ins=[dram["cc_wk_i"].opt()], outs=[dram["cc_wk_o"].opt()]), waits=[ev1])
            evz = ondv(lambda e: e.memset(wk2[:, 0:PGM], 0.0))
            evz = dvfence()
            for j in range(NCORE):
                evl = dma("sync", q_ld, wk1[:, 0:PGM],
                          dram["cc_wk_o"][j].rearrange("p a b -> p (a b)"), waits=[evc])
                def wkj(e, j=j):
                    return e.scalar_tensor_tensor(out=wk2[:, 0:PGM], in0=wk1[:, 0:PGM],
                                                  scalar=k_psel[:, j:j + 1], in1=wk2[:, 0:PGM],
                                                  op0=OP.mult, op1=OP.add)
                ondv(wkj, waits=[evl, evz])
                dvfence()
            ondv(lambda e: e.tensor_copy(out=sA[:, :, HALO - 1:HALO],
                                         in_=wk2[:, 0:PGM].rearrange("p (a b) -> p a b", b=1)))
            ev_wkh2 = sch.ev(e_dv)

            ondv(lambda e: e.memset(V[:].rearrange("p a b -> p (a b)"), 0.0))
            ev_v0 = sch.ev(e_dv)

            atk_cm = nc.sbuf_tensor("atk", [128, 8, DM], BF16)
            atk = atk_cm.__enter__()
            w2v = ap3(wk2, 0, [[SC, PGM], [1, SC]])
            w1v = ap3(wk1, 0, [[SC, PGM], [1, SC]])
            for c in range(NSC):
                t0 = HALO + c * SC
                sch.phase = "att_win"
                def wsp1(e, t0=t0):
                    e.tensor_copy(out=o2hw[:, :, 1:SC + 1], in_=sA[:, :, t0:t0 + SC])
                    return e.tensor_copy(out=w2v, in_=sA[:, :, t0:t0 + SC])
                ondv(wsp1, waits=[ev_wkh2] if c == 0 else [])
                def wsp2(e, t0=t0):
                    e.tensor_copy(out=w1v, in_=o2hw[:, :, 1:SC + 1])
                    return e.tensor_copy(out=o2hw[:, :, 0:1], in_=sA[:, :, t0 - 1:t0])
                ondv(wsp2)
                dvfence()
                def wsp3(e, t0=t0):
                    e.tensor_sub(out=o2lw[:, :, 1:SC + 1], in0=w2v, in1=w1v)
                    return e.tensor_copy(out=ap3(k_tmp8, 0, [[1, 8], [1, 1]]),
                                         in_=o2hw[:, :, 0:1])
                ondv(wsp3)
                dvfence()
                ondv(lambda e, t0=t0: e.tensor_sub(out=o2lw[:, :, 0:1],
                                                   in0=sA[:, :, t0 - 1:t0],
                                                   in1=ap3(k_tmp8, 0, [[1, 8], [1, 1]])))
                ev_w = sch.ev(e_dv)
                sch.phase = "att_v"
                for kg in range(PGM):
                    evww = dma("sync", q_ld, wkb2[:, 0:1024],
                               dram["b_wwT"][kg * 128:(kg + 1) * 128, :],
                               waits=[ev_wprep, ev_w] if c == 0 and kg == 0 else [])
                    evwl = dma("sync", q_ld, wkb2[:, 1024:2048],
                               dram["b_wwTlo"][kg * 128:(kg + 1) * 128, :])
                    def vq(e, kg=kg):
                        r = None
                        for tg in range(2):
                            for nh in range(2):
                                for tm, (lh, ro) in enumerate([(o2hw, 0), (o2hw, 1024),
                                                               (o2lw, 0)]):
                                    r = e.matmul(
                                        out=pbig[:, 2 * tg + nh, 0:512],
                                        lhsT=lh[:, kg, 1 + tg * 128:1 + tg * 128 + 128],
                                        rhs=wkb2[:, ro + nh * 512:ro + nh * 512 + 512],
                                        start=(kg == 0 and tm == 0),
                                        stop=(kg == PGM - 1 and tm == 2))
                        return r
                    onpe(vq, waits=[evww, evwl])
                for tg in range(2):
                    def vev(e, tg=tg):
                        e.tensor_scalar(out=wk2[:, 0:1024],
                                        in0=ap3(pbig, (2 * tg) * 512, [[1, 1024]]),
                                        scalar1=k_gw[:, tg, :], scalar2=None, op0=OP.mult)
                        e.tensor_copy(out=atk[:, 4 + tg, :], in_=wk2[:, 0:1024])
                        e.tensor_copy(out=wk1[:, 0:1024], in_=atk[:, 4 + tg, :])
                        return e.tensor_sub(out=atk[:, 6 + tg, :], in0=wk2[:, 0:1024],
                                            in1=wk1[:, 0:1024])
                    ondv(vev, waits=[sch.ev(e_pe)])
                ev_vt = sch.ev(e_dv)
                sch.phase = "att_wkt"
                for tg in range(2):
                    for dq in range(2):
                        def trq(e, tg=tg, dq=dq, t0=t0):
                            r = None
                            for di in range(4):
                                dg = dq * 4 + di
                                r = e.transpose(
                                    out=p1[:, di * 128:(di + 1) * 128],
                                    in_=sA[:, dg, t0 - 1 + tg * 128:t0 - 1 + tg * 128 + 128],
                                    identity=k_eyef[:])
                            return r
                        evt = onpe(trq, waits=[ev_vt] if dq == 0 else [])
                        ondv(lambda e, dq=dq: e.tensor_copy(
                            out=wk2[:, dq * 512:(dq + 1) * 512], in_=p1[0:128, 0:512]),
                            waits=[evt])
                    def wksp(e, tg=tg):
                        e.tensor_copy(out=atk[:, 0 + tg, :], in_=wk2[:, 0:1024])
                        e.tensor_copy(out=wk1[:, 0:1024], in_=atk[:, 0 + tg, :])
                        return e.tensor_sub(out=atk[:, 2 + tg, :], in0=wk2[:, 0:1024],
                                            in1=wk1[:, 0:1024])
                    ondv(wksp)
                ev_wkt = sch.ev(e_dv)
                sch.phase = "att_rks"
                def rksf(e, t0=t0):
                    e.tensor_tensor(out=w2v, in0=sA[:, :, t0:t0 + SC],
                                    in1=ap3(ginf, 0, [[0, PGM], [1, SC]]), op=OP.mult)
                    e.tensor_copy(out=sqb[:, :, 0:SC], in_=w2v)
                    e.tensor_copy(out=w1v, in_=sqb[:, :, 0:SC])
                    return e.tensor_sub(out=sqb[:, :, SC:2 * SC], in0=w2v, in1=w1v)
                evrk = ondv(rksf, waits=[ev_gin])
                sch.phase = "att_vsp"
                def vsp(e):
                    for eb in range(PGM):
                        e.tensor_copy(out=Vbf[:, eb, :], in_=V[:, eb, :])
                        e.tensor_copy(out=wk1[:, 1024:2048], in_=Vbf[:, eb, :])
                        e.tensor_sub(out=Vlo[:, eb, :], in0=V[:, eb, :],
                                     in1=wk1[:, 1024:2048])
                    return e.tensor_copy(out=k_fence[:, 2:3], in_=k_fence[:, 3:4])
                ondv(vsp, waits=[ev_v0] if c == 0 else [])
                evvb = sch.ev(e_dv)
                sch.phase = "att_s"
                for ub in range(2):
                    pS = p3 if ub == 0 else p4
                    def sq_(e, ub=ub, pS=pS):
                        r = None
                        for tm, (lh, rh) in enumerate([(o2hw, o2hw), (o2hw, o2lw),
                                                       (o2lw, o2hw)]):
                            for kg in range(PGM):
                                r = e.matmul(
                                    out=pS[:, 0:SC],
                                    lhsT=lh[:, kg, ub * 128:ub * 128 + 128],
                                    rhs=rh[:, kg, 1:1 + SC],
                                    start=(tm == 0 and kg == 0),
                                    stop=(tm == 2 and kg == PGM - 1))
                        return r
                    onpe(sq_, waits=[ev_w])
                    ondv(lambda e, ub=ub, pS=pS: e.scalar_tensor_tensor(
                        out=wk2[:, ub * SC:(ub + 1) * SC], in0=pS[:, 0:SC],
                        scalar=k_gwi[:, ub, :], in1=k_MT[:, ub, :], op0=OP.mult, op1=OP.mult),
                        waits=[sch.ev(e_pe)])
                def smsp(e):
                    e.tensor_copy(out=wkb[:, 0:2 * SC], in_=wk2[:, 0:2 * SC])
                    e.tensor_copy(out=wk1[:, 0:2 * SC], in_=wkb[:, 0:2 * SC])
                    return e.tensor_sub(out=wkb[:, 2 * SC:4 * SC], in0=wk2[:, 0:2 * SC],
                                        in1=wk1[:, 0:2 * SC])
                ondv(smsp)
                ev_sm = sch.ev(e_dv)
                sch.phase = "att_reads"
                for mb in range(PGM):
                    def rdq(e, mb=mb):
                        first = True
                        r = None
                        for kg in range(PGM):
                            for lhs, r0 in [(Vbf, 0), (Vbf, SC), (Vlo, 0)]:
                                r = e.matmul(
                                    out=ap3(pbig, mb * SC, [[1, SC]]),
                                    lhsT=lhs[:, kg, mb * 128:(mb + 1) * 128],
                                    rhs=sqb[:, kg, r0:r0 + SC], start=first, stop=False)
                                first = False
                        for ub in range(2):
                            for tm, (ti, wo) in enumerate([(4, 0), (4, 2 * SC), (6, 0)]):
                                r = e.matmul(
                                    out=ap3(pbig, mb * SC, [[1, SC]]),
                                    lhsT=atk[:, ti + ub, mb * 128:(mb + 1) * 128],
                                    rhs=wkb[:, wo + ub * SC:wo + ub * SC + SC],
                                    start=False, stop=(ub == 1 and tm == 2))
                        return r
                    onpe(rdq, waits=[evvb, evrk, ev_vt, ev_sm])
                evrd = ondv(lambda e: e.tensor_copy(
                    out=wk1[:, 0:PGM * SC],
                    in_=ap3(pbig, 0, [[1, PGM * SC]])), waits=[sch.ev(e_pe)])
                dma("scalar", q_s, dram["s_reads"][:, :, c * SC:(c + 1) * SC],
                    w1v, waits=[evrd])
                sch.phase = "att_vup"
                for eb in range(PGM):
                    def vupq(e, eb=eb):
                        r = None
                        for nh0 in range(2):
                            for ukg in range(2):
                                for tm, (li, ri) in enumerate([(0, 4), (0, 6), (2, 4)]):
                                    r = e.matmul(
                                        out=pbig[:, 2 * (eb % 2) + nh0, 0:512],
                                        lhsT=atk[:, li + ukg, eb * 128:(eb + 1) * 128],
                                        rhs=atk[:, ri + ukg, nh0 * 512:(nh0 + 1) * 512],
                                        start=(ukg == 0 and tm == 0),
                                        stop=(ukg == 1 and tm == 2))
                        return r
                    last = onpe(vupq, waits=[ev_wkt, evrd])
                    def vupd(e, eb=eb):
                        r = None
                        for nh0 in range(2):
                            r = e.scalar_tensor_tensor(
                                out=V[:, eb, nh0 * 512:(nh0 + 1) * 512],
                                in0=V[:, eb, nh0 * 512:(nh0 + 1) * 512], scalar=k_g256[:, 0:1],
                                in1=pbig[:, 2 * (eb % 2) + nh0, 0:512], op0=OP.mult, op1=OP.add)
                        return r
                    ondv(vupd, waits=[last])
                if c == 0:
                    tap("V0", V[:, :, :], [sch.ev(e_dv)] + barrier())
                if c in (5, 6):
                    tap("sq%d" % c, sqb[:, :, :], barrier())
                    tap("smw%d" % c, wkb[:, 0:4 * SC], barrier())
                    tap("atk%d" % c, atk[:, :, :], barrier())
                    tap("vb%d" % c, Vbf[:, :, :], barrier())
                    tap("vl%d" % c, Vlo[:, :, :], barrier())
                    tap("whi%d" % c, o2hw[:, :, :], barrier())
                    tap("wlo%d" % c, o2lw[:, :, :], barrier())
            ev_att = sch.ev(e_dv)
            tap("ginf", ginf[:, :], [ev_gin] + barrier())
            tap("srow", dram["s_row"], barrier())
            tap("g256", k_g256[:, :], barrier())
            tap("kgw", k_gw[:, :, :], barrier())
            tap("kmisc", k_misc[:, 0:80], barrier())
            tap("MT", k_MT[:, :, :], barrier())
            tap("sreads", dram["s_reads"], [ev_att] + barrier())
            tap("Vagg", V[:, :, :], [ev_att] + barrier())

            sch.phase = "att_agg"
            evvb = ondv(lambda e: e.tensor_copy(out=Vbf[:].rearrange("p a b -> p (a b)"),
                                                in_=V[:].rearrange("p a b -> p (a b)")),
                        waits=[ev_att])
            ev1 = dma("gpsimd", q_g, dram["cc_ag_i"], Vbf[:], waits=[evvb])
            evc = ongp(lambda e: e.collective_compute(
                "AllGather", OP.bypass, replica_groups=[list(range(NCORE))],
                ins=[dram["cc_ag_i"].opt()], outs=[dram["cc_ag_o"].opt()]), waits=[ev1])
            carry = V
            ondv(lambda e: e.memset(carry[:].rearrange("p a b -> p (a b)"), 0.0), waits=[evvb])
            ondv(lambda e: e.tensor_scalar(out=wk2[:, 64:64 + NCORE - 1],
                                           in0=k_chm[:, 0:NCORE - 1],
                                           scalar1=k_g2048[:, 0:1], scalar2=None, op0=OP.mult))
            dvfence()
            ondv(lambda e: e.tensor_add(out=wk2[:, 64:64 + NCORE - 1],
                                        in0=wk2[:, 64:64 + NCORE - 1],
                                        in1=k_chi[:, 0:NCORE - 1]))
            dvfence()
            for j in range(NCORE - 1):
                for pg in range(PGM):
                    evl = dma("sync", q_ld, wkb[:, 0:1024], dram["cc_ag_o"][j, :, pg, :],
                              waits=[evc])
                    def cj(e, j=j, pg=pg):
                        e.tensor_scalar(out=carry[:, pg, :], in0=carry[:, pg, :],
                                        scalar1=wk2[:, 64 + j:65 + j], scalar2=None, op0=OP.mult)
                        e.tensor_scalar(out=wk1[:, 0:1024], in0=wkb[:, 0:1024],
                                        scalar1=k_chm[:, j:j + 1], scalar2=None, op0=OP.mult)
                        return e.tensor_add(out=carry[:, pg, :], in0=carry[:, pg, :],
                                            in1=wk1[:, 0:1024])
                    ondv(cj, waits=[evl])
            evcar = ondv(lambda e: e.tensor_copy(out=Vbf[:].rearrange("p a b -> p (a b)"),
                                                 in_=carry[:].rearrange("p a b -> p (a b)")))

            sch.phase = "att_corr"
            atk_cm.__exit__(None, None, None)
            CCH = 256
            rdch_cm = nc.sbuf_tensor("rdch", [128, PGM, CCH], F32)
            rdch = rdch_cm.__enter__()
            rdlo_cm = nc.sbuf_tensor("rdlo", [128, PGM, CCH], BF16)
            rdlo = rdlo_cm.__enter__()
            gtauf_cm = nc.sbuf_tensor("gtauf", [128, CCH], F32)
            gtauf = gtauf_cm.__enter__()
            w2c = ap3(wk2, 0, [[CCH, PGM], [1, CCH]])
            for inch, nc0 in enumerate(range(0, TLOC, CCH)):
                ev_gtau = dma("gpsimd", q_g, gtauf[:],
                              bass.AP(rowd.tensor, rowd.offset + nc0, [[0, 128], [1, CCH]]),
                              waits=ev_grow + [sch.ev(e_dv)])
                evsc = ondv(lambda e, nc0=nc0: e.tensor_tensor(
                    out=sqb[:, :, 0:CCH], in0=sA[:, :, HALO + nc0:HALO + nc0 + CCH],
                    in1=ap3(gtauf, 0, [[0, PGM], [1, CCH]]), op=OP.mult),
                    waits=[evcar, ev_gtau])
                evread = dma("sync", q_ld, rdch[:],
                             dram["s_reads"][:, :, nc0:nc0 + CCH],
                             waits=[sch.ev(e_dv), sch.ev(e_pe)])
                for mb in range(PGM):
                    def crq(e, mb=mb):
                        r = None
                        for kg in range(PGM):
                            r = e.matmul(
                                out=pbig[:, mb % 4, 0:CCH],
                                lhsT=Vbf[:, kg, mb * 128:(mb + 1) * 128],
                                rhs=sqb[:, kg, 0:CCH], start=(kg == 0), stop=(kg == PGM - 1))
                        return r
                    last = onpe(crq, waits=[evsc])
                    ondv(lambda e, mb=mb: e.tensor_add(
                        out=rdch[:, mb, :], in0=rdch[:, mb, :], in1=pbig[:, mb % 4, 0:CCH]),
                        waits=[last, evread])
                def rdsp(e):
                    e.tensor_copy(out=sqb[:, :, CCH:2 * CCH],
                                  in_=rdch[:].rearrange("p a b -> p (a b)")
                                  if False else rdch[:, :, :])
                    e.tensor_copy(out=w2c, in_=sqb[:, :, CCH:2 * CCH])
                    return e.tensor_sub(out=rdlo[:, :, :], in0=rdch[:, :, :], in1=w2c)
                ondv(rdsp)
                ev_rt = sch.ev(e_dv)
                def bank(mb):
                    return pbig[:, mb, 0:CCH] if mb < 4 else [p1, p2, p3, p4][mb - 4][:, 0:CCH]
                last = None
                for kg in range(PGM):
                    evwr = dma("sync", q_ld, wkb2[:, 0:1024],
                               dram["b_wrT"][kg * 128:(kg + 1) * 128, :],
                               waits=[ev_rt] if kg == 0 else [])
                    evwrl = dma("sync", q_ld, wkb2[:, 1024:2048],
                                dram["b_wrTlo"][kg * 128:(kg + 1) * 128, :])
                    def wrq(e, kg=kg):
                        r = None
                        for mb in range(PGM):
                            for tm, (wo, rsrc) in enumerate([(0, 0), (0, 1), (1024, 0)]):
                                r = e.matmul(
                                    out=bank(mb), lhsT=wkb2[:, wo + mb * 128:wo + mb * 128 + 128],
                                    rhs=sqb[:, kg, CCH:2 * CCH] if rsrc == 0 else rdlo[:, kg, :],
                                    start=(kg == 0 and tm == 0),
                                    stop=(kg == PGM - 1 and tm == 2))
                        return r
                    last = onpe(wrq, waits=[evwr, evwrl])
                for mb in range(PGM):
                    evx1 = dma("sync", q_ld, wk1[:, 0:CCH],
                               dram["s_s1"][:, mb, HALO + nc0:HALO + nc0 + CCH], waits=[ev_s1sp])
                    evxf = dma("sync", q_ld, wk2[:, 0:CCH], dram["s_xfm"][:, mb, nc0:nc0 + CCH])
                    evo2 = dma("sync", q_ld, wk2[:, CCH:2 * CCH],
                               dram["s_o2"][:, mb, nc0:nc0 + CCH], waits=[ev_o2sp])
                    def fin(e, mb=mb, nc0=nc0):
                        e.scalar_tensor_tensor(out=sA[:, mb, HALO + nc0:HALO + nc0 + CCH],
                                               in0=bank(mb), scalar=ALPHA,
                                               in1=wk2[:, CCH:2 * CCH], op0=OP.mult, op1=OP.add)
                        e.tensor_add(out=sA[:, mb, HALO + nc0:HALO + nc0 + CCH],
                                     in0=sA[:, mb, HALO + nc0:HALO + nc0 + CCH],
                                     in1=wk1[:, 0:CCH])
                        return e.tensor_add(out=sA[:, mb, HALO + nc0:HALO + nc0 + CCH],
                                            in0=sA[:, mb, HALO + nc0:HALO + nc0 + CCH],
                                            in1=wk2[:, 0:CCH])
                    ondv(fin, waits=[last, evx1, evxf, evo2])
            gtauf_cm.__exit__(None, None, None)
            rdlo_cm.__exit__(None, None, None)
            rdch_cm.__exit__(None, None, None)
        ev_xa = sch.ev(e_dv)
        tap("xafter", sA[:, :, HALO:NT], [ev_xa])
        if SN < 6:
            sch.enabled = False

        for pg in range(PGM):
            dma("scalar", q_s, dram["s_s1"][:, pg, :], sA[:, pg, :], waits=[ev_xa])
        ev_xasp = sch.ev(q_s)

        stB_cm = [nc.sbuf_tensor("stB%d" % i, [128, NT], F32) for i in range(3)]
        st_box["mu"], st_box["ri"], st_box["t"] = [c.__enter__() for c in stB_cm]
        with nc.sbuf_tensor("hib", [128, PGM, NT], BF16) as hib, \
             nc.sbuf_tensor("gts2", [72, TLOC], F32) as gts2, \
             nc.sbuf_tensor("gtok", [128, 16, 80], F32) as gtok, \
             nc.sbuf_tensor("gg", [128, 16, 80], F32) as gg, \
             nc.sbuf_tensor("gA", [128, 16, 80], F32) as gA, \
             nc.sbuf_tensor("ohT", [64, TLOC], BF16) as ohT, \
             nc.sbuf_tensor("compb", [2, TLOC], BF16) as compb, \
             nc.sbuf_tensor("cw1b", [128, PGM, 256], BF16) as cw1b:
            ev_ln1 = emit_ln(sA, sA, None, False, NT, cast_bf=hib)
            xf = sA
            hi = hib

            sch.phase = "ffn_scores"
            for inch, nc0 in enumerate(range(0, TLOC, TCH)):
                last1 = last2 = None
                for kg in range(PGM):
                    def lof(e, kg=kg, nc0=nc0):
                        e.tensor_copy(out=wk1[:, 0:TCH], in_=hi[:, kg, HALO + nc0:HALO + nc0 + TCH])
                        e.tensor_sub(out=wk1[:, 0:TCH], in0=xf[:, kg, HALO + nc0:HALO + nc0 + TCH],
                                     in1=wk1[:, 0:TCH])
                        return e.tensor_copy(out=wkb2[:, 0:TCH], in_=wk1[:, 0:TCH])
                    evlo = ondv(lof, waits=[ev_ln1] if inch == 0 and kg == 0 else [])
                    def scq(e, kg=kg, nc0=nc0):
                        e.matmul(out=p1[0:64, 0:TCH], lhsT=k_qT[:, kg, :],
                                 rhs=hi[:, kg, HALO + nc0:HALO + nc0 + TCH],
                                 start=(kg == 0), stop=False)
                        e.matmul(out=p1[0:64, 0:TCH], lhsT=k_qT[:, kg, :], rhs=wkb2[:, 0:TCH],
                                 start=False, stop=(kg == PGM - 1))
                        e.matmul(out=p2[0:8, 0:TCH], lhsT=k_csT[:, kg, :],
                                 rhs=hi[:, kg, HALO + nc0:HALO + nc0 + TCH],
                                 start=(kg == 0), stop=False)
                        return e.matmul(out=p2[0:8, 0:TCH], lhsT=k_csT[:, kg, :],
                                        rhs=wkb2[:, 0:TCH], start=False,
                                        stop=(kg == PGM - 1))
                    last1 = last2 = onpe(scq, waits=[evlo])
                def sev(e, nc0=nc0):
                    e.tensor_copy(out=gts2[0:64, nc0:nc0 + TCH], in_=p1[0:64, 0:TCH])
                    return e.tensor_copy(out=gts2[64:72, nc0:nc0 + TCH], in_=p2[0:8, 0:TCH])
                ondv(sev, waits=[last1, last2])
            ev_sc = sch.ev(e_dv)
            tap("scores", gts2[0:72, 0:TLOC], [ev_sc])

            sch.phase = "ffn_gtok"
            for tg in range(16):
                evt = onpe(lambda e, tg=tg: e.transpose(out=p3[:, 0:128],
                                                        in_=gts2[0:72, tg * 128:(tg + 1) * 128],
                                                        identity=k_eyef[0:72, :]), waits=[ev_sc])
                ondv(lambda e, tg=tg: e.tensor_copy(out=gtok[:, tg, 0:72], in_=p3[0:128, 0:72]),
                     waits=[evt])
            ev_gt = sch.ev(e_dv)

            sch.phase = "ffn_calib"
            ondv(lambda e: e.tensor_scalar(out=gtok[:].rearrange("p a b -> p (a b)"),
                                           in0=gtok[:].rearrange("p a b -> p (a b)"),
                                           scalar1=k_misc[:, 64:65], scalar2=None, op0=OP.mult),
                 waits=[ev_gt])
            onac(lambda e: e.activation(out=gtok[:].rearrange("p a b -> p (a b)"),
                                        in_=gtok[:].rearrange("p a b -> p (a b)"),
                                        func=AF.Sigmoid), waits=[sch.ev(e_dv)])
            ondv(lambda e: e.tensor_scalar_mul(out=gtok[:].rearrange("p a b -> p (a b)"),
                                               in0=gtok[:].rearrange("p a b -> p (a b)"),
                                               scalar1=7.0), waits=[sch.ev(e_ac)])
            def tele(e):
                v = gtok[:].rearrange("p a b -> p (a b)")
                a = gA[:].rearrange("p a b -> p (a b)")
                g = gg[:].rearrange("p a b -> p (a b)")
                e.tensor_scalar(out=a, in0=v, scalar1=0.0, scalar2=k_kA[:, 0:1],
                                op0=OP.mult, op1=OP.add)
                e.tensor_scalar(out=g, in0=v, scalar1=0.0, scalar2=k_kB[:, 0:1],
                                op0=OP.mult, op1=OP.add)
                for k in range(1, 7):
                    e.tensor_single_scalar(out=wk1[:, 0:1280], in_=v, scalar=float(k), op=OP.is_ge)
                    e.scalar_tensor_tensor(out=a, in0=wk1[:, 0:1280], scalar=k_kA[:, k:k + 1],
                                           in1=a, op0=OP.mult, op1=OP.add)
                    e.scalar_tensor_tensor(out=g, in0=wk1[:, 0:1280], scalar=k_kB[:, k:k + 1],
                                           in1=g, op0=OP.mult, op1=OP.add)
                e.tensor_tensor(out=g, in0=g, in1=v, op=OP.mult)
                return e.tensor_add(out=g, in0=g, in1=a)
            ondv(tele, waits=[sch.ev(e_dv)])
            ev_gates = sch.ev(e_dv)
            tap("gates", gg[:, :, 0:72], [ev_gates])

            ondv(lambda e: e.tensor_reduce(out=wk1[:, 0:16], in_=gg[:, :, 64:72], axis=AX.X,
                                           op=OP.max), waits=[ev_gates])
            dvfence()
            def amax(e):
                mb_ = ap3(wk1, 0, [[1, 16], [0, 8]])
                return e.tensor_tensor(out=gA[:, :, 64:72], in0=gg[:, :, 64:72], in1=mb_,
                                       op=OP.is_ge)
            ondv(amax)
            dvfence()
            ondv(lambda e: e.tensor_tensor(out=gA[:, :, 64:72], in0=gA[:, :, 64:72],
                                           in1=ap3(k_iom, 0, [[0, 16], [1, 8]]), op=OP.mult))
            dvfence()
            ondv(lambda e: e.tensor_reduce(out=wk1[:, 16:32], in_=gA[:, :, 64:72], axis=AX.X,
                                           op=OP.min))
            dvfence()
            ondv(lambda e: e.tensor_scalar_add(out=wk1[:, 16:32], in0=wk1[:, 16:32],
                                               scalar1=999.0))
            dvfence()
            ondv(lambda e: e.memset(gA[:, :, 0:8], 0.0))
            for cc in range(8):
                def selcc(e, cc=cc):
                    return e.tensor_single_scalar(out=wk1[:, 32:48], in_=wk1[:, 16:32],
                                                  scalar=float(cc), op=OP.is_equal)
                ondv(selcc)
                dvfence()
                def selm(e, cc=cc):
                    ohb = ap3(wk1, 32, [[1, 16], [0, 8]])
                    return e.tensor_tensor(out=gg[:, :, 72:80], in0=gg[:, :, cc * 8:(cc + 1) * 8],
                                           in1=ohb, op=OP.mult)
                ondv(selm)
                dvfence()
                ondv(lambda e: e.tensor_add(out=gA[:, :, 0:8], in0=gA[:, :, 0:8],
                                            in1=gg[:, :, 72:80]))
                dvfence()
            ondv(lambda e: e.tensor_reduce(out=wk1[:, 48:64], in_=gA[:, :, 0:8], axis=AX.X,
                                           op=OP.max))
            dvfence()
            def selg(e):
                mb_ = ap3(wk1, 48, [[1, 16], [0, 8]])
                return e.tensor_tensor(out=gA[:, :, 8:16], in0=gA[:, :, 0:8], in1=mb_,
                                       op=OP.is_ge)
            ondv(selg)
            dvfence()
            ondv(lambda e: e.tensor_tensor(out=gA[:, :, 8:16], in0=gA[:, :, 8:16],
                                           in1=ap3(k_iom, 0, [[0, 16], [1, 8]]), op=OP.mult))
            dvfence()
            ondv(lambda e: e.tensor_reduce(out=wk1[:, 64:80], in_=gA[:, :, 8:16], axis=AX.X,
                                           op=OP.min))
            dvfence()
            ondv(lambda e: e.tensor_scalar(out=wk1[:, 80:96], in0=wk1[:, 16:32], scalar1=8.0,
                                           scalar2=None, op0=OP.mult))
            dvfence()
            ondv(lambda e: e.tensor_add(out=wk1[:, 80:96], in0=wk1[:, 80:96], in1=wk1[:, 64:80]))
            dvfence()
            ondv(lambda e: e.tensor_scalar_add(out=wk1[:, 80:96], in0=wk1[:, 80:96],
                                               scalar1=999.0))
            dvfence()
            ev_ti = sch.ev(e_dv)

            def ohf(e):
                tib = ap3(wk1, 80, [[1, 16], [0, 64]])
                iob = ap3(k_io64, 0, [[0, 16], [1, 64]])
                return e.tensor_tensor(out=ap3(wk2, 0, [[64, 16], [1, 64]]), in0=iob, in1=tib,
                                       op=OP.is_equal)
            ondv(ohf, waits=[ev_ti])
            ev_oh = sch.ev(e_dv)
            for tg in range(16):
                evt = onpe(lambda e, tg=tg: e.transpose(out=p3[0:64, 0:128],
                                                        in_=ap3(wk2, tg * 64, [[1, 64]]),
                                                        identity=k_eyef[:]), waits=[ev_oh])
                ondv(lambda e, tg=tg: e.tensor_copy(out=ohT[0:64, tg * 128:(tg + 1) * 128],
                                                    in_=p3[0:64, 0:128]), waits=[evt])
            ev_ohT = sch.ev(e_dv)
            tap("ohT", ohT[0:64, 0:TLOC], [ev_ohT])

            sch.phase = "ffn_comp"
            for kg in range(PGM):
                dma("sync", q_ld, cw1b[:, kg, :], dram["b_cw1"][kg * 128:(kg + 1) * 128, :],
                    waits=[ev_wprep])
            ev_cw1 = sch.ev(q_ld)
            ev_cw1b = sch.ev(q_l2)
            ev_cw1c = sch.ev(q_l3)
            ev_cw1d = sch.ev(q_l4)
            for inch, nc0 in enumerate(range(0, TLOC, TCH)):
                def hgq(e, nc0=nc0):
                    r = None
                    for mg in range(2):
                        for kg in range(PGM):
                            r = e.matmul(
                                out=(p1 if mg == 0 else p2)[:, 0:TCH],
                                lhsT=cw1b[:, kg, mg * 128:(mg + 1) * 128],
                                rhs=hi[:, kg, HALO + nc0:HALO + nc0 + TCH],
                                start=(kg == 0), stop=(kg == PGM - 1))
                    return r
                last = onpe(hgq, waits=[ev_cw1, ev_cw1b, ev_cw1c, ev_cw1d, ev_ln1])
                def hga(e, nc0=nc0):
                    e.activation(out=sqb[:, 0, 0:TCH], in_=p1[:, 0:TCH],
                                 func=AF.Gelu, bias=k_cb1[:, 0, :])
                    return e.activation(out=sqb[:, 1, 0:TCH], in_=p2[:, 0:TCH],
                                        func=AF.Gelu, bias=k_cb1[:, 1, :])
                onac(hga, waits=[last])
                last = None
                for kg in range(2):
                    last = onpe(lambda e, kg=kg: e.matmul(
                        out=p3[0:2, 0:TCH], lhsT=k_cw2[:, kg, :], rhs=sqb[:, kg, 0:TCH],
                        start=(kg == 0), stop=(kg == 1)), waits=[sch.ev(e_ac)])
                onac(lambda e, nc0=nc0: e.activation(out=compb[0:2, nc0:nc0 + TCH],
                                                     in_=p3[0:2, 0:TCH], func=AF.Tanh,
                                                     bias=k_cb2[0:2, 0:1]), waits=[last])
                ev_cp = sch.ev(e_ac)
                for mg in range(PGM):
                    evh = onpe(lambda e, mg=mg, nc0=nc0: e.matmul(
                        out=p4[:, 0:TCH], lhsT=k_w1a[0:2, mg * 128:(mg + 1) * 128],
                        rhs=compb[0:2, nc0:nc0 + TCH], start=True, stop=True), waits=[ev_cp])
                    evr = onac(lambda e, mg=mg: e.activation(out=wk1[:, 0:TCH], in_=p4[:, 0:TCH],
                                                             func=AF.Relu, bias=k_b1f[:, mg, :]),
                               waits=[evh])
                    evw2 = ondv(lambda e, mg=mg: e.tensor_scalar(
                        out=wkb2[:, 0:TCH], in0=wk1[:, 0:TCH], scalar1=k_w2f[:, mg, :],
                        scalar2=None, op0=OP.mult), waits=[evr])
                    onpe(lambda e, mg=mg: e.matmul(out=p3[0:64, 0:TCH], lhsT=k_blk8[:, mg, :],
                                                   rhs=wkb2[:, 0:TCH], start=(mg == 0),
                                                   stop=(mg == PGM - 1)), waits=[evw2])
                def magf(e, nc0=nc0):
                    e.tensor_scalar(out=wk1[0:64, 0:TCH], in0=p3[0:64, 0:TCH],
                                    scalar1=k_b2p[0:64, 0:1], scalar2=None, op0=OP.add)
                    return e.tensor_tensor(out=wkb[0:64, 0:TCH], in0=wk1[0:64, 0:TCH],
                                           in1=ohT[0:64, nc0:nc0 + TCH], op=OP.mult)
                evm = ondv(magf, waits=[sch.ev(e_pe), ev_ohT])
                evms = onpe(lambda e: e.matmul(out=p4[:, 0:TCH], lhsT=k_ones[0:64, :],
                                               rhs=wkb[0:64, 0:TCH], start=True, stop=True),
                            waits=[evm])
                evmsel = ondv(lambda e, nc0=nc0: e.tensor_copy(out=st_box["ri"][:, nc0:nc0 + TCH],
                                                                in_=p4[:, 0:TCH]), waits=[evms])
                for mq in range(2):
                    def dsq(e, mq=mq, nc0=nc0):
                        r = None
                        for mi in range(4):
                            mb = mq * 4 + mi
                            r = e.matmul(out=pbig[:, mi, 0:TCH],
                                         lhsT=k_dirs[0:64, mb * 128:(mb + 1) * 128],
                                         rhs=ohT[0:64, nc0:nc0 + TCH], start=True, stop=True)
                        return r
                    last = onpe(dsq, waits=[ev_ohT])
                    def ffin(e, mq=mq, nc0=nc0):
                        r = None
                        for mi in range(4):
                            mb = mq * 4 + mi
                            e.tensor_scalar(out=wk1[:, 0:TCH], in0=pbig[:, mi, 0:TCH],
                                            scalar1=k_osc[:, 0:1], scalar2=None, op0=OP.mult)
                            e.tensor_mul(out=wk1[:, 0:TCH], in0=wk1[:, 0:TCH],
                                         in1=st_box["ri"][:, nc0:nc0 + TCH])
                            r = e.tensor_add(out=xf[:, mb, HALO + nc0:HALO + nc0 + TCH],
                                             in0=xf[:, mb, HALO + nc0:HALO + nc0 + TCH],
                                             in1=wk1[:, 0:TCH])
                        return r
                    ondv(ffin, waits=[last, evmsel])
            ev_y = sch.ev(e_dv)
            tap("ffny", sA[:, :, HALO:NT], [ev_y])

            sch.phase = "ffn_final"
            ev_lnf = emit_ln(sA, sA, None, False, NT, gb=k_gam, bb=k_bet)
            for inch, nc0 in enumerate(range(0, TLOC, TCH)):
                for mb in range(PGM):
                    evr = dma("sync", q_ld, wk1[:, 0:TCH],
                              dram["s_s1"][:, mb, HALO + nc0:HALO + nc0 + TCH], waits=[ev_xasp])
                    ondv(lambda e, mb=mb, nc0=nc0: e.tensor_add(
                        out=sA[:, mb, HALO + nc0:HALO + nc0 + TCH],
                        in0=sA[:, mb, HALO + nc0:HALO + nc0 + TCH], in1=wk1[:, 0:TCH]),
                        waits=[evr, ev_lnf])
        for c in reversed(stB_cm):
            c.__exit__(None, None, None)
        ev_fin = sch.ev(e_dv)

        sch.phase = "out_store"
        sch.enabled = True
        bar_out = barrier()
        nst = 0
        for tg in range(16):
            evv = None
            xo = (tg % 2) * DM
            for dq in range(2):
                def otrq(e, tg=tg, dq=dq):
                    r = None
                    for di in range(4):
                        dg = dq * 4 + di
                        r = e.transpose(
                            out=p1[:, di * 128:(di + 1) * 128],
                            in_=sA[:, dg, HALO + tg * 128:HALO + (tg + 1) * 128],
                            identity=k_eyef[:])
                    return r
                evt = onpe(otrq, waits=bar_out if tg == 0 and dq == 0 else [])
                evv = ondv(lambda e, dq=dq, xo=xo: e.tensor_copy(
                    out=wk2[:, xo + dq * 512:xo + (dq + 1) * 512],
                    in_=p1[0:128, 0:512]), waits=[evt])
            dma("gpsimd", q_st, dram["out"][tg * 128:(tg + 1) * 128, :],
                wk2[:, xo:xo + DM], waits=[evv])
            nst += 16
        sch.emit("gpsimd", lambda e: e.engine_nop(), waits=[(q_st, nst)])

        sch.build(blockctx)
    return nc, dram


def _pack_inputs(inputs):
    from ml_dtypes import bfloat16
    x = np.asarray(inputs["L0_attn_x"], np.float32).reshape(NCORE * TLOC, DM)
    eye = np.eye(128, dtype=np.float32)
    packP = lambda a, pgs: np.ascontiguousarray(
        np.asarray(a, np.float32).reshape(pgs, 128, -1).transpose(1, 0, 2))
    rep = lambda v: np.full((128, 1), np.float32(v), np.float32)
    blk8 = np.zeros((128, 8, 64), np.float32)
    for p in range(128):
        for pg in range(8):
            blk8[p, pg, (pg * 128 + p) // 16] = 1.0
    base = {
        "w_inT": np.ascontiguousarray(np.asarray(inputs["L0_attn_in_proj_w"], np.float32).T),
        "w_outT": np.ascontiguousarray(np.asarray(inputs["L0_attn_out_proj_w"], np.float32).T),
        "xp_wT": np.ascontiguousarray(np.asarray(inputs["L0_attn_x_proj_w"], np.float32).T),
        "dt_wT": np.ascontiguousarray(np.asarray(inputs["L0_attn_dt_proj_w"], np.float32).T),
        "ww_T": np.ascontiguousarray(np.asarray(inputs["L0_attn_proj_write_w"], np.float32).T),
        "wr_T": np.ascontiguousarray(np.asarray(inputs["L0_attn_proj_read_w"], np.float32).T),
        "in_b": packP(inputs["L0_attn_in_proj_b"], 32),
        "out_b": packP(inputs["L0_attn_out_proj_b"], 8),
        "conv_w": packP(np.asarray(inputs["L0_attn_conv_w"], np.float32).reshape(DI, 4), 16),
        "conv_b": packP(inputs["L0_attn_conv_b"], 16),
        "dt_b": packP(inputs["L0_attn_dt_proj_b"], 16),
        "A_log": packP(inputs["L0_attn_A_log"], 16),
        "D_ssm": packP(inputs["L0_attn_D_ssm"], 16),
        "n1w": packP(inputs["L0_attn_norm1_w"], 8),
        "n2w": packP(inputs["L0_attn_norm2_w"], 8),
        "decay_r": rep(np.asarray(inputs["L0_attn_decay"], np.float32)),
        "sig_T": np.ascontiguousarray(np.asarray(inputs["L0_ffn_signatures_raw"], np.float32).T),
        "knots_r": np.ascontiguousarray(np.broadcast_to(
            np.asarray(inputs["L0_ffn_knot_values"], np.float32), (128, 8))),
        "temp_r": rep(np.asarray(inputs["L0_ffn_temperature"], np.float32)[0]),
        "cw1": np.asarray(inputs["L0_ffn_compress_w1"], np.float32),
        "cb1": packP(inputs["L0_ffn_compress_b1"], 2),
        "cw2": np.asarray(inputs["L0_ffn_compress_w2"], np.float32),
        "w1a": np.ascontiguousarray(np.asarray(inputs["L0_ffn_spline_w1"], np.float32)
                                    .transpose(1, 0, 2).reshape(2, 1024)),
        "b1f": packP(np.asarray(inputs["L0_ffn_spline_b1"], np.float32).reshape(-1), 8),
        "w2f": packP(np.asarray(inputs["L0_ffn_spline_w2"], np.float32).reshape(-1), 8),
        "b2p": np.asarray(inputs["L0_ffn_spline_b2"], np.float32).reshape(64, 1),
        "dirs": np.asarray(inputs["L0_ffn_directions"], np.float32),
        "gam": packP(inputs["L0_ffn_norm_gamma"], 8),
        "bet": packP(inputs["L0_ffn_norm_beta"], 8),
        "oscale_r": rep(np.asarray(inputs["L0_ffn_output_scale"], np.float32)[0]),
        "eye_bf": eye.astype(bfloat16),
        "eye_f32": eye,
        "ones_bf": np.ones((128, 128), np.float32).astype(bfloat16),
        "iota64": np.ascontiguousarray(np.broadcast_to(np.arange(64, dtype=np.float32), (128, 64))),
        "iotam": np.ascontiguousarray(np.broadcast_to(
            np.arange(8, dtype=np.float32) - 999.0, (128, 8))),
        "blk8": blk8.astype(bfloat16),
        "iota256": np.arange(SC, dtype=np.float32)[None, :],
        "iotaT": np.arange(TLOC, dtype=np.float32)[None, :],
        "consts8": np.ascontiguousarray(np.broadcast_to(np.array(
            [1e-5, 1.0, -1.0, -0.5, 256.0, 2048.0, 0.5, 0.0], np.float32), (128, 8))),
    }
    cb2 = np.zeros((128, 1), np.float32)
    cb2[0:2, 0] = np.asarray(inputs["L0_ffn_compress_b2"], np.float32)
    base["cb2"] = cb2
    maps = []
    for i in range(NCORE):
        m = dict(base)
        sl = np.zeros((NT, DM), np.float32)
        lo = i * TLOC - HALO
        s0 = max(lo, 0)
        sl[s0 - lo:, :] = x[s0:(i + 1) * TLOC, :]
        m["x_slab"] = sl
        m["c0mask"] = rep(0.0 if i == 0 else 1.0)
        chm = np.zeros((128, 7), np.float32)
        chm[:, :i] = 1.0
        m["chainmask"] = chm
        m["chaininv"] = np.ascontiguousarray(1.0 - chm)
        psel = np.zeros((128, 8), np.float32)
        if i > 0:
            psel[:, i - 1] = 1.0
        m["prevsel"] = psel
        maps.append(m)
    return maps


def kernel(**inputs):
    maps = _pack_inputs(inputs)
    nc, dram = build_nc()
    res = run_bass_kernel_spmd(nc, maps, core_ids=list(range(NCORE)))
    out = np.concatenate([res.results[i]["out"] for i in range(NCORE)], axis=0)
    return out.reshape(1, NCORE * TLOC, DM).astype(np.float32)
